# version 01
# baseline (speedup 1.0000x reference)
import contextlib
import numpy as np
import ml_dtypes
import concourse.bass as bass
import concourse.mybir as mybir
from concourse.bass_utils import run_bass_kernel_spmd
from concourse.aot_env import allow_ici_all_to_all

F32 = mybir.dt.float32
BF16 = mybir.dt.bfloat16
AF = mybir.ActivationFunctionType
ALU = mybir.AluOpType
AX = mybir.AxisListType
BF = ml_dtypes.bfloat16

NCORES = 8
B, T, DM = 4, 4096, 1024
TL = 2048
NCH = DM // 128
NT = TL // 512
C_DIM = float(DM)
MAGIC = 12582912.0
PAIRS = [[0, 1], [2, 3], [4, 5], [6, 7]]
FULL8 = [list(range(8))]

ALL_LAYERS = ["l0_attn", "l0_ffn", "l1_attn", "l1_ffn",
              "l2_attn", "l2_ffn", "l3_attn", "l3_ffn"]

L3F_NIT = 6
L3F_ATTN_ONLY = False

CONST_VALUES = [1e-3, 1e-5, 1e-6, 1.0, 0.5, -1e30, 1e-8, 1e-12, 0.0, 1e-24]



_SERIAL_OPS = {
    "tensor_copy", "tensor_tensor", "tensor_scalar", "scalar_tensor_tensor",
    "tensor_tensor_scan", "tensor_reduce", "reduce_sum", "reduce_max",
    "reciprocal", "memset", "memzero", "select", "bn_stats", "bn_aggr",
    "activation", "copy", "mul", "add", "square", "sqrt", "sign", "iota",
    "tensor_add", "tensor_sub", "tensor_mul", "tensor_max", "tensor_relu",
    "tensor_scalar_add", "tensor_scalar_mul", "tensor_scalar_sub",
    "tensor_scalar_max", "tensor_scalar_min", "stream_shuffle",
}


class SerialEng:

    def __init__(self, C, raw, sname):
        self._C = C
        self._raw = raw
        self._s = sname

    def __getattr__(self, attr):
        fn = getattr(self._raw, attr)
        if attr not in _SERIAL_OPS:
            return fn
        C, s, raw = self._C, self._s, self._raw

        def wrapped(*a, **k):
            C.wait(raw, C.tok(s))
            ins = fn(*a, **k)
            return C.inc(ins, s)
        return wrapped


class Ctx:
    def __init__(self, nc, block, es):
        self.nc = nc
        self.block = block
        self.es_stack = [es]
        self.sems = {}
        self.fence_toks = []
        self.cc_idx = 0
        self.uid = 0

    def sem(self, name):
        if name not in self.sems:
            h = self.es_stack[0].enter_context(self.nc.semaphore(name))
            self.sems[name] = [h, 0]
        return self.sems[name]

    def inc(self, instr, name, k=1):
        s = self.sem(name)
        instr.then_inc(s[0], k)
        s[1] += k
        return (name, s[1])

    def inc16(self, instr, name):
        return self.inc(instr, name, 16)

    def pre(self, name, k):
        s = self.sem(name)
        vals = [(name, s[1] + i + 1) for i in range(k)]
        s[1] += k
        return vals

    def inc_pre(self, instr, tok, k=1):
        instr.then_inc(self.sem(tok[0])[0], k)

    def tok(self, name):
        return (name, self.sem(name)[1])

    def wait(self, eng, tok):
        name, val = tok
        if val > 0:
            eng.wait_ge(self.sems[name][0], val)

    def future(self, name, j):
        return (name, self.sem(name)[1] + j)

    def fw(self, eng):
        for t in self.fence_toks:
            self.wait(eng, t)

    def set_fence(self, toks):
        best = {}
        for name, val in toks:
            best[name] = max(best.get(name, 0), val)
        self.fence_toks = [(n, v) for n, v in best.items()]

    def fresh_cc_sem(self):
        self.cc_idx += 1
        return self.es_stack[0].enter_context(self.nc.semaphore(f"cc{self.cc_idx}"))

    def sb(self, name, shape, dtype):
        self.uid += 1
        return self.es_stack[-1].enter_context(
            self.nc.sbuf_tensor(f"{name}_{self.uid}", shape, dtype))

    def ps(self, name, shape, dtype=F32):
        self.uid += 1
        return self.es_stack[-1].enter_context(
            self.nc.psum_tensor(f"{name}_{self.uid}", shape, dtype))

    def ser(self, raw, sname):
        self.sem(sname)
        return SerialEng(self, raw, sname)

    def dma_rr(self, q, out, in_, prefix, idx, n=2):
        name = f"{prefix}{idx % n}"
        ins = q.dma_start(out=out, in_=in_)
        return self.inc16(ins, name)

    def constf(self, val, rows=128):
        idx = CONST_VALUES.index(val)
        return self.cmisc[0:rows, idx:idx + 1]

    @contextlib.contextmanager
    def scope(self):
        with contextlib.ExitStack() as es:
            self.es_stack.append(es)
            try:
                yield
            finally:
                self.es_stack.pop()


def emit_ln(C, tag, src, dst, eps, gamma=None, beta=None, dst_bf=None,
            absmax_into=None):
    blk = C.block
    C.es_stack.append(contextlib.ExitStack())
    S = C.ps("ln_S", [128, 512], F32)
    S2 = C.ps("ln_S2", [128, 512], F32)
    BC1 = C.ps("ln_BC1", [128, 512], F32)
    BC2 = C.ps("ln_BC2", [128, 512], F32)
    xbt = [C.sb(f"ln_xb{ch}", [128, 512], BF16) for ch in range(NCH)]
    sqt = [C.sb(f"ln_sq{ch}", [128, 512], BF16) for ch in range(NCH)]
    r_mu = C.sb("ln_rmu", [1, 512], F32)
    r_var = C.sb("ln_rvar", [1, 512], F32)
    r_rstd2 = [C.sb(f"ln_rrstd{i}", [1, 512], F32) for i in range(2)]
    r_nmr2 = [C.sb(f"ln_rnmr{i}", [1, 512], F32) for i in range(2)]
    tmp = C.sb("ln_tmp", [128, 512], F32)
    rcol = C.sb("ln_rcol", [128, 1], F32)

    cast_toks, sq_toks, mm_toks = {}, {}, {}
    row_toks, bc_toks, apply_toks = {}, {}, {}

    for ti in range(NT):
        sl = slice(ti * 512, (ti + 1) * 512)
        bank = S if ti % 2 == 0 else S2
        r_rstd = r_rstd2[ti % 2]
        r_nmr = r_nmr2[ti % 2]

        @blk.vector
        def _(v, ti=ti, sl=sl):
            v = C.ser(v, "sDV")
            if ti == 0:
                C.fw(v._raw)
            for ch in range(NCH):
                if ti >= 1:
                    C.wait(v._raw, mm_toks[(ti - 1, ch, "s")])
                cast_toks[(ti, ch)] = v.tensor_copy(xbt[ch][:], src[ch][:, sl])

        @blk.scalar
        def _(s, ti=ti, sl=sl):
            s = C.ser(s, "sAC")
            if ti == 0:
                C.fw(s._raw)
            for ch in range(NCH):
                if ti >= 1:
                    C.wait(s._raw, mm_toks[(ti - 1, ch, "q")])
                sq_toks[(ti, ch)] = s.activation(out=sqt[ch][:], in_=src[ch][:, sl],
                                                 func=AF.Square)

        @blk.tensor
        def _(t, ti=ti, bank=bank):
            if ti == 0:
                C.fw(t)
            if ti >= 2:
                C.wait(t, row_toks[(ti - 2, "done")])
            for ch in range(NCH):
                C.wait(t, cast_toks[(ti, ch)])
                ins = t.matmul(bank[0:1, :], C.ones_bf[:], xbt[ch][:],
                               start=(ch == 0), stop=(ch == NCH - 1))
                mm_toks[(ti, ch, "s")] = C.inc(ins, "TE")
            for ch in range(NCH):
                C.wait(t, sq_toks[(ti, ch)])
                ins = t.matmul(bank[32:33, :], C.ones_bf[:], sqt[ch][:],
                               start=(ch == 0), stop=(ch == NCH - 1))
                mm_toks[(ti, ch, "q")] = C.inc(ins, "TE")

        @blk.vector
        def _(v, ti=ti, bank=bank):
            v = C.ser(v, "sDV")
            C.wait(v._raw, mm_toks[(ti, NCH - 1, "q")])
            if ti >= 2:
                C.wait(v._raw, bc_toks[ti - 2])
            v.tensor_scalar(out=r_mu[:], in0=bank[0:1, :], scalar1=1.0 / C_DIM,
                            scalar2=None, op0=ALU.mult)
            v.scalar_tensor_tensor(out=r_var[:], in0=r_mu[:], scalar=-1.0,
                                   op0=ALU.mult, in1=r_mu[:], op1=ALU.mult)
            row_toks[(ti, "done")] = v.scalar_tensor_tensor(
                out=r_var[:], in0=bank[32:33, :], scalar=1.0 / C_DIM, op0=ALU.mult,
                in1=r_var[:], op1=ALU.add)

        @blk.scalar
        def _(s, ti=ti, r_rstd=r_rstd):
            s = C.ser(s, "sAC")
            C.wait(s._raw, row_toks[(ti, "done")])
            s.activation(out=r_rstd[:], in_=r_var[:], func=AF.Ln,
                         bias=C.constf(eps, rows=1))
            row_toks[(ti, "rstd")] = s.activation(out=r_rstd[:], in_=r_rstd[:],
                                                  func=AF.Exp, scale=-0.5)

        @blk.vector
        def _(v, ti=ti, r_rstd=r_rstd, r_nmr=r_nmr):
            v = C.ser(v, "sDV")
            C.wait(v._raw, row_toks[(ti, "rstd")])
            row_toks[(ti, "nmr")] = v.scalar_tensor_tensor(
                out=r_nmr[:], in0=r_mu[:], scalar=-1.0, op0=ALU.mult,
                in1=r_rstd[:], op1=ALU.mult)

        @blk.tensor
        def _(t, ti=ti, r_rstd=r_rstd, r_nmr=r_nmr):
            C.wait(t, row_toks[(ti, "nmr")])
            if ti >= 1:
                C.wait(t, apply_toks[(ti - 1, NCH - 1)])
            t.matmul(BC1[:], C.ones_f[:], r_rstd[:], start=True, stop=True)
            ins = t.matmul(BC2[:], C.ones_f[:], r_nmr[:], start=True, stop=True)
            bc_toks[ti] = C.inc(ins, "TE")

        @blk.vector
        def _(v, ti=ti, sl=sl):
            v = C.ser(v, "sDV")
            C.wait(v._raw, bc_toks[ti])
            for ch in range(NCH):
                v.tensor_tensor(out=tmp[:], in0=src[ch][:, sl], in1=BC1[:], op=ALU.mult)
                if gamma is None:
                    tk = v.tensor_tensor(out=dst[ch][:, sl], in0=tmp[:], in1=BC2[:],
                                         op=ALU.add)
                else:
                    v.tensor_tensor(out=tmp[:], in0=tmp[:], in1=BC2[:], op=ALU.add)
                    tk = v.tensor_scalar(out=dst[ch][:, sl], in0=tmp[:],
                                         scalar1=gamma[:, ch:ch + 1],
                                         scalar2=beta[:, ch:ch + 1],
                                         op0=ALU.mult, op1=ALU.add)
                apply_toks[(ti, ch)] = tk
                if dst_bf is not None:
                    apply_toks[(ti, ch)] = v.tensor_copy(dst_bf[ch][:, sl],
                                                         dst[ch][:, sl])

    out_toks = [apply_toks[(NT - 1, ch)] for ch in range(NCH)]

    if absmax_into is not None:
        @blk.vector
        def _(v):
            v = C.ser(v, "sDV")
            rtmp = rcol
            v.memset(absmax_into[:], 0.0)
            tk = None
            for ch in range(NCH):
                v.reduce_max(rtmp[:], dst[ch][:], axis=AX.X, apply_absolute_value=True)
                tk = v.tensor_tensor(out=absmax_into[:], in0=absmax_into[:],
                                     in1=rtmp[:], op=ALU.max)
            out_toks.append(tk)

    C.es_stack.pop().close()
    return out_toks





def emit_l0_attn(C):
    nc = C.nc
    blk = C.block
    toks = {}
    bq = C.sbc["l0a_bq"]
    bdn = C.sbc["l0a_bdn"]
    bk = C.sbc["l0a_bk"]
    bv = C.sbc["l0a_bv"]
    bo = C.sbc["l0a_bo"]
    with C.scope():
        ctab = C.sb("l0a_ct", [128, TL], BF16)
        stab = C.sb("l0a_st", [128, TL], BF16)
        permh = C.sb("l0a_ph", [128, 128], BF16)
        Lg = C.sb("l0a_lg", [128, 512], BF16)
        Lsum = C.sb("l0a_ls", [32, 2], BF16)
        Lbc = C.sb("l0a_lb", [2, 32], F32)
        idstk = C.sb("l0a_id", [128, 64], BF16)
        sel2 = C.sb("l0a_sel", [32, 2048], BF16)
        wqs = [C.sb(f"l0a_wqs{i}", [128, DM], BF16) for i in range(2)]
        kpair = [C.sb(f"l0a_k3{i}", [128, TL], BF16) for i in range(8)]
        vpair = [C.sb(f"l0a_vp{i}", [128, TL], BF16) for i in range(8)]
        qp3 = C.sb("l0a_qp3", [128, TL], BF16)
        qrep = [C.sb(f"l0a_qr{i}", [128, 512], BF16) for i in range(2)]
        qb = C.sb("l0a_qb", [128, 512], BF16)
        kb = C.sb("l0a_kb", [128, 512], BF16)
        t1b = C.sb("l0a_t1", [128, 512], BF16)
        e1b = C.sb("l0a_e1", [128, 512], F32)
        et = C.sb("l0a_et", [32, 512], BF16)
        att = C.sb("l0a_att", [32, 512], BF16)
        r2 = C.sb("l0a_r2", [2, 512], F32)
        prods = [C.sb(f"l0a_pr{i}", [128, 512], BF16) for i in range(2)]
        prod2 = [C.sb(f"l0a_p2{i}", [128, 512], BF16) for i in range(2)]
        xb = [C.sb(f"l0a_xb{i}", [128, 512], BF16) for i in range(NCH)]
        ysl = [C.sb(f"l0a_ysl{i}", [128, 512], BF16) for i in range(2)]
        yld = [C.sb(f"l0a_yld{i}", [128, TL], BF16) for i in range(2)]
        d_y = nc.dram_tensor("l0a_dy", [DM, TL], BF16)
        PS = [C.ps(f"l0a_ps{i}", [128, 512], F32) for i in range(6)]

        with C.scope():
            wdn = [C.sb(f"l0a_wdn{i}", [128, DM], BF16) for i in range(2)]
            wup = [C.sb(f"l0a_wup{i}", [128, 256], BF16) for i in range(16)]
            lat = [C.sb(f"l0a_lat{i}", [128, TL], BF16) for i in range(2)]

            @blk.gpsimd
            def _(g):
                C.fw(g)
                for dst, nm in [(ctab, "l0a_c"), (stab, "l0a_s"),
                                (permh, "l0a_permh"), (Lg, "l0a_Lg"),
                                (Lsum, "l0a_Lsum"), (idstk, "l0a_idstk"),
                                (sel2, "l0a_sel2")]:
                    ins = g.dma_start(out=dst[:], in_=C.dram[nm])
                    C.inc16(ins, "0GT")
                ins = g.dma_start(out=Lbc[:], in_=C.dram["l0a_Lbc"])
                C.inc16(ins, "0GT")
                for i in range(2):
                    ins = g.dma_start(out=wdn[i][:], in_=C.dram["l0a_wdn"][i])
                    C.inc16(ins, "0GT")
                for i in range(16):
                    ins = g.dma_start(out=wup[i][:], in_=C.dram["l0a_wup"][i])
                    C.inc16(ins, "0GT")
                toks["cst"] = C.tok("0GT")

            for tt in range(NT):
                @blk.gpsimd
                def _(g, tt=tt):
                    g = C.ser(g, "sPL")
                    if tt == 0:
                        C.fw(g._raw)
                    else:
                        C.wait(g._raw, toks[("lm", tt - 1, 1)])
                    for c2 in range(NCH):
                        toks[("lxb", tt, c2)] = g.tensor_copy(
                            xb[c2][:], C.x[c2][:, tt * 512:(tt + 1) * 512])

                @blk.tensor
                def _(t, tt=tt):
                    if tt == 0:
                        C.fw(t)
                        C.wait(t, toks["cst"])
                    C.wait(t, toks[("lxb", tt, NCH - 1)])
                    for lo in range(2):
                        if tt >= 1:
                            C.wait(t, toks[("le", tt - 1, lo)])
                        for c2 in range(NCH):
                            ins = t.matmul(PS[lo][:],
                                           wdn[lo][:, c2 * 128:(c2 + 1) * 128],
                                           xb[c2][:],
                                           start=(c2 == 0),
                                           stop=(c2 == NCH - 1))
                        toks[("lm", tt, lo)] = C.inc(ins, "TE")

                @blk.scalar
                def _(s, tt=tt):
                    s = C.ser(s, "sAC")
                    for lo in range(2):
                        C.wait(s._raw, toks[("lm", tt, lo)])
                        toks[("le", tt, lo)] = s.activation(
                            out=lat[lo][:, tt * 512:(tt + 1) * 512],
                            in_=PS[lo][:], func=AF.Identity,
                            bias=bdn[:, lo:lo + 1])

            def kv_ev_tok(kch, tt):
                return toks[("kbe", kch, tt) if kch < 8 else ("kve", kch, tt)]

            for kch in range(16):
                for tt in range(NT):
                    @blk.tensor
                    def _(t, kch=kch, tt=tt):
                        if (kch, tt) == (0, 0):
                            C.wait(t, toks[("le", NT - 1, 1)])
                        elif tt >= 2:
                            C.wait(t, kv_ev_tok(kch, tt - 2))
                        elif kch >= 1:
                            C.wait(t, kv_ev_tok(kch - 1, tt + 2))
                        for c2 in range(2):
                            ins = t.matmul(PS[tt % 2][:],
                                           wup[kch][:, c2 * 128:(c2 + 1) * 128],
                                           lat[c2][:, tt * 512:(tt + 1) * 512],
                                           start=(c2 == 0), stop=(c2 == 1))
                        toks[("kvm", kch, tt)] = C.inc(ins, "TE")

                    if kch < 8:
                        @blk.scalar
                        def _(s, kch=kch, tt=tt):
                            s = C.ser(s, "sAC")
                            C.wait(s._raw, toks[("kvm", kch, tt)])
                            if (kch, tt) > (0, 0):
                                pk = (kch, tt - 1) if tt else (kch - 1, NT - 1)
                                if pk[0] < 8:
                                    C.wait(s._raw, toks[("kpm",) + pk])
                                    C.wait(s._raw, toks[("kt1",) + pk])
                            toks[("kbe", kch, tt)] = s.activation(
                                out=kb[:], in_=PS[tt % 2][:], func=AF.Identity,
                                bias=bk[:, kch:kch + 1])

                        @blk.tensor
                        def _(t, kch=kch, tt=tt):
                            C.wait(t, toks[("kbe", kch, tt)])
                            if (kch, tt) > (0, 0):
                                pk = (kch, tt - 1) if tt else (kch - 1, NT - 1)
                                C.wait(t, toks[("ke1",) + pk])
                            ins = t.matmul(PS[2 + tt % 2][:], permh[:], kb[:],
                                           start=True, stop=True)
                            toks[("kpm", kch, tt)] = C.inc(ins, "TE")

                        @blk.gpsimd
                        def _(g, kch=kch, tt=tt):
                            g = C.ser(g, "sPL")
                            C.wait(g._raw, toks[("kbe", kch, tt)])
                            toks[("kt1", kch, tt)] = g.tensor_tensor(
                                out=t1b[:], in0=kb[:],
                                in1=ctab[:, tt * 512:(tt + 1) * 512],
                                op=ALU.mult)

                        @blk.vector
                        def _(v, kch=kch, tt=tt):
                            v = C.ser(v, "sDV")
                            C.wait(v._raw, toks[("kpm", kch, tt)])
                            if (kch, tt) > (0, 0):
                                pk = (kch, tt - 1) if tt else (kch - 1, NT - 1)
                                if pk[0] < 8:
                                    C.wait(v._raw, toks[("kfin",) + pk])
                            toks[("ke1", kch, tt)] = v.tensor_tensor(
                                out=e1b[:], in0=PS[2 + tt % 2][:],
                                in1=stab[:, tt * 512:(tt + 1) * 512],
                                op=ALU.mult)

                        @blk.gpsimd
                        def _(g, kch=kch, tt=tt):
                            g = C.ser(g, "sPL")
                            C.wait(g._raw, toks[("ke1", kch, tt)])
                            toks[("kfin", kch, tt)] = g.tensor_tensor(
                                out=kpair[kch][:, tt * 512:(tt + 1) * 512],
                                in0=t1b[:], in1=e1b[:], op=ALU.add)
                    else:
                        @blk.scalar
                        def _(s, kch=kch, tt=tt):
                            s = C.ser(s, "sAC")
                            C.wait(s._raw, toks[("kvm", kch, tt)])
                            toks[("kve", kch, tt)] = s.activation(
                                out=vpair[kch - 8][:, tt * 512:(tt + 1) * 512],
                                in_=PS[tt % 2][:], func=AF.Identity,
                                bias=bv[:, kch - 8:kch - 7])

        for qch in range(NCH):
            @blk.gpsimd
            def _(g, qch=qch):
                if qch >= 2:
                    C.wait(g, toks[("qm", qch - 2, NT - 1)])
                ins = g.dma_start(out=wqs[qch % 2][:], in_=C.dram["l0a_wq"][qch])
                toks[("ql", qch)] = C.inc16(ins, f"0GQ{qch % 2}")

            for tt in range(NT):
                @blk.gpsimd
                def _(g, qch=qch, tt=tt):
                    g = C.ser(g, "sPL")
                    if (qch, tt) > (0, 0):
                        pk = (qch, tt - 1) if tt else (qch - 1, NT - 1)
                        C.wait(g._raw, toks[("qm",) + pk])
                    for c2 in range(NCH):
                        toks[("qxb", qch, tt, c2)] = g.tensor_copy(
                            xb[c2][:], C.x[c2][:, tt * 512:(tt + 1) * 512])

                @blk.tensor
                def _(t, qch=qch, tt=tt):
                    if (qch, tt) == (0, 0):
                        C.wait(t, toks[("kve", 15, NT - 1)])
                    elif tt >= 2:
                        C.wait(t, toks[("qbe", qch, tt - 2)])
                    elif qch >= 1:
                        C.wait(t, toks[("qbe", qch - 1, tt + 2)])
                    if tt == 0:
                        C.wait(t, toks[("ql", qch)])
                        if qch >= 1:
                            C.wait(t, toks[("att", qch - 1, NT - 1)])
                    C.wait(t, toks[("qxb", qch, tt, NCH - 1)])
                    for c2 in range(NCH):
                        ins = t.matmul(PS[tt % 2][:],
                                       wqs[qch % 2][:, c2 * 128:(c2 + 1) * 128],
                                       xb[c2][:],
                                       start=(c2 == 0), stop=(c2 == NCH - 1))
                    toks[("qm", qch, tt)] = C.inc(ins, "TE")

                @blk.scalar
                def _(s, qch=qch, tt=tt):
                    s = C.ser(s, "sAC")
                    C.wait(s._raw, toks[("qm", qch, tt)])
                    if (qch, tt) > (0, 0):
                        pk = (qch, tt - 1) if tt else (qch - 1, NT - 1)
                        C.wait(s._raw, toks[("qpm",) + pk])
                        C.wait(s._raw, toks[("qt1",) + pk])
                    toks[("qbe", qch, tt)] = s.activation(
                        out=qb[:], in_=PS[tt % 2][:], func=AF.Identity,
                        bias=bq[:, qch:qch + 1])

                @blk.tensor
                def _(t, qch=qch, tt=tt):
                    C.wait(t, toks[("qbe", qch, tt)])
                    if (qch, tt) > (0, 0):
                        pk = (qch, tt - 1) if tt else (qch - 1, NT - 1)
                        C.wait(t, toks[("qe1",) + pk])
                    ins = t.matmul(PS[2 + tt % 2][:], permh[:], qb[:],
                                   start=True, stop=True)
                    toks[("qpm", qch, tt)] = C.inc(ins, "TE")

                @blk.gpsimd
                def _(g, qch=qch, tt=tt):
                    g = C.ser(g, "sPL")
                    C.wait(g._raw, toks[("qbe", qch, tt)])
                    toks[("qt1", qch, tt)] = g.tensor_tensor(
                        out=t1b[:], in0=qb[:],
                        in1=ctab[:, tt * 512:(tt + 1) * 512], op=ALU.mult)

                @blk.vector
                def _(v, qch=qch, tt=tt):
                    v = C.ser(v, "sDV")
                    C.wait(v._raw, toks[("qpm", qch, tt)])
                    if (qch, tt) > (0, 0):
                        pk = (qch, tt - 1) if tt else (qch - 1, NT - 1)
                        C.wait(v._raw, toks[("qrp",) + pk])
                    toks[("qe1", qch, tt)] = v.tensor_tensor(
                        out=e1b[:], in0=PS[2 + tt % 2][:],
                        in1=stab[:, tt * 512:(tt + 1) * 512], op=ALU.mult)

                @blk.gpsimd
                def _(g, qch=qch, tt=tt):
                    g = C.ser(g, "sPL")
                    C.wait(g._raw, toks[("qe1", qch, tt)])
                    toks[("qrp", qch, tt)] = g.tensor_tensor(
                        out=qp3[:, tt * 512:(tt + 1) * 512],
                        in0=t1b[:], in1=e1b[:], op=ALU.add)

            for tt in range(NT):
                sl = slice(tt * 512, (tt + 1) * 512)

                @blk.gpsimd
                def _(g, qch=qch, tt=tt, sl=sl):
                    g = C.ser(g, "sPL")
                    C.wait(g._raw, toks[("qrp", qch, tt)])
                    if (qch, tt) > (0, 0):
                        pk = (qch, tt - 1) if tt else (qch - 1, NT - 1)
                        C.wait(g._raw, toks[("sred",) + pk + (15,)])
                    for hp in range(2):
                        g.tensor_copy(qrep[hp][0:64, :],
                                      qp3[hp * 64:hp * 64 + 64, sl])
                        g.tensor_copy(qrep[hp][64:128, :],
                                      qp3[hp * 64:hp * 64 + 64, sl])
                    te0 = C.sem("TE")[1]
                    for i in range(16):
                        hp, j = i // 8, i % 8
                        if i >= 2:
                            C.wait(g._raw, ("TE", te0 + i - 1))
                        toks[("spr", qch, tt, i)] = g.tensor_tensor(
                            out=prods[i % 2][:], in0=qrep[hp][:],
                            in1=kpair[j][:, sl], op=ALU.mult)

                @blk.tensor
                def _(t, qch=qch, tt=tt):
                    if (qch, tt) > (0, 0):
                        pk = (qch, tt - 1) if tt else (qch - 1, NT - 1)
                        C.wait(t, toks[("exp",) + pk])
                    for i in range(16):
                        hp, j = i // 8, i % 8
                        v_ = j * 2 + hp
                        C.wait(t, toks[("spr", qch, tt, i)])
                        ins = t.matmul(PS[2][0:32, :],
                                       Lg[:, v_ * 32:(v_ + 1) * 32],
                                       prods[i % 2][:],
                                       start=(i == 0), stop=(i == 15))
                        toks[("sred", qch, tt, i)] = C.inc(ins, "TE")

                @blk.scalar
                def _(s, qch=qch, tt=tt):
                    s = C.ser(s, "sAC")
                    C.wait(s._raw, toks[("sred", qch, tt, 15)])
                    if (qch, tt) > (0, 0):
                        pk = (qch, tt - 1) if tt else (qch - 1, NT - 1)
                        C.wait(s._raw, toks[("lred",) + pk])
                    toks[("exp", qch, tt)] = s.activation(
                        out=et[:], in_=PS[2][0:32, :], func=AF.Exp, scale=0.125)

                @blk.tensor
                def _(t, qch=qch, tt=tt):
                    C.wait(t, toks[("exp", qch, tt)])
                    if (qch, tt) > (0, 0):
                        pk = (qch, tt - 1) if tt else (qch - 1, NT - 1)
                        C.wait(t, toks[("att",) + pk])
                    ins = t.matmul(PS[0][0:2, :], Lsum[:], et[:],
                                   start=True, stop=True)
                    toks[("lred", qch, tt)] = C.inc(ins, "TE")

                @blk.scalar
                def _(s, qch=qch, tt=tt):
                    s = C.ser(s, "sAC")
                    C.wait(s._raw, toks[("lred", qch, tt)])
                    s.activation(out=r2[:], in_=PS[0][0:2, :], func=AF.Ln)
                    toks[("rr", qch, tt)] = s.activation(
                        out=r2[:], in_=r2[:], func=AF.Exp, scale=-1.0)

                @blk.tensor
                def _(t, qch=qch, tt=tt):
                    C.wait(t, toks[("rr", qch, tt)])
                    ins = t.matmul(PS[0][32:64, :], Lbc[:], r2[:],
                                   start=True, stop=True)
                    toks[("rbc", qch, tt)] = C.inc(ins, "TE")

                @blk.vector
                def _(v, qch=qch, tt=tt):
                    v = C.ser(v, "sDV")
                    C.wait(v._raw, toks[("rbc", qch, tt)])
                    toks[("att", qch, tt)] = v.tensor_tensor(
                        out=att[:], in0=et[:], in1=PS[0][32:64, :], op=ALU.mult)

                @blk.tensor
                def _(t, qch=qch, tt=tt):
                    C.wait(t, toks[("att", qch, tt)])
                    if (qch, tt) > (0, 0):
                        pk = (qch, tt - 1) if tt else (qch - 1, NT - 1)
                        C.wait(t, toks[("ye",) + pk])
                    dv0 = C.sem("sDV")[1]
                    for i in range(17):
                        if i < 16:
                            if i >= 2:
                                C.wait(t, ("sDV", dv0 + i - 1))
                            hp, gp = i // 8, i % 8
                            vsl = (gp * 2 + hp) * 128
                            ins = t.matmul(PS[3 + i % 2][:],
                                           sel2[:, vsl:vsl + 128], att[:],
                                           start=True, stop=True)
                            toks[("sel", qch, tt, i)] = C.inc(ins, "TE")
                        if i >= 1:
                            j = i - 1
                            hp, gp = j // 8, j % 8
                            C.wait(t, ("sDV", dv0 + j + 1))
                            ins = t.matmul(PS[5][hp * 64:hp * 64 + 64, :],
                                           idstk[:], prod2[j % 2][:],
                                           start=(gp == 0), stop=(gp == 7))
                            toks[("avr", qch, tt, j)] = C.inc(ins, "TE")

                @blk.vector
                def _(v, qch=qch, tt=tt, sl=sl):
                    v = C.ser(v, "sDV")
                    for i in range(16):
                        gp = i % 8
                        C.wait(v._raw, toks[("sel", qch, tt, i)])
                        toks[("pr2", qch, tt, i)] = v.tensor_tensor(
                            out=prod2[i % 2][:], in0=vpair[gp][:, sl],
                            in1=PS[3 + i % 2][:], op=ALU.mult)

                @blk.scalar
                def _(s, qch=qch, tt=tt):
                    s = C.ser(s, "sAC")
                    C.wait(s._raw, toks[("avr", qch, tt, 15)])
                    if (qch, tt) > (0, 1):
                        pk = (qch, tt - 2) if tt >= 2 else (qch - 1, tt + 2)
                        C.wait(s._raw, toks[("yw",) + pk])
                    toks[("ye", qch, tt)] = s.activation(
                        out=ysl[tt % 2][:], in_=PS[5][:], func=AF.Identity)

                @blk.gpsimd
                def _(g, qch=qch, tt=tt, sl=sl):
                    C.wait(g, toks[("ye", qch, tt)])
                    ins = g.dma_start(
                        out=d_y.ap()[qch * 128:(qch + 1) * 128, sl],
                        in_=ysl[tt % 2][:])
                    toks[("yw", qch, tt)] = C.inc16(ins, f"0GY{tt % 2}")

        for o in range(NCH):
            @blk.gpsimd
            def _(g, o=o):
                if o >= 2:
                    C.wait(g, toks[("om", o - 2, NCH - 1)])
                else:
                    C.wait(g, toks[("qm", NCH - 1, NT - 1)])
                ins = g.dma_start(out=wqs[o % 2][:], in_=C.dram["l0a_wo"][o])
                toks[("ol", o)] = C.inc16(ins, f"0GQ{o % 2}")

            for c2 in range(NCH):
                @blk.gpsimd
                def _(g, o=o, c2=c2):
                    if c2 >= 2:
                        C.wait(g, toks[("om", o, c2 - 2)])
                    elif o >= 1:
                        C.wait(g, toks[("om", o - 1, c2 + 6)])
                    elif c2 == 0:
                        C.wait(g, toks[("yw", NCH - 1, NT - 1)])
                        C.wait(g, toks[("yw", NCH - 1, NT - 2)])
                    ins = g.dma_start(out=yld[c2 % 2][:],
                                      in_=d_y.ap()[c2 * 128:(c2 + 1) * 128, :])
                    toks[("yl", o, c2)] = C.inc16(ins, f"0GZ{c2 % 2}")

                @blk.tensor
                def _(t, o=o, c2=c2):
                    C.wait(t, toks[("yl", o, c2)])
                    if c2 == 0:
                        C.wait(t, toks[("ol", o)])
                        if o == 0:
                            C.wait(t, toks[("ye", NCH - 1, NT - 1)])
                        else:
                            C.wait(t, toks[("oev", o - 1, NT - 1)])
                    for tt in range(NT):
                        ins = t.matmul(PS[tt][:],
                                       wqs[o % 2][:, c2 * 128:(c2 + 1) * 128],
                                       yld[c2 % 2][:, tt * 512:(tt + 1) * 512],
                                       start=(c2 == 0), stop=(c2 == NCH - 1))
                    toks[("om", o, c2)] = C.inc(ins, "TE")

            @blk.vector
            def _(v, o=o):
                v = C.ser(v, "sDV")
                for tt in range(NT):
                    C.wait(v._raw, toks[("om", o, NCH - 1)])
                    sl = slice(tt * 512, (tt + 1) * 512)
                    toks[("oev", o, tt)] = v.scalar_tensor_tensor(
                        out=C.x[o][:, sl], in0=PS[tt][:],
                        scalar=bo[:, o:o + 1], op0=ALU.add,
                        in1=C.x[o][:, sl], op1=ALU.add)

        C.set_fence([toks[("oev", NCH - 1, NT - 1)]])


def emit_l0_ffn(C):
    nc = C.nc
    blk = C.block
    wd = C.dram["l0f_w"]
    with C.scope():
        Wt = [C.sb(f"l0fWt{i}", [128, DM], F32) for i in range(2)]
        Wq = [C.sb(f"l0fq{ch}", [128, DM], BF16) for ch in range(NCH)]
        xn = [C.sb(f"l0fx{ch}", [128, TL], BF16) for ch in range(NCH)]
        g_col = C.sb("l0f_g", [128, 1], F32)
        rq_col = C.sb("l0f_rq", [128, 1], F32)
        sc_col = C.sb("l0f_sc", [128, 1], F32)
        bet_col = C.sb("l0f_bet", [128, 1], F32)
        rb_col = C.sb("l0f_rb", [128, 1], F32)
        bsum = C.sb("l0f_bs", [128, NCH + 1], F32)
        amax = C.sb("l0f_am", [128, 1], F32)
        srow = C.sb("l0f_sr", [1, 128], F32)
        gscal = C.sb("l0f_gs", [1, 1], F32)
        PS_B = C.ps("l0f_psb", [128, 512], F32)
        MMB = [C.ps(f"l0f_mm{i}", [128, 512], F32) for i in range(2)]
        d_col = nc.dram_tensor("l0f_dcol", [128, 1], F32).ap()
        d_g1 = nc.dram_tensor("l0f_dg1", [1, 1], F32).ap()
        d_g2 = nc.dram_tensor("l0f_dg2", [1, 1], F32).ap()
        cc_sem = C.fresh_cc_sem()
        toks = {}

        for ch in range(NCH):
            @blk.sync
            def _(q, ch=ch):
                if ch == 0:
                    C.fw(q)
                if ch >= 2:
                    C.wait(q, toks[("wsum", ch - 2)])
                toks[("wld", ch)] = C.dma_rr(q, Wt[ch % 2][:],
                                             wd[ch * 128:(ch + 1) * 128, :], "LW", ch)

            @blk.vector
            def _(v, ch=ch):
                v = C.ser(v, "sDV")
                C.wait(v._raw, toks[("wld", ch)])
                toks[("wsum", ch)] = v.reduce_sum(bsum[:, ch:ch + 1], Wt[ch % 2][:],
                                                  axis=AX.X, apply_absolute_value=True)

        @blk.vector
        def _(v):
            v = C.ser(v, "sDV")
            toks["wsumall"] = v.reduce_sum(bsum[:, NCH:NCH + 1], bsum[:, 0:NCH],
                                           axis=AX.X)

        ln_toks = emit_ln(C, "l0fln", C.x, xn, 1e-3, gamma=C.sbc["l0f_g"],
                          beta=C.sbc["l0f_b"], absmax_into=amax)

        @blk.tensor
        def _(t):
            C.wait(t, toks["wsumall"])
            ins = t.matmul(PS_B[0:1, 0:1], bsum[:, NCH:NCH + 1], C.ones_f_col[:],
                           start=True, stop=True)
            toks["bmm"] = C.inc(ins, "TE")

        @blk.vector
        def _(v):
            v = C.ser(v, "sDV")
            C.wait(v._raw, toks["bmm"])
            toks["beta"] = v.tensor_scalar(out=gscal[:], in0=PS_B[0:1, 0:1],
                                           scalar1=1.0 / (DM * DM), scalar2=None,
                                           op0=ALU.mult)

        @blk.tensor
        def _(t):
            C.wait(t, toks["beta"])
            ins = t.matmul(PS_B[:, 1:2], C.ones_f[:], gscal[:], start=True, stop=True)
            toks["bbc"] = C.inc(ins, "TE")

        @blk.vector
        def _(v):
            v = C.ser(v, "sDV")
            C.wait(v._raw, toks["bbc"])
            v.tensor_copy(bet_col[:], PS_B[:, 1:2])
            v.tensor_scalar(out=rb_col[:], in0=PS_B[:, 1:2], scalar1=1e-5,
                            scalar2=None, op0=ALU.add)
            toks["rb"] = v.reciprocal(rb_col[:], rb_col[:])

        for ch in range(NCH):
            @blk.sync
            def _(q, ch=ch):
                if ch == 0:
                    C.wait(q, toks["rb"])
                if ch >= 2:
                    C.wait(q, toks[("wq3", ch - 2)])
                toks[("wld2", ch)] = C.dma_rr(q, Wt[ch % 2][:],
                                              wd[ch * 128:(ch + 1) * 128, :], "LW", ch)

            @blk.gpsimd
            def _(g, ch=ch):
                g = C.ser(g, "sPL")
                if ch == 0:
                    C.wait(g._raw, toks["rb"])
                C.wait(g._raw, toks[("wld2", ch)])
                toks[("wq1", ch)] = g.tensor_scalar(out=Wt[ch % 2][:], in0=Wt[ch % 2][:],
                                                    scalar1=rb_col[:], scalar2=0.0,
                                                    op0=ALU.mult, op1=ALU.add)

            @blk.vector
            def _(v, ch=ch):
                v = C.ser(v, "sDV")
                C.wait(v._raw, toks[("wq1", ch)])
                toks[("wq2", ch)] = v.tensor_scalar(out=Wt[ch % 2][:], in0=Wt[ch % 2][:],
                                                    scalar1=MAGIC, scalar2=MAGIC,
                                                    op0=ALU.add, op1=ALU.subtract)

            @blk.gpsimd
            def _(g, ch=ch):
                g = C.ser(g, "sPL")
                C.wait(g._raw, toks[("wq2", ch)])
                toks[("wq3", ch)] = g.tensor_scalar(out=Wq[ch][:], in0=Wt[ch % 2][:],
                                                    scalar1=1.0, scalar2=-1.0,
                                                    op0=ALU.min, op1=ALU.max)

        @blk.sync
        def _(q):
            for tkn in ln_toks:
                C.wait(q, tkn)
            ins = q.dma_start(out=d_col, in_=amax[:])
            toks["gd1"] = C.inc16(ins, "LDg")
            C.wait(q, toks["gd1"])
            ins = q.dma_start(out=srow[:], in_=d_col.rearrange("a b -> b a"))
            toks["gd2"] = C.inc16(ins, "LDg")

        @blk.vector
        def _(v):
            v = C.ser(v, "sDV")
            C.wait(v._raw, toks["gd2"])
            toks["gloc"] = v.reduce_max(gscal[:], srow[:], axis=AX.X)

        @blk.gpsimd
        def _(g):
            C.wait(g, toks["gloc"])
            ins = g.dma_start(out=d_g1, in_=gscal[:])
            toks["gd3"] = C.inc16(ins, "GDg")
            C.wait(g, toks["gd3"])
            g.collective_compute("AllReduce", ALU.max, replica_groups=FULL8,
                                 ins=[d_g1.opt()], outs=[d_g2.opt()]).then_inc(cc_sem)
            g.wait_ge(cc_sem, 1)
            ins = g.dma_start(out=gscal[:], in_=d_g2)
            toks["gd4"] = C.inc16(ins, "GDg")

        @blk.tensor
        def _(t):
            C.wait(t, toks["gd4"])
            C.wait(t, toks["rb"])
            ins = t.matmul(PS_B[:, 2:3], C.ones_f[:], gscal[:], start=True, stop=True)
            toks["gbc"] = C.inc(ins, "TE")

        @blk.vector
        def _(v):
            v = C.ser(v, "sDV")
            C.wait(v._raw, toks["gbc"])
            v.tensor_copy(g_col[:], PS_B[:, 2:3])
            v.reciprocal(rq_col[:], g_col[:])
            v.tensor_scalar(out=rq_col[:], in0=rq_col[:], scalar1=128.0, scalar2=None,
                            op0=ALU.mult)
            toks["scal"] = v.scalar_tensor_tensor(out=sc_col[:], in0=g_col[:],
                                                  scalar=1.0 / 128.0, op0=ALU.mult,
                                                  in1=bet_col[:], op1=ALU.mult)

        @blk.gpsimd
        def _(g):
            g = C.ser(g, "sPL")
            C.wait(g._raw, toks["scal"])
            for ch in range(NCH):
                g.tensor_scalar(out=xn[ch][:], in0=xn[ch][:], scalar1=rq_col[:],
                                scalar2=0.0, op0=ALU.mult, op1=ALU.add)
                toks[("xq", ch)] = g.tensor_scalar(out=xn[ch][:], in0=xn[ch][:],
                                                   scalar1=128.0 - 1e-5,
                                                   scalar2=-(128.0 - 1e-5),
                                                   op0=ALU.min, op1=ALU.max)

        ev_toks = {}
        for ti in range(NT):
            for o in range(NCH):
                k = ti * NCH + o

                @blk.tensor
                def _(t, ti=ti, o=o, k=k):
                    if k == 0:
                        C.wait(t, toks[("xq", NCH - 1)])
                        C.wait(t, toks[("wq3", NCH - 1)])
                    bank = MMB[k % 2]
                    if k >= 2:
                        C.wait(t, ev_toks[k - 2])
                    for ch in range(NCH):
                        ins = t.matmul(bank[:], Wq[ch][:, o * 128:(o + 1) * 128],
                                       xn[ch][:, ti * 512:(ti + 1) * 512],
                                       start=(ch == 0), stop=(ch == NCH - 1))
                    toks[("mm", k)] = C.inc(ins, "TE")

                @blk.vector
                def _(v, ti=ti, o=o, k=k):
                    v = C.ser(v, "sDV")
                    if k == 0:
                        C.wait(v._raw, toks["scal"])
                    bank = MMB[k % 2]
                    C.wait(v._raw, toks[("mm", k)])
                    ev_toks[k] = v.scalar_tensor_tensor(
                        out=C.x[o][:, ti * 512:(ti + 1) * 512], in0=bank[:],
                        scalar=sc_col[:], op0=ALU.mult,
                        in1=C.x[o][:, ti * 512:(ti + 1) * 512], op1=ALU.add)

        C.set_fence([ev_toks[NT * NCH - 1], ev_toks[NT * NCH - 2]])


def emit_l1_ffn(C):
    nc = C.nc
    blk = C.block
    GATES = ["f", "i", "c"]
    with C.scope():
        xst = [[C.sb(f"l1f_xs{r}{ch}", [128, 512], BF16) for ch in range(NCH)]
               for r in range(2)]
        ct = [C.sb(f"l1f_c{ch}", [128, 512], F32) for ch in range(NCH)]
        cb = [C.sb(f"l1f_cb{ch}", [128, 512], BF16) for ch in range(NCH)]
        Ub = {g: [C.sb(f"l1f_U{g}{m}", [128, DM], BF16) for m in range(NCH)]
              for g in GATES}
        Uo2 = [C.sb(f"l1f_Uo{r}", [128, DM], BF16) for r in range(2)]
        Wb = [{g: C.sb(f"l1f_W{g}{r}", [128, DM], BF16) for g in GATES + ["o"]}
              for r in range(2)]
        gt = [{g: C.sb(f"l1f_g{g}{r}", [128, 512], F32) for g in GATES}
              for r in range(2)]
        ucf = C.sb("l1f_ucf", [128, 512], F32)
        ht = [C.sb(f"l1f_h{r}", [128, 512], F32) for r in range(2)]
        hbuf = C.sb("l1f_hb", [128, TL], F32)
        PS = [C.ps(f"l1f_ps{i}", [128, 512], F32) for i in range(4)]
        a2a_i = nc.dram_tensor("l1f_a2ai", [8 * DM, 256], F32)
        a2a_o = nc.dram_tensor("l1f_a2ao", [8 * DM, 256], F32)
        b2a_i = nc.dram_tensor("l1f_b2ai", [8 * DM, 256], F32)
        b2a_o = nc.dram_tensor("l1f_b2ao", [8 * DM, 256], F32)
        cc1 = C.fresh_cc_sem()
        cc2 = C.fresh_cc_sem()
        bias = C.sbc["l1f_bias"]
        inc_c = C.sbc["l1f_inc"]
        toks = {}

        a2ai_v = a2a_i.ap().rearrange("(j c) t -> c j t", c=DM)
        a2ao_v = a2a_o.ap().rearrange("(j c) t -> c j t", c=DM)
        b2ao_v = b2a_o.ap().rearrange("(j c) t -> c j t", c=DM)

        @blk.sync
        def _(q):
            C.fw(q)
            for ch in range(NCH):
                ins = q.dma_start(
                    out=a2ai_v[ch * 128:(ch + 1) * 128, :, :],
                    in_=C.x[ch][:].rearrange("c (j t) -> c j t", j=8))
                toks[("ax", ch)] = C.inc16(ins, "SA")

        @blk.gpsimd
        def _(g):
            C.wait(g, toks[("ax", NCH - 1)])
            with allow_ici_all_to_all():
                g.collective_compute("AllToAll", ALU.bypass, replica_groups=FULL8,
                                     ins=[a2a_i.ap().opt()],
                                     outs=[a2a_o.ap().opt()]).then_inc(cc1)
            g.wait_ge(cc1, 1)
            for g_ in GATES:
                for m in range(NCH):
                    ins = g.dma_start(out=Ub[g_][m][:],
                                      in_=C.dram[f"l1f_u{g_}"][m * 128:(m + 1) * 128, :])
                    C.inc16(ins, "GU")
            toks["uld"] = C.tok("GU")

        @blk.vector
        def _(v):
            v = C.ser(v, "sDV")
            C.fw(v._raw)
            for ch in range(NCH):
                toks[("cinit", ch)] = v.tensor_scalar(
                    out=ct[ch][:], in0=C.x[ch][:, 0:512], scalar1=0.0,
                    scalar2=inc_c[:, ch:ch + 1], op0=ALU.mult, op1=ALU.add)

        for t in range(4):
            @blk.gpsimd
            def _(g, t=t):
                if t >= 2:
                    C.wait(g, toks[("cupd", t - 2, NCH - 1)])
                sem = f"GX{t % 2}"
                for ch in range(NCH):
                    ins = g.dma_start(
                        out=xst[t % 2][ch][:],
                        in_=a2ao_v[ch * 128:(ch + 1) * 128, 2 * t:2 * t + 2, :])
                    C.inc16(ins, sem)
                toks[("xs", t)] = C.tok(sem)

            @blk.vector
            def _(v, t=t):
                v = C.ser(v, "sDV")
                for ch in range(NCH):
                    if t == 0:
                        C.wait(v._raw, toks[("cinit", ch)])
                    toks[("cb", t, ch)] = v.tensor_copy(cb[ch][:], ct[ch][:])

            for o in range(NCH):
                @blk.gpsimd
                def _(g, t=t, o=o):
                    k = t * NCH + o
                    sem = f"GW{k % 2}"
                    if k >= 2:
                        km = k - 2
                        C.wait(g, toks[("gmm", km // NCH, km % NCH, "c")])
                    for g_ in GATES:
                        ins = g.dma_start(out=Wb[k % 2][g_][:],
                                          in_=C.dram[f"l1f_w{g_}"][o])
                        C.inc16(ins, sem)
                    toks[("wld", k)] = C.tok(sem)

                @blk.tensor
                def _(tt, t=t, o=o):
                    k = t * NCH + o
                    if k == 0:
                        C.wait(tt, toks["uld"])
                    C.wait(tt, toks[("xs", t)])
                    C.wait(tt, toks[("wld", k)])
                    C.wait(tt, toks[("cb", t, NCH - 1)])
                    for gi, g_ in enumerate(GATES):
                        bank = PS[gi]
                        if k >= 1:
                            C.wait(tt, toks[("act", t if o >= 1 else t - 1,
                                             (o - 1) % NCH, g_)])
                        for ch in range(NCH):
                            tt.matmul(bank[:], Wb[k % 2][g_][:, ch * 128:(ch + 1) * 128],
                                      xst[t % 2][ch][:], start=(ch == 0), stop=False)
                        for m in range(NCH):
                            ins = tt.matmul(bank[:], Ub[g_][m][:, o * 128:(o + 1) * 128],
                                            cb[m][:], start=False, stop=(m == NCH - 1))
                        toks[("gmm", t, o, g_)] = C.inc(ins, "TE")

                @blk.scalar
                def _(s, t=t, o=o):
                    s = C.ser(s, "sAC")
                    k = t * NCH + o
                    funcs = {"f": AF.Sigmoid, "i": AF.Sigmoid, "c": AF.Tanh}
                    if k >= 2:
                        km = k - 2
                        C.wait(s._raw, toks[("cupd", km // NCH, km % NCH)])
                    for gi, g_ in enumerate(GATES):
                        C.wait(s._raw, toks[("gmm", t, o, g_)])
                        toks[("act", t, o, g_)] = s.activation(
                            out=gt[k % 2][g_][:], in_=PS[gi][:], func=funcs[g_],
                            bias=bias[:, gi * NCH + o:gi * NCH + o + 1])

                @blk.vector
                def _(v, t=t, o=o):
                    v = C.ser(v, "sDV")
                    k = t * NCH + o
                    G = gt[k % 2]
                    C.wait(v._raw, toks[("act", t, o, "c")])
                    v.tensor_tensor(out=G["i"][:], in0=G["i"][:], in1=G["c"][:],
                                    op=ALU.mult)
                    v.tensor_tensor(out=ct[o][:], in0=G["f"][:], in1=ct[o][:],
                                    op=ALU.mult)
                    toks[("cupd", t, o)] = v.tensor_tensor(
                        out=ct[o][:], in0=ct[o][:], in1=G["i"][:], op=ALU.add)

        @blk.vector
        def _(v):
            v = C.ser(v, "sDV")
            C.wait(v._raw, toks[("cupd", 3, NCH - 1)])
            for ch in range(NCH):
                toks[("cbf", ch)] = v.tensor_copy(cb[ch][:], ct[ch][:])

        for o in range(NCH):
            @blk.gpsimd
            def _(g, o=o):
                if o >= 2:
                    C.wait(g, toks[("uomm", o - 2)])
                ins = g.dma_start(out=Uo2[o % 2][:], in_=C.dram["l1f_uo"][o])
                C.inc16(ins, f"GO{o % 2}")
                toks[("uold", o)] = C.tok(f"GO{o % 2}")

            @blk.tensor
            def _(tt, o=o):
                C.wait(tt, toks[("cbf", NCH - 1)])
                C.wait(tt, toks[("uold", o)])
                if o >= 1:
                    C.wait(tt, toks[("ucf_done", o - 1)])
                for m in range(NCH):
                    ins = tt.matmul(PS[3][:], Uo2[o % 2][:, m * 128:(m + 1) * 128],
                                    cb[m][:], start=(m == 0), stop=(m == NCH - 1))
                toks[("uomm", o)] = C.inc(ins, "TE")

            @blk.vector
            def _(v, o=o):
                v = C.ser(v, "sDV")
                C.wait(v._raw, toks[("uomm", o)])
                if o >= 1:
                    C.wait(v._raw, toks[("hadd", o - 1, 3)])
                toks[("ucf_done", o)] = v.tensor_copy(ucf[:], PS[3][:])

            for t in range(4):
                @blk.gpsimd
                def _(g, o=o, t=t):
                    k2 = o * 4 + t
                    sem = f"GW{k2 % 2}"
                    if k2 < 2:
                        C.wait(g, toks[("gmm", 3, 6 + k2, "c")])
                    else:
                        km2 = k2 - 2
                        C.wait(g, toks[("homm", km2 // 4, km2 % 4)])
                    ins = g.dma_start(out=Wb[k2 % 2]["o"][:], in_=C.dram["l1f_wo"][o])
                    C.inc16(ins, sem)
                    toks[("wold", o, t)] = C.tok(sem)
                    sem2 = f"GX{k2 % 2}"
                    for ch in range(NCH):
                        ins = g.dma_start(
                            out=xst[k2 % 2][ch][:],
                            in_=a2ao_v[ch * 128:(ch + 1) * 128, 2 * t:2 * t + 2, :])
                        C.inc16(ins, sem2)
                    toks[("xso", o, t)] = C.tok(sem2)

                @blk.tensor
                def _(tt, o=o, t=t):
                    k2 = o * 4 + t
                    C.wait(tt, toks[("wold", o, t)])
                    C.wait(tt, toks[("xso", o, t)])
                    C.wait(tt, toks[("ucf_done", o)])
                    if k2 < 2:
                        C.wait(tt, toks[("act", 3, NCH - 1, "c")])
                    if k2 >= 2:
                        km2 = k2 - 2
                        C.wait(tt, toks[("hdma", km2 // 4, km2 % 4)])
                    for ch in range(NCH):
                        ins = tt.matmul(PS[k2 % 2][:],
                                        Wb[k2 % 2]["o"][:, ch * 128:(ch + 1) * 128],
                                        xst[k2 % 2][ch][:],
                                        start=(ch == 0), stop=(ch == NCH - 1))
                    toks[("homm", o, t)] = C.inc(ins, "TE")

                @blk.vector
                def _(v, o=o, t=t):
                    v = C.ser(v, "sDV")
                    k2 = o * 4 + t
                    C.wait(v._raw, toks[("homm", o, t)])
                    toks[("hadd", o, t)] = v.tensor_tensor(
                        out=ht[k2 % 2][:], in0=PS[k2 % 2][:], in1=ucf[:], op=ALU.add)

                @blk.scalar
                def _(s, o=o, t=t):
                    s = C.ser(s, "sAC")
                    k2 = o * 4 + t
                    C.wait(s._raw, toks[("hadd", o, t)])
                    toks[("htanh", o, t)] = s.activation(
                        out=ht[k2 % 2][:], in_=ht[k2 % 2][:], func=AF.Tanh,
                        bias=bias[:, 3 * NCH + o:3 * NCH + o + 1])

                @blk.sync
                def _(q, o=o, t=t):
                    k2 = o * 4 + t
                    C.wait(q, toks[("htanh", o, t)])
                    sem = f"SH{k2 % 2}"
                    q.dma_start(out=b2a_i.ap()[(2 * t) * DM + o * 128:
                                               (2 * t) * DM + o * 128 + 128, :],
                                in_=ht[k2 % 2][:, 0:256]).then_inc(C.sem(sem)[0], 16)
                    C.sems[sem][1] += 16
                    ins = q.dma_start(out=b2a_i.ap()[(2 * t + 1) * DM + o * 128:
                                                     (2 * t + 1) * DM + o * 128 + 128, :],
                                      in_=ht[k2 % 2][:, 256:512])
                    C.inc16(ins, sem)
                    toks[("hdma", o, t)] = C.tok(sem)

        @blk.gpsimd
        def _(g):
            for o in range(NCH):
                for t in range(4):
                    C.wait(g, toks[("hdma", o, t)])
            with allow_ici_all_to_all():
                g.collective_compute("AllToAll", ALU.bypass, replica_groups=FULL8,
                                     ins=[b2a_i.ap().opt()],
                                     outs=[b2a_o.ap().opt()]).then_inc(cc2)
            g.wait_ge(cc2, 1)
            toks["cc2done"] = True

        for ch in range(NCH):
            @blk.gpsimd
            def _(g, ch=ch):
                if ch >= 1:
                    C.wait(g, toks[("xadd", ch - 1)])
                ins = g.dma_start(
                    out=hbuf[:].rearrange("c (j t) -> c j t", j=8),
                    in_=b2ao_v[ch * 128:(ch + 1) * 128, :, :])
                C.inc16(ins, "GH")
                toks[("hload", ch)] = C.tok("GH")

            @blk.vector
            def _(v, ch=ch):
                v = C.ser(v, "sDV")
                C.wait(v._raw, toks[("hload", ch)])
                toks[("xadd", ch)] = v.tensor_tensor(out=C.x[ch][:], in0=C.x[ch][:],
                                                     in1=hbuf[:], op=ALU.add)

        C.set_fence([toks[("xadd", NCH - 1)]])


def emit_l3_attn(C):
    nc = C.nc
    blk = C.block
    with C.scope():
        xbf = [C.sb(f"l3a_xb{ch}", [128, TL], BF16) for ch in range(4)]
        K = C.sb("l3a_K", [128, TL], F32)
        V = C.sb("l3a_V", [128, TL], F32)
        T1 = C.sb("l3a_T1", [128, TL], F32)
        SR = C.sb("l3a_SR", [128, TL], F32)
        buf_a = C.sb("l3a_ba", [128, TL + 1], F32)
        buf_b = C.sb("l3a_bb", [128, TL + 1], F32)
        srw = C.sb("l3a_srw", [128, TL], BF16)
        Wb = [[C.sb(f"l3a_W{w}{r}", [128, DM], BF16) for r in range(2)]
              for w in range(3)]
        wrow = C.sb("l3a_wrow", [1, DM], F32)
        trow = C.sb("l3a_trow", [1, TL], F32)
        eu = C.sb("l3a_eu", [128, NCH], F32)
        ew = C.sb("l3a_ew", [128, NCH], F32)
        omix = C.sb("l3a_omix", [128, 24], F32)
        acol = C.sb("l3a_acol", [128, 2], F32)
        bnd = C.sb("l3a_bnd", [128, 4], F32)
        bndin = C.sb("l3a_bndin", [128, 4], F32)
        PS = [C.ps(f"l3a_ps{i}", [128, 512], F32) for i in range(4)]
        d_bnd_i = nc.dram_tensor("l3a_bndi", [128, 4], F32)
        d_bnd_o = nc.dram_tensor("l3a_bndo", [256, 4], F32)
        d_car_i = [nc.dram_tensor(f"l3a_cri{o}", [2, 128], F32) for o in range(NCH)]
        d_car_o = [nc.dram_tensor(f"l3a_cro{o}", [4, 128], F32) for o in range(NCH)]
        d_srw = nc.dram_tensor("l3a_dsrw", [DM, TL], BF16)
        cc_b = C.fresh_cc_sem()
        cc_c = [C.fresh_cc_sem() for o in range(NCH)]
        toks = {}
        mix = C.sbc["l3a_mix"]
        flags = C.sbc["c_flags"]
        wcols = C.sbc["l3a_wcols"]
        ucols = C.sbc["l3a_ucols"]
        MIXI = {"k": 0, "v": 1, "r": 2}

        @blk.vector
        def _(v):
            v = C.ser(v, "sDV")
            C.fw(v._raw)
            v.tensor_copy(bnd[:, 0:1], C.x[0][:, TL - 1:TL])
            v.tensor_copy(bnd[:, 1:2], C.x[1][:, TL - 1:TL])
            v.tensor_copy(bnd[:, 2:3], C.x[2][:, 0:1])
            toks["bnd"] = v.tensor_copy(bnd[:, 3:4], C.x[3][:, 0:1])
            v.tensor_scalar(out=ew[:], in0=wcols[:], scalar1=1.0 / T, scalar2=None,
                            op0=ALU.mult)
            toks["wu"] = v.tensor_scalar(out=eu[:], in0=ucols[:], scalar1=1.0 / T,
                                         scalar2=None, op0=ALU.mult)
            toks["omix"] = v.tensor_scalar(out=omix[:], in0=mix[:], scalar1=-1.0,
                                           scalar2=1.0, op0=ALU.mult, op1=ALU.add)

        @blk.scalar
        def _(s):
            s = C.ser(s, "sAC")
            C.fw(s._raw)
            C.wait(s._raw, toks["wu"])
            s.activation(out=ew[:], in_=ew[:], func=AF.Exp)
            toks["ewu"] = s.activation(out=eu[:], in_=eu[:], func=AF.Exp)

        @blk.sync
        def _(q):
            C.fw(q)
            C.wait(q, toks["bnd"])
            ins = q.dma_start(out=d_bnd_i.ap(), in_=bnd[:])
            toks["bd1"] = C.inc16(ins, "LDg")
            ins = q.dma_start(out=wrow[:], in_=C.dram["l3a_wrowd"])
            C.inc16(ins, "LDw")
            ins = q.dma_start(out=trow[:], in_=C.const_drams["c_trow"])
            toks["wrow"] = C.inc16(ins, "LDw")

        @blk.gpsimd
        def _(g):
            C.wait(g, toks["bd1"])
            g.collective_compute("AllGather", ALU.bypass, replica_groups=PAIRS,
                                 ins=[d_bnd_i.ap().opt()],
                                 outs=[d_bnd_o.ap().opt()]).then_inc(cc_b)
            g.wait_ge(cc_b, 1)
            ins = g.dma_start(out=bndin[:, 0:2], in_=d_bnd_o.ap()[0:128, 0:2])
            C.inc16(ins, "GDg")
            ins = g.dma_start(out=bndin[:, 2:4], in_=d_bnd_o.ap()[128:256, 2:4])
            C.inc16(ins, "GDg")
            toks["bndld"] = C.tok("GDg")

        @blk.vector
        def _(v):
            v = C.ser(v, "sDV")
            C.wait(v._raw, toks["wrow"])
            toks["wrowd"] = v.tensor_scalar(out=wrow[:], in0=wrow[:], scalar1=1.0 / T,
                                            scalar2=None, op0=ALU.mult)
            C.wait(v._raw, toks["bndld"])
            v.tensor_scalar(out=bndin[:, 0:2], in0=bndin[:, 0:2],
                            scalar1=flags[:, 1:2], scalar2=None, op0=ALU.mult)
            toks["bndv"] = v.tensor_scalar(out=bndin[:, 2:4], in0=bndin[:, 2:4],
                                           scalar1=flags[:, 0:1], scalar2=None,
                                           op0=ALU.mult)

        with C.scope():
            xm = {m: [C.sb(f"l3a_x{m}{ch}", [128, TL], BF16) for ch in range(4)]
                  for m in "kvr"}

            for m in "kvr":
                mi = MIXI[m]
                for ch in range(4):
                    fwd = ch < 2
                    mcol = mix[:, mi * NCH + ch:mi * NCH + ch + 1]
                    ocol = omix[:, mi * NCH + ch:mi * NCH + ch + 1]

                    @blk.gpsimd
                    def _(g, m=m, ch=ch, mcol=mcol):
                        g = C.ser(g, "sPL")
                        if (m, ch) == ("k", 0):
                            C.fw(g._raw)
                        toks[("lp1", m, ch)] = g.tensor_scalar(
                            out=xm[m][ch][:], in0=C.x[ch][:], scalar1=mcol,
                            scalar2=0.0, op0=ALU.mult, op1=ALU.add)

                    @blk.vector
                    def _(v, m=m, ch=ch, fwd=fwd, ocol=ocol):
                        v = C.ser(v, "sDV")
                        C.wait(v._raw, toks[("lp1", m, ch)])
                        if fwd:
                            osl, isl, bc = slice(1, TL), slice(0, TL - 1), 0
                        else:
                            osl, isl, bc = slice(0, TL - 1), slice(1, TL), TL - 1
                        v.scalar_tensor_tensor(
                            out=xm[m][ch][:, osl], in0=C.x[ch][:, isl], scalar=ocol,
                            op0=ALU.mult, in1=xm[m][ch][:, osl], op1=ALU.add)
                        C.wait(v._raw, toks["bndv"])
                        toks[("lp2", m, ch)] = v.scalar_tensor_tensor(
                            out=xm[m][ch][:, bc:bc + 1], in0=bndin[:, ch:ch + 1],
                            scalar=ocol, op0=ALU.mult,
                            in1=xm[m][ch][:, bc:bc + 1], op1=ALU.add)

            @blk.gpsimd
            def _(g):
                g = C.ser(g, "sPL")
                for ch in range(4):
                    toks[("xbf", ch)] = g.tensor_copy(xbf[ch][:], C.x[ch + 4][:])

            def rhs_for(m, ch, tsl):
                return (xm[m][ch][:, tsl] if ch < 4 else xbf[ch - 4][:, tsl])

            for o in range(NCH):
                @blk.gpsimd
                def _(g, o=o):
                    if o >= 2:
                        C.wait(g, toks[("mmdone", o - 2)])
                    sem = f"GW{o % 2}"
                    for wi, wname in enumerate(["l3a_wk", "l3a_wv", "l3a_wr"]):
                        ins = g.dma_start(out=Wb[wi][o % 2][:], in_=C.dram[wname][o])
                        C.inc16(ins, sem)
                    toks[("wld", o)] = C.tok(sem)

                for wi, m in [(0, "k"), (1, "v")]:
                    @blk.tensor
                    def _(t, o=o, wi=wi, m=m):
                        C.wait(t, toks[("wld", o)])
                        if o == 0 and wi == 0:
                            C.wait(t, toks[("lp2", "r", 3)])
                            C.wait(t, toks[("xbf", 3)])
                        for tt in range(NT):
                            tsl = slice(tt * 512, (tt + 1) * 512)
                            bank = PS[tt]
                            if wi > 0:
                                C.wait(t, toks[("kvev", o, 0, tt)])
                            elif o > 0:
                                if tt < 2:
                                    C.wait(t, toks[("srev", o - 1, 2 + tt)])
                                else:
                                    C.wait(t, toks[("eev", o - 1, tt)])
                            for ch in range(NCH):
                                ins = t.matmul(bank[:],
                                               Wb[wi][o % 2][:, ch * 128:(ch + 1) * 128],
                                               rhs_for(m, ch, tsl),
                                               start=(ch == 0), stop=(ch == NCH - 1))
                            toks[("kvmm", o, wi, tt)] = C.inc(ins, "TE")

                    if wi == 0:
                        @blk.scalar
                        def _(s, o=o):
                            s = C.ser(s, "sAC")
                            if o >= 1:
                                C.wait(s._raw, toks[("p2done", o - 1)])
                            for tt in range(NT):
                                tsl = slice(tt * 512, (tt + 1) * 512)
                                C.wait(s._raw, toks[("kvmm", o, 0, tt)])
                                toks[("kvev", o, 0, tt)] = s.activation(
                                    out=K[:, tsl], in_=PS[tt][:], func=AF.Exp)
                    else:
                        @blk.vector
                        def _(v, o=o):
                            v = C.ser(v, "sDV")
                            for tt in range(NT):
                                tsl = slice(tt * 512, (tt + 1) * 512)
                                C.wait(v._raw, toks[("kvmm", o, 1, tt)])
                                toks[("kvev", o, 1, tt)] = v.tensor_copy(
                                    V[:, tsl], PS[tt][:])

                @blk.vector
                def _(v, o=o):
                    v = C.ser(v, "sDV")
                    if o >= 1:
                        C.wait(v._raw, toks[("p2done", o - 1)])
                    v.tensor_scalar(out=SR[:], in0=K[:], scalar1=0.0,
                                    scalar2=ew[:, o:o + 1], op0=ALU.mult, op1=ALU.add)
                    v.tensor_tensor(out=T1[:], in0=K[:], in1=V[:], op=ALU.mult)
                    v.memset(buf_a[:, 0:1], 0.0)
                    v.memset(buf_b[:, 0:1], 0.0)
                    v.tensor_tensor_scan(out=buf_a[:, 1:TL + 1], data0=SR[:],
                                         data1=T1[:], initial=0.0,
                                         op0=ALU.mult, op1=ALU.add)
                    toks[("scan", o)] = v.tensor_tensor_scan(
                        out=buf_b[:, 1:TL + 1], data0=SR[:], data1=K[:],
                        initial=0.0, op0=ALU.mult, op1=ALU.add)

                @blk.sync
                def _(q, o=o):
                    C.wait(q, toks[("scan", o)])
                    sem = f"SC{o % 2}"
                    ins = q.dma_start(out=d_car_i[o].ap()[0:1, :].rearrange("a b -> b a"),
                                      in_=buf_a[:, TL:TL + 1])
                    C.inc16(ins, sem)
                    ins = q.dma_start(out=d_car_i[o].ap()[1:2, :].rearrange("a b -> b a"),
                                      in_=buf_b[:, TL:TL + 1])
                    C.inc16(ins, sem)
                    toks[("cdma", o)] = C.tok(sem)

                @blk.gpsimd
                def _(g, o=o):
                    C.wait(g, toks[("cdma", o)])
                    g.collective_compute("AllGather", ALU.bypass, replica_groups=PAIRS,
                                         ins=[d_car_i[o].ap().opt()],
                                         outs=[d_car_o[o].ap().opt()]).then_inc(cc_c[o])
                    g.wait_ge(cc_c[o], 1)
                    if o >= 1:
                        C.wait(g, toks[("acolused", o - 1)])
                    ins = g.dma_start(out=acol[:, 0:1],
                                      in_=d_car_o[o].ap()[0:1, :].rearrange("a b -> b a"))
                    C.inc16(ins, "GDc")
                    ins = g.dma_start(out=acol[:, 1:2],
                                      in_=d_car_o[o].ap()[1:2, :].rearrange("a b -> b a"))
                    C.inc16(ins, "GDc")
                    toks[("cld", o)] = C.tok("GDc")

                for tt in range(NT):
                    @blk.tensor
                    def _(t, o=o, tt=tt):
                        tsl = slice(tt * 512, (tt + 1) * 512)
                        bank = PS[2 + tt % 2]
                        C.wait(t, toks[("scan", o)])
                        if tt >= 2:
                            C.wait(t, toks[("eev", o, tt - 2)])
                        else:
                            C.wait(t, toks[("kvev", o, 1, 2 + tt)])
                        C.wait(t, toks["wrowd"])
                        ins = t.matmul(bank[:], wrow[0:1, o * 128:(o + 1) * 128],
                                       trow[0:1, tsl], start=True, stop=True)
                        toks[("emm", o, tt)] = C.inc(ins, "TE")

                    @blk.scalar
                    def _(s, o=o, tt=tt):
                        s = C.ser(s, "sAC")
                        tsl = slice(tt * 512, (tt + 1) * 512)
                        C.wait(s._raw, toks[("emm", o, tt)])
                        toks[("eev", o, tt)] = s.activation(
                            out=V[:, tsl], in_=PS[2 + tt % 2][:], func=AF.Exp)

                @blk.vector
                def _(v, o=o):
                    v = C.ser(v, "sDV")
                    C.wait(v._raw, toks[("cld", o)])
                    toks[("acg", o)] = v.tensor_scalar(
                        out=acol[:], in0=acol[:], scalar1=flags[:, 1:2],
                        scalar2=None, op0=ALU.mult)

                @blk.vector
                def _(v, o=o):
                    v = C.ser(v, "sDV")
                    eucol = eu[:, o:o + 1]
                    C.wait(v._raw, toks[("eev", o, NT - 1)])
                    C.wait(v._raw, toks[("acg", o)])
                    v.tensor_scalar(out=T1[:], in0=T1[:], scalar1=eucol,
                                    scalar2=None, op0=ALU.mult)
                    v.tensor_tensor(out=T1[:], in0=buf_a[:, 0:TL], in1=T1[:],
                                    op=ALU.add)
                    v.scalar_tensor_tensor(out=T1[:], in0=V[:], scalar=acol[:, 0:1],
                                           op0=ALU.mult, in1=T1[:], op1=ALU.add)
                    v.tensor_scalar(out=K[:], in0=K[:], scalar1=eucol,
                                    scalar2=None, op0=ALU.mult)
                    v.tensor_tensor(out=K[:], in0=buf_b[:, 0:TL], in1=K[:],
                                    op=ALU.add)
                    tk = v.scalar_tensor_tensor(out=K[:], in0=V[:], scalar=acol[:, 1:2],
                                                op0=ALU.mult, in1=K[:], op1=ALU.add)
                    toks[("acolused", o)] = tk
                    toks[("dn", o)] = tk

                @blk.scalar
                def _(s, o=o):
                    s = C.ser(s, "sAC")
                    C.wait(s._raw, toks[("dn", o)])
                    s.activation(out=K[:], in_=K[:], func=AF.Ln)
                    toks[("rec", o)] = s.activation(out=K[:], in_=K[:], func=AF.Exp,
                                                    scale=-1.0)

                for tt in range(NT):
                    @blk.tensor
                    def _(t, o=o, tt=tt):
                        tsl = slice(tt * 512, (tt + 1) * 512)
                        bank = PS[tt % 2]
                        C.wait(t, toks[("scan", o)])
                        if tt < 2:
                            C.wait(t, toks[("kvev", o, 1, tt)])
                        else:
                            C.wait(t, toks[("srev", o, tt - 2)])
                        for ch in range(NCH):
                            ins = t.matmul(bank[:],
                                           Wb[2][o % 2][:, ch * 128:(ch + 1) * 128],
                                           rhs_for("r", ch, tsl),
                                           start=(ch == 0), stop=(ch == NCH - 1))
                        toks[("rmm", o, tt)] = C.inc(ins, "TE")

                    @blk.scalar
                    def _(s, o=o, tt=tt):
                        s = C.ser(s, "sAC")
                        tsl = slice(tt * 512, (tt + 1) * 512)
                        C.wait(s._raw, toks[("rmm", o, tt)])
                        toks[("srev", o, tt)] = s.activation(
                            out=SR[:, tsl], in_=PS[tt % 2][:], func=AF.Sigmoid)
                toks[("mmdone", o)] = toks[("rmm", o, NT - 1)]

                @blk.vector
                def _(v, o=o):
                    v = C.ser(v, "sDV")
                    C.wait(v._raw, toks[("rec", o)])
                    C.wait(v._raw, toks[("srev", o, NT - 1)])
                    if o >= 1:
                        C.wait(v._raw, toks[("swd", o - 1)])
                    v.tensor_tensor(out=T1[:], in0=T1[:], in1=K[:], op=ALU.mult)
                    tk = v.tensor_tensor(out=srw[:], in0=T1[:], in1=SR[:],
                                         op=ALU.mult)
                    toks[("p2done", o)] = tk
                    toks[("srwv", o)] = tk

                @blk.sync
                def _(q, o=o):
                    C.wait(q, toks[("srwv", o)])
                    sem = f"SW{o % 2}"
                    ins = q.dma_start(out=d_srw.ap()[o * 128:(o + 1) * 128, :],
                                      in_=srw[:])
                    C.inc16(ins, sem)
                    toks[("swd", o)] = C.tok(sem)

        srl = [[C.sb(f"l3a_srl{r}{c}", [128, 512], BF16) for c in range(NCH)]
               for r in range(2)]
        Wo2 = [C.sb(f"l3a_Wo{r}", [128, DM], BF16) for r in range(2)]
        for tt in range(NT):
            @blk.sync
            def _(q, tt=tt):
                tsl = slice(tt * 512, (tt + 1) * 512)
                if tt == 0:
                    C.wait(q, toks[("swd", NCH - 1)])
                    C.wait(q, toks[("swd", NCH - 2)])
                if tt >= 2:
                    C.wait(q, toks[("fmm", tt - 2, NCH - 1)])
                sem = f"SL{tt % 2}"
                for ch in range(NCH):
                    ins = q.dma_start(out=srl[tt % 2][ch][:],
                                      in_=d_srw.ap()[ch * 128:(ch + 1) * 128, tsl])
                    C.inc16(ins, sem)
                toks[("srl", tt)] = C.tok(sem)

            for o in range(NCH):
                @blk.gpsimd
                def _(g, tt=tt, o=o):
                    k = tt * NCH + o
                    if k >= 2:
                        km = k - 2
                        C.wait(g, toks[("fmm", km // NCH, km % NCH)])
                    else:
                        C.wait(g, toks[("mmdone", NCH - 1)])
                    ins = g.dma_start(out=Wo2[k % 2][:], in_=C.dram["l3a_wo"][o])
                    C.inc16(ins, f"GO{k % 2}")
                    toks[("wol", tt, o)] = C.tok(f"GO{k % 2}")

                @blk.tensor
                def _(t, tt=tt, o=o):
                    k = tt * NCH + o
                    tsl = slice(tt * 512, (tt + 1) * 512)
                    C.wait(t, toks[("srl", tt)])
                    C.wait(t, toks[("wol", tt, o)])
                    bank = PS[k % 2]
                    if k >= 2:
                        C.wait(t, toks[("fev", k - 2)])
                    elif k < 2:
                        C.wait(t, toks[("srev", NCH - 1, NT - 1)])
                    for ch in range(NCH):
                        ins = t.matmul(bank[:], Wo2[k % 2][:, ch * 128:(ch + 1) * 128],
                                       srl[tt % 2][ch][:],
                                       start=(ch == 0), stop=(ch == NCH - 1))
                    toks[("fmm", tt, o)] = C.inc(ins, "TE")

                @blk.vector
                def _(v, tt=tt, o=o):
                    v = C.ser(v, "sDV")
                    k = tt * NCH + o
                    tsl = slice(tt * 512, (tt + 1) * 512)
                    C.wait(v._raw, toks[("fmm", tt, o)])
                    toks[("fev", k)] = v.tensor_tensor(
                        out=C.x[o][:, tsl], in0=PS[k % 2][:], in1=C.x[o][:, tsl],
                        op=ALU.add)

        C.set_fence([toks[("fev", NT * NCH - 1)], toks[("fev", NT * NCH - 2)]])


def emit_l2_attn(C):
    nc = C.nc
    blk = C.block
    NH, HD = 16, 64
    VW = NH * (HD + 1)
    ALEN = {0: 2, 1: 4}
    with C.scope():
        toks = {}
        qidx = C.sbc["l2_qidx"]
        kidx = C.sbc["l2_kidx"]
        aidx = C.sbc["l2_aidx"]
        jflag = C.sbc["l2_jflag"]
        bq = C.sbc["l2_bq"]
        bk = C.sbc["l2_bk"]
        bent = C.sbc["l2_bent"]
        bo01 = C.sbc["l2_bo01"]
        trimask = C.sb("l2_trim", [128, 4 * 512], BF16)
        bva = C.sb("l2_bva", [1, VW], BF16)
        d_qin = nc.dram_tensor("l2_qin", [2 * DM, 1024], BF16)
        d_qout = nc.dram_tensor("l2_qout", [4 * DM, 1024], BF16)
        d_kin = nc.dram_tensor("l2_kin", [2 * DM, 1024], BF16)
        d_kout = nc.dram_tensor("l2_kout", [4 * DM, 1024], BF16)
        d_vin = nc.dram_tensor("l2_vin", [2 * 1024, VW], BF16)
        d_vout = nc.dram_tensor("l2_vout", [4 * 1024, VW], BF16)
        d_aoin = nc.dram_tensor("l2_aoin", [2 * DM, 1024], BF16)
        d_aoout = nc.dram_tensor("l2_aoout", [4 * DM, 1024], BF16)
        cc_q = C.fresh_cc_sem()
        cc_k = C.fresh_cc_sem()
        cc_v = C.fresh_cc_sem()
        cc_a = C.fresh_cc_sem()

        @blk.gpsimd
        def _(g):
            C.fw(g)
            ins = g.dma_start(out=trimask[:], in_=C.dram["l2_trimask"])
            C.inc16(ins, "GDg")
            ins = g.dma_start(out=bva[:], in_=C.dram["l2_bvad"])
            C.inc16(ins, "GDg")
            toks["consts"] = C.tok("GDg")

        with C.scope():
            xn = [C.sb(f"l2_xn{ch}", [128, TL], BF16) for ch in range(NCH)]
            ln_toks = emit_ln(C, "l2ln", C.x, xn, 1e-6, gamma=C.sbc["l2_g"],
                              beta=C.sbc["l2_b"])
            PSq = [C.ps(f"l2_psq{i}", [128, 512], F32) for i in range(2)]
            PSv = [C.ps(f"l2_psv{i}", [128, 512], F32) for i in range(3)]
            PSw = C.ps("l2_psw", [128, 16], F32)
            Wst = [C.sb(f"l2_Wst{r}", [128, DM], BF16) for r in range(2)]
            ptile = [C.sb(f"l2_pt{r}", [128, 512], BF16) for r in range(2)]
            vtile = [C.sb(f"l2_vt{r}", [128, VW], BF16) for r in range(2)]
            Wva = [C.sb(f"l2_wva{ch}", [128, VW], BF16) for ch in range(NCH)]
            wentb = C.sb("l2_wentb", [128, NCH], BF16)
            gcol = C.sb("l2_gcol", [128, 1], F32)

            @blk.gpsimd
            def _(g):
                for tkn in ln_toks:
                    C.wait(g, tkn)
                ins = g.dma_start(out=wentb[:], in_=C.dram["l2_wentd"])
                C.inc16(ins, "GDg")
                for ch in range(NCH):
                    ins = g.dma_start(out=Wva[ch][:],
                                      in_=C.dram["l2_wva"][ch * 128:(ch + 1) * 128, :])
                    C.inc16(ins, "GDg")
                toks["wva"] = C.tok("GDg")

            for wi, wname, bias, dest in [(0, "l2_wq", bq, d_qin),
                                          (1, "l2_wk", bk, d_kin)]:
                for o in range(NCH):
                    wu = wi * NCH + o
                    wslot = wu % 2

                    @blk.gpsimd
                    def _(g, wu=wu, wname=wname, o=o, wslot=wslot):
                        if wu >= 2:
                            C.wait(g, toks[("qkmm", (wu - 2) * 4 + 3)])
                        ins = g.dma_start(out=Wst[wslot][:], in_=C.dram[wname][o])
                        C.inc16(ins, f"GW{wslot}")
                        toks[("qkw", wu)] = C.tok(f"GW{wslot}")

                    for j in range(2):
                        for half in range(2):
                            u = wu * 4 + j * 2 + half

                            @blk.tensor
                            def _(t, wu=wu, o=o, j=j, half=half, u=u, wslot=wslot):
                                if u == 0:
                                    for tkn in ln_toks:
                                        C.wait(t, tkn)
                                C.wait(t, toks[("qkw", wu)])
                                if u >= 2:
                                    C.wait(t, toks[("qkev", u - 2)])
                                bank = PSq[u % 2]
                                cbase = j * 1024 + half * 512
                                for ch in range(NCH):
                                    ins = t.matmul(
                                        bank[:], Wst[wslot][:, ch * 128:(ch + 1) * 128],
                                        xn[ch][:, cbase:cbase + 512],
                                        start=(ch == 0), stop=(ch == NCH - 1))
                                toks[("qkmm", u)] = C.inc(ins, "TE")

                            @blk.vector
                            def _(v, o=o, u=u, bias=bias):
                                v = C.ser(v, "sDV")
                                C.wait(v._raw, toks[("qkmm", u)])
                                if u >= 2:
                                    C.wait(v._raw, toks[("qkdma", u - 2)])
                                toks[("qkev", u)] = v.tensor_scalar(
                                    out=ptile[u % 2][:], in0=PSq[u % 2][:],
                                    scalar1=bias[:, o:o + 1], scalar2=None,
                                    op0=ALU.add)

                            @blk.sync
                            def _(q, o=o, j=j, half=half, u=u, dest=dest):
                                C.wait(q, toks[("qkev", u)])
                                ins = q.dma_start(
                                    out=dest.ap()[j * DM + o * 128:
                                                  j * DM + (o + 1) * 128,
                                                  half * 512:(half + 1) * 512],
                                    in_=ptile[u % 2][:])
                                C.inc16(ins, f"SQ{u % 2}")
                                toks[("qkdma", u)] = C.tok(f"SQ{u % 2}")

            NSPLIT = [(0, 512), (512, 512), (1024, 16)]
            for tt in range(16):
                @blk.tensor
                def _(t, tt=tt):
                    if tt == 0:
                        C.wait(t, toks["wva"])
                        for tkn in ln_toks:
                            C.wait(t, tkn)
                    if tt >= 1:
                        C.wait(t, toks[("gact", tt - 1)])
                    for ch in range(NCH):
                        ins = t.matmul(PSw[:, 0:1],
                                       xn[ch][:, tt * 128:(tt + 1) * 128],
                                       wentb[:, ch:ch + 1],
                                       start=(ch == 0), stop=(ch == NCH - 1))
                    toks[("gmm", tt)] = C.inc(ins, "TE")
                    for gi, (cb, cn) in enumerate(NSPLIT):
                        bank = PSv[gi]
                        if tt >= 1:
                            C.wait(t, toks[("vev", tt - 1, gi)])
                        for ch in range(NCH):
                            t.matmul(bank[:, 0:cn],
                                     xn[ch][:, tt * 128:(tt + 1) * 128],
                                     Wva[ch][:, cb:cb + cn],
                                     start=(ch == 0), stop=False)
                        ins = t.matmul(bank[:, 0:cn], C.ones_bf_row[:],
                                       bva[:, cb:cb + cn], start=False, stop=True)
                        toks[("vmm", tt, gi)] = C.inc(ins, "TE")

                @blk.scalar
                def _(s, tt=tt):
                    s = C.ser(s, "sAC")
                    C.wait(s._raw, toks[("gmm", tt)])
                    if tt >= 1:
                        C.wait(s._raw, toks[("vgate", tt - 1)])
                    toks[("gact", tt)] = s.activation(out=gcol[:], in_=PSw[:, 0:1],
                                                      func=AF.Sigmoid, bias=bent[:])

                @blk.vector
                def _(v, tt=tt):
                    v = C.ser(v, "sDV")
                    C.wait(v._raw, toks[("gact", tt)])
                    v.tensor_scalar(out=gcol[:], in0=gcol[:], scalar1=0.1,
                                    scalar2=2.0, op0=ALU.max, op1=ALU.min)
                    if tt >= 2:
                        C.wait(v._raw, toks[("vdma", tt - 2)])
                    for gi, (cb, cn) in enumerate(NSPLIT):
                        C.wait(v._raw, toks[("vmm", tt, gi)])
                        toks[("vev", tt, gi)] = v.tensor_copy(
                            vtile[tt % 2][:, cb:cb + cn], PSv[gi][:, 0:cn])
                    va = vtile[tt % 2][:].rearrange("p (h w) -> p h w", w=HD + 1)
                    toks[("vgate", tt)] = v.tensor_scalar(
                        out=va[:, :, 0:HD], in0=va[:, :, 0:HD],
                        scalar1=gcol[:], scalar2=None, op0=ALU.mult)

                @blk.sync
                def _(q, tt=tt):
                    C.wait(q, toks[("vgate", tt)])
                    ins = q.dma_start(out=d_vin.ap()[tt * 128:(tt + 1) * 128, :],
                                      in_=vtile[tt % 2][:])
                    C.inc16(ins, f"SV{tt % 2}")
                    toks[("vdma", tt)] = C.tok(f"SV{tt % 2}")

        @blk.gpsimd
        def _(g):
            for u in range(60, 64):
                C.wait(g, toks[("qkdma", u)])
            C.wait(g, toks[("vdma", 14)])
            C.wait(g, toks[("vdma", 15)])
            g.collective_compute("AllGather", ALU.bypass, replica_groups=PAIRS,
                                 ins=[d_qin.ap().opt()],
                                 outs=[d_qout.ap().opt()]).then_inc(cc_q)
            g.collective_compute("AllGather", ALU.bypass, replica_groups=PAIRS,
                                 ins=[d_kin.ap().opt()],
                                 outs=[d_kout.ap().opt()]).then_inc(cc_k)
            g.collective_compute("AllGather", ALU.bypass, replica_groups=PAIRS,
                                 ins=[d_vin.ap().opt()],
                                 outs=[d_vout.ap().opt()]).then_inc(cc_v)

        PSs = [C.ps(f"l2_pss{i}", [128, 512], F32) for i in range(4)]
        with C.scope():
            qful = [C.sb(f"l2_qf{ch}", [128, 1024], BF16) for ch in range(NCH)]
            kslot = [C.sb(f"l2_ks{ch}", [128, 1024], BF16) for ch in range(NCH)]
            vslot = [C.sb(f"l2_vs{kt}", [128, VW], BF16) for kt in range(NCH)]
            etile = [C.sb(f"l2_et{i}", [128, 512], BF16) for i in range(4)]
            accV = [C.sb(f"l2_acc{h}", [65, 1024], F32) for h in range(NH)]
            aot = [C.sb(f"l2_aot{r}", [64, 512], BF16) for r in range(2)]
            rrow = C.sb("l2_rrow", [1, 512], F32)
            PSav = [C.ps(f"l2_psav{i}", [65, 512], F32) for i in range(2)]
            PSbc = [C.ps(f"l2_psbc{i}", [64, 512], F32) for i in range(2)]

            def sc_group(t, ei, qo, h, kts, u, qh):
                for kt in kts:
                    bank = PSs[kt % 4]
                    ch = h // 2
                    rb = 64 * (h % 2)
                    masked = ei == 0 and ((qo == 0 and kt < 4)
                                          or (qo == 1 and kt >= 4))
                    ins = t.matmul(bank[:],
                                   kslot[ch][rb:rb + 64, kt * 128:(kt + 1) * 128],
                                   qful[ch][rb:rb + 64, qo * 512:(qo + 1) * 512],
                                   start=True, stop=not masked)
                    if masked:
                        v_ = kt if qo == 0 else kt - 4
                        ins = t.matmul(bank[:], C.ident_bf[:],
                                       trimask[:, v_ * 512:(v_ + 1) * 512],
                                       start=False, stop=True)
                    toks[("scmm", u, kt)] = C.inc(ins, "TE")

            for a in range(2):
                @blk.gpsimd
                def _(g, a=a):
                    g.wait_ge(cc_q, 1)
                    if a == 1:
                        C.wait(g, toks[("explast", (1, ALEN[0] - 1, NH - 1))])
                    for ch in range(NCH):
                        ins = g.indirect_dma_start(
                            out=qful[ch][:], out_offset=None, in_=d_qout.ap(),
                            in_offset=bass.IndirectOffsetOnAxis(
                                ap=qidx[:, a * NCH + ch:a * NCH + ch + 1], axis=0))
                        C.inc16(ins, f"GQ{a % 2}")
                    toks[("qfill", a)] = C.tok(f"GQ{a % 2}")

                for ei in range(ALEN[a]):
                    @blk.gpsimd
                    def _(g, a=a, ei=ei):
                        g.wait_ge(cc_k, 1)
                        g.wait_ge(cc_v, 1)
                        if "prev_e" in toks:
                            C.wait(g, toks[("entdone", toks["prev_e"])])
                        e = (0 if a == 0 else ALEN[0]) + ei
                        for ch in range(NCH):
                            ins = g.indirect_dma_start(
                                out=kslot[ch][:], out_offset=None, in_=d_kout.ap(),
                                in_offset=bass.IndirectOffsetOnAxis(
                                    ap=kidx[:, e * NCH + ch:e * NCH + ch + 1],
                                    axis=0))
                            C.inc16(ins, "GK")
                        for kt in range(NCH):
                            ins = g.indirect_dma_start(
                                out=vslot[kt][:], out_offset=None, in_=d_vout.ap(),
                                in_offset=bass.IndirectOffsetOnAxis(
                                    ap=kidx[:, e * NCH + kt:e * NCH + kt + 1],
                                    axis=0))
                            C.inc16(ins, "GK")
                        toks[("kvfill", a, ei)] = C.tok("GK")

                    for qo in range(2):
                        qh = a * 2 + qo
                        short = (ei == 0 and qo == 0)
                        kts1 = [0, 1, 2, 3]
                        kts2 = [] if short else [4, 5, 6, 7]
                        for h in range(NH):
                            u = (qh, ei, h)

                            @blk.tensor
                            def _(t, a=a, ei=ei, qo=qo, h=h, u=u, kts1=kts1, qh=qh):
                                C.wait(t, toks[("qfill", a)])
                                C.wait(t, toks[("kvfill", a, ei)])
                                if "explast_prev" in toks:
                                    C.wait(t, toks["explast_prev"])
                                sc_group(t, ei, qo, h, kts1, u, qh)

                            @blk.scalar
                            def _(s, qh=qh, ei=ei, h=h, u=u, kts1=kts1):
                                s = C.ser(s, "sAC")
                                if "av_prev" in toks:
                                    C.wait(s._raw, toks["av_prev"])
                                for kt in kts1:
                                    C.wait(s._raw, toks[("scmm", u, kt)])
                                    col = (qh * 6 + ei) * 8 + kt
                                    toks[("exp", u, kt)] = s.activation(
                                        out=etile[kt % 4][:], in_=PSs[kt % 4][:],
                                        func=AF.Exp, scale=1.25,
                                        bias=jflag[:, col:col + 1])

                            @blk.tensor
                            def _(t, ei=ei, qo=qo, h=h, u=u, kts1=kts1, kts2=kts2,
                                  short=short, qh=qh):
                                bank = PSav[h % 2]
                                if ("accev2", h % 2) in toks:
                                    C.wait(t, toks[("accev2", h % 2)])
                                for kt in kts1:
                                    C.wait(t, toks[("exp", u, kt)])
                                    ins = t.matmul(
                                        bank[:], vslot[kt][:, 65 * h:65 * h + 65],
                                        etile[kt % 4][:], start=(kt == 0),
                                        stop=(short and kt == 3))
                                    if kt == kts1[-1]:
                                        C.inc(ins, "TE")
                                if not short:
                                    sc_group(t, ei, qo, h, kts2, u, qh)

                            if not short:
                                @blk.scalar
                                def _(s, qh=qh, ei=ei, h=h, u=u, kts2=kts2):
                                    s = C.ser(s, "sAC")
                                    for kt in kts2:
                                        C.wait(s._raw, toks[("scmm", u, kt)])
                                        col = (qh * 6 + ei) * 8 + kt
                                        toks[("exp", u, kt)] = s.activation(
                                            out=etile[kt % 4][:], in_=PSs[kt % 4][:],
                                            func=AF.Exp, scale=1.25,
                                            bias=jflag[:, col:col + 1])

                                @blk.tensor
                                def _(t, h=h, u=u, kts2=kts2):
                                    bank = PSav[h % 2]
                                    for kt in kts2:
                                        C.wait(t, toks[("exp", u, kt)])
                                        ins = t.matmul(
                                            bank[:],
                                            vslot[kt][:, 65 * h:65 * h + 65],
                                            etile[kt % 4][:], start=False,
                                            stop=(kt == 7))
                                    toks[("avlast", u)] = C.inc(ins, "TE")
                            else:
                                toks[("avlast", u)] = C.tok("TE")

                            toks[("explast", u)] = toks[("exp", u,
                                                         3 if short else 7)]
                            toks["explast_prev"] = toks[("explast", u)]
                            toks["av_prev"] = toks[("avlast", u)]

                            @blk.vector
                            def _(v, ei=ei, qo=qo, h=h, u=u):
                                v = C.ser(v, "sDV")
                                C.wait(v._raw, toks[("avlast", u)])
                                dst = accV[h][:, qo * 512:(qo + 1) * 512]
                                if ei == 0:
                                    tk = v.tensor_copy(dst, PSav[h % 2][:])
                                else:
                                    tk = v.tensor_tensor(out=dst, in0=PSav[h % 2][:],
                                                         in1=dst, op=ALU.add)
                                toks[("accev", u)] = tk
                                toks[("accev2", h % 2)] = tk

                    toks["prev_e"] = (a, ei)
                    toks[("entdone", (a, ei))] = toks[("accev",
                                                       (a * 2 + 1, ei, NH - 1))]

                for qo in range(2):
                    qh = a * 2 + qo
                    for h in range(NH):
                        @blk.scalar
                        def _(s, a=a, qo=qo, qh=qh, h=h):
                            s = C.ser(s, "sAC")
                            C.wait(s._raw,
                                   toks[("accev", (a * 2 + qo, ALEN[a] - 1, h))])
                            if "nbc_prev" in toks:
                                C.wait(s._raw, toks["nbc_prev"])
                            s.activation(out=rrow[:],
                                         in_=accV[h][64:65, qo * 512:(qo + 1) * 512],
                                         func=AF.Ln)
                            toks[("nrec", qh, h)] = s.activation(
                                out=rrow[:], in_=rrow[:], func=AF.Exp, scale=-1.0)

                        @blk.tensor
                        def _(t, qh=qh, h=h):
                            C.wait(t, toks[("nrec", qh, h)])
                            if ("nmul2", h % 2) in toks:
                                C.wait(t, toks[("nmul2", h % 2)])
                            ins = t.matmul(PSbc[h % 2][:], C.ones_f[:, 0:64],
                                           rrow[:], start=True, stop=True)
                            toks[("nbc", qh, h)] = C.inc(ins, "TE")
                            toks["nbc_prev"] = toks[("nbc", qh, h)]

                        @blk.vector
                        def _(v, qo=qo, qh=qh, h=h):
                            v = C.ser(v, "sDV")
                            C.wait(v._raw, toks[("nbc", qh, h)])
                            if ("ndma2", h % 2) in toks:
                                C.wait(v._raw, toks[("ndma2", h % 2)])
                            tk = v.tensor_tensor(
                                out=aot[h % 2][:],
                                in0=accV[h][0:64, qo * 512:(qo + 1) * 512],
                                in1=PSbc[h % 2][:], op=ALU.mult)
                            toks[("nmul", qh, h)] = tk
                            toks[("nmul2", h % 2)] = tk

                        @blk.sync
                        def _(q, a=a, qo=qo, qh=qh, h=h):
                            C.wait(q, toks[("nmul", qh, h)])
                            ins = q.dma_start(
                                out=d_aoin.ap()[a * DM + h * 64:a * DM + (h + 1) * 64,
                                                qo * 512:(qo + 1) * 512],
                                in_=aot[h % 2][:])
                            C.inc16(ins, f"SN{h % 2}")
                            toks[("ndma", qh, h)] = C.tok(f"SN{h % 2}")
                            toks[("ndma2", h % 2)] = toks[("ndma", qh, h)]

        @blk.gpsimd
        def _(g):
            for qh in range(4):
                for h in [NH - 2, NH - 1]:
                    C.wait(g, toks[("ndma", qh, h)])
            g.collective_compute("AllGather", ALU.bypass, replica_groups=PAIRS,
                                 ins=[d_aoin.ap().opt()],
                                 outs=[d_aoout.ap().opt()]).then_inc(cc_a)

        astr = [C.sb(f"l2_as{ch}", [128, 1024], BF16) for ch in range(NCH)]
        Wo2 = [C.sb(f"l2_wo{r}", [128, DM], BF16) for r in range(2)]
        l2tmp = [C.sb(f"l2_tmp{r}", [128, 512], F32) for r in range(2)]
        for m in range(2):
            @blk.gpsimd
            def _(g, m=m):
                g.wait_ge(cc_a, 1)
                if m == 1:
                    C.wait(g, toks[("fmm", 0, NCH - 1, 1)])
                for ch in range(NCH):
                    ins = g.indirect_dma_start(
                        out=astr[ch][:], out_offset=None, in_=d_aoout.ap(),
                        in_offset=bass.IndirectOffsetOnAxis(
                            ap=aidx[:, m * NCH + ch:m * NCH + ch + 1], axis=0))
                    C.inc16(ins, f"GA{m % 2}")
                toks[("afill", m)] = C.tok(f"GA{m % 2}")

            for o in range(NCH):
                @blk.gpsimd
                def _(g, m=m, o=o):
                    k2 = m * NCH + o
                    if k2 >= 2:
                        km = k2 - 2
                        C.wait(g, toks[("fmm", km // NCH, km % NCH, 1)])
                    ins = g.dma_start(out=Wo2[k2 % 2][:], in_=C.dram["l2_wo"][o])
                    C.inc16(ins, f"GO{k2 % 2}")
                    toks[("wol", m, o)] = C.tok(f"GO{k2 % 2}")

                for half in range(2):
                    @blk.tensor
                    def _(t, m=m, o=o, half=half):
                        k2 = m * NCH + o
                        uu = k2 * 2 + half
                        C.wait(t, toks[("afill", m)])
                        C.wait(t, toks[("wol", m, o)])
                        if uu >= 2:
                            pu = uu - 2
                            C.wait(t, toks[("fact", pu // 2 // NCH,
                                            (pu // 2) % NCH, pu % 2)])
                        bank = PSs[uu % 2]
                        for ch in range(NCH):
                            ins = t.matmul(bank[:],
                                           Wo2[k2 % 2][:, ch * 128:(ch + 1) * 128],
                                           astr[ch][:, half * 512:(half + 1) * 512],
                                           start=(ch == 0), stop=(ch == NCH - 1))
                        toks[("fmm", m, o, half)] = C.inc(ins, "TE")

                    @blk.scalar
                    def _(s, m=m, o=o, half=half):
                        s = C.ser(s, "sAC")
                        uu = (m * NCH + o) * 2 + half
                        C.wait(s._raw, toks[("fmm", m, o, half)])
                        if uu >= 2:
                            pu = uu - 2
                            C.wait(s._raw, toks[("fev", pu // 2 // NCH,
                                                 (pu // 2) % NCH, pu % 2)])
                        toks[("fact", m, o, half)] = s.activation(
                            out=l2tmp[uu % 2][:], in_=PSs[uu % 2][:],
                            func=AF.Identity, scale=0.1, bias=bo01[:, o:o + 1])

                    @blk.vector
                    def _(v, m=m, o=o, half=half):
                        v = C.ser(v, "sDV")
                        uu = (m * NCH + o) * 2 + half
                        C.wait(v._raw, toks[("fact", m, o, half)])
                        xsl = slice(m * 1024 + half * 512,
                                    m * 1024 + (half + 1) * 512)
                        toks[("fev", m, o, half)] = v.tensor_tensor(
                            out=C.x[o][:, xsl], in0=l2tmp[uu % 2][:],
                            in1=C.x[o][:, xsl], op=ALU.add)

        C.set_fence([toks[("fev", 1, NCH - 1, 1)], toks[("fev", 1, NCH - 2, 1)]])


def emit_l2_ffn(C):
    nc = C.nc
    blk = C.block
    E_R, F = 8, 4096
    NFB = F // 128
    FBG = 8
    NG = NFB // FBG
    with C.scope():
        toks = {}
        rbias = C.sbc["l2f_rbias"]
        cent = C.sbc["l2f_cent"]
        xbf = [C.sb(f"l2f_xb{ch}", [128, TL], BF16) for ch in range(NCH)]
        hgrp = [C.sb(f"l2f_h{fb}", [128, TL], BF16) for fb in range(FBG)]
        Wfc = [[C.sb(f"l2f_wfc{r}_{i}", [128, DM], BF16) for i in range(FBG)]
               for r in range(2)]
        Wpj = [C.sb(f"l2f_wpj{i}", [128, DM], BF16) for i in range(NCH)]
        cbc = [C.sb(f"l2f_cbc{tt}", [128, 512], BF16) for tt in range(NT)]
        onehot = C.sb("l2f_oh", [8, 8 * 128], BF16)
        comb_sb = C.sb("l2f_comb", [8, TL], BF16)
        scr = C.sb("l2f_scr", [128, 8], F32)
        bal = C.sb("l2f_bal", [128, 8], F32)
        eq1 = C.sb("l2f_eq1", [128, 8], F32)
        m1 = C.sb("l2f_m1", [128, 1], F32)
        s12 = C.sb("l2f_s12", [128, 4], F32)
        ctok = C.sb("l2f_ctok", [128, 8], BF16)
        t8 = C.sb("l2f_t8", [128, 8], F32)
        PSr = C.ps("l2f_psr", [128, 512], F32)
        PStr = C.ps("l2f_pstr", [8, 128], BF16)
        PSf = [C.ps(f"l2f_psf{i}", [128, 512], F32) for i in range(4)]
        PSp = [C.ps(f"l2f_psp{i}", [128, 512], F32) for i in range(2)]

        @blk.gpsimd
        def _(g):
            C.fw(g)
            ins = g.dma_start(out=onehot[:], in_=C.dram["l2f_onehot"])
            C.inc16(ins, "GDg")
            toks["oh"] = C.tok("GDg")

        @blk.vector
        def _(v):
            v = C.ser(v, "sDV")
            C.fw(v._raw)
            for ch in range(NCH):
                toks[("xbf", ch)] = v.tensor_copy(xbf[ch][:], C.x[ch][:])

        for tt2 in range(16):
            @blk.tensor
            def _(t, tt2=tt2):
                if tt2 == 0:
                    C.fw(t)
                    C.wait(t, toks[("xbf", NCH - 1)])
                    C.wait(t, toks["oh"])
                if tt2 >= 1:
                    C.wait(t, toks[("badd", tt2 - 1)])
                for ch in range(NCH):
                    ins = t.matmul(PSr[:, 0:8],
                                   C.x[ch][:, tt2 * 128:(tt2 + 1) * 128],
                                   cent[:, ch * 8:(ch + 1) * 8],
                                   start=(ch == 0), stop=(ch == NCH - 1))
                toks[("rmm", tt2)] = C.inc(ins, "TE")

            @blk.scalar
            def _(s, tt2=tt2):
                s = C.ser(s, "sAC")
                C.wait(s._raw, toks[("rmm", tt2)])
                if tt2 >= 1:
                    C.wait(s._raw, toks[("comb", tt2 - 1)])
                toks[("rsig", tt2)] = s.activation(out=scr[:], in_=PSr[:, 0:8],
                                                   func=AF.Sigmoid)

            @blk.vector
            def _(v, tt2=tt2):
                v = C.ser(v, "sDV")
                C.wait(v._raw, toks[("rsig", tt2)])
                toks[("badd", tt2)] = v.tensor_tensor(out=bal[:], in0=PSr[:, 0:8],
                                                      in1=rbias[:], op=ALU.add)

            @blk.scalar
            def _(s, tt2=tt2):
                s = C.ser(s, "sAC")
                C.wait(s._raw, toks[("badd", tt2)])
                toks[("bsig", tt2)] = s.activation(out=bal[:], in_=bal[:],
                                                   func=AF.Sigmoid)

            @blk.vector
            def _(v, tt2=tt2):
                v = C.ser(v, "sDV")
                C.wait(v._raw, toks[("bsig", tt2)])
                v.reduce_max(m1[:], bal[:], axis=AX.X)
                v.tensor_scalar(out=eq1[:], in0=bal[:], scalar1=m1[:], scalar2=None,
                                op0=ALU.is_ge)
                v.tensor_tensor(out=t8[:], in0=scr[:], in1=eq1[:], op=ALU.mult)
                v.reduce_sum(s12[:, 0:1], t8[:], axis=AX.X)
                v.scalar_tensor_tensor(out=bal[:], in0=eq1[:], scalar=-1e30,
                                       op0=ALU.mult, in1=bal[:], op1=ALU.add)
                v.reduce_max(m1[:], bal[:], axis=AX.X)
                v.tensor_scalar(out=bal[:], in0=bal[:], scalar1=m1[:], scalar2=None,
                                op0=ALU.is_ge)
                v.tensor_tensor(out=t8[:], in0=scr[:], in1=bal[:], op=ALU.mult)
                v.reduce_sum(s12[:, 1:2], t8[:], axis=AX.X)
                toks[("sdiff", tt2)] = v.tensor_tensor(
                    out=s12[:, 2:3], in0=s12[:, 0:1], in1=s12[:, 1:2],
                    op=ALU.subtract)

            @blk.scalar
            def _(s, tt2=tt2):
                s = C.ser(s, "sAC")
                C.wait(s._raw, toks[("sdiff", tt2)])
                if tt2 >= 1:
                    C.wait(s._raw, toks[("comb", tt2 - 1)])
                toks[("wsig", tt2)] = s.activation(out=s12[:, 2:3], in_=s12[:, 2:3],
                                                   func=AF.Sigmoid)

            @blk.vector
            def _(v, tt2=tt2):
                v = C.ser(v, "sDV")
                C.wait(v._raw, toks[("wsig", tt2)])
                v.tensor_scalar(out=s12[:, 3:4], in0=s12[:, 2:3], scalar1=-1.0,
                                scalar2=1.0, op0=ALU.mult, op1=ALU.add)
                v.tensor_scalar(out=ctok[:], in0=bal[:], scalar1=s12[:, 3:4],
                                scalar2=None, op0=ALU.mult)
                toks[("comb", tt2)] = v.scalar_tensor_tensor(
                    out=ctok[:], in0=eq1[:], scalar=s12[:, 2:3], op0=ALU.mult,
                    in1=ctok[:], op1=ALU.add)

            @blk.tensor
            def _(t, tt2=tt2):
                C.wait(t, toks[("comb", tt2)])
                C.wait(t, toks["oh"])
                if tt2 >= 1:
                    C.wait(t, toks[("combsb", tt2 - 1)])
                ins = t.transpose(PStr[:], ctok[:], C.ident_bf[:])
                toks[("trc", tt2)] = C.inc(ins, "TE")

            @blk.vector
            def _(v, tt2=tt2):
                v = C.ser(v, "sDV")
                C.wait(v._raw, toks[("trc", tt2)])
                toks[("combsb", tt2)] = v.tensor_copy(
                    comb_sb[:, tt2 * 128:(tt2 + 1) * 128], PStr[:])

        @blk.vector
        def _(v):
            v = C.ser(v, "sDV")
            C.wait(v._raw, toks[("rmm", 15)])
            for ch in range(NCH):
                toks[("x2", ch)] = v.tensor_scalar(out=C.x[ch][:], in0=C.x[ch][:],
                                                   scalar1=2.0, scalar2=None,
                                                   op0=ALU.mult)

        for e in range(10):
            routed = e < E_R
            if routed:
                @blk.tensor
                def _(t, e=e):
                    if e == 0:
                        C.wait(t, toks[("combsb", 15)])
                    dv0 = C.sem("sDV")[1]
                    for tt in range(NT):
                        if e >= 1:
                            C.wait(t, toks[("hsc", e - 1, FBG - 1, tt)])
                        if tt >= 1:
                            C.wait(t, C.future("sDV", tt))
                        ins = t.matmul(PSr[:], onehot[:, e * 128:(e + 1) * 128],
                                       comb_sb[:, tt * 512:(tt + 1) * 512],
                                       start=True, stop=True)
                        toks[("cbcmm", e, tt)] = C.inc(ins, "TE")

                @blk.vector
                def _(v, e=e):
                    v = C.ser(v, "sDV")
                    for tt in range(NT):
                        C.wait(v._raw, toks[("cbcmm", e, tt)])
                        toks[("cbcev", e, tt)] = v.tensor_copy(cbc[tt][:], PSr[:])

            for fg in range(NG):
                kk = e * NG + fg

                @blk.gpsimd
                def _(g, e=e, fg=fg, kk=kk):
                    if kk >= 2:
                        C.wait(g, toks[("fcmm", kk - 2, FBG - 1, NT - 1)])
                    for i in range(FBG):
                        ins = g.dma_start(out=Wfc[kk % 2][i][:],
                                          in_=C.dram["l2f_wfc"][e, fg * FBG + i])
                        C.inc16(ins, f"GW{kk % 2}")
                    toks[("wld", kk)] = C.tok(f"GW{kk % 2}")
                    if kk >= 1:
                        C.wait(g, toks[("pjmm", kk - 1, NCH - 1, NT - 1)])
                    for do in range(NCH):
                        ins = g.dma_start(out=Wpj[do][:],
                                          in_=C.dram["l2f_wpj"][e, do, fg])
                        C.inc16(ins, "GWP")
                    toks[("wldp", kk)] = C.tok("GWP")

                @blk.tensor
                def _(t, e=e, fg=fg, kk=kk, routed=routed):
                    C.wait(t, toks[("wld", kk)])
                    if routed:
                        C.wait(t, toks[("cbcev", e, NT - 1)])
                    ac0 = C.sem("sAC")[1]
                    for i in range(FBG):
                        for tt in range(NT):
                            k2 = i * NT + tt
                            bank = PSf[k2 % 4]
                            if k2 >= 4:
                                C.wait(t, ("sAC", ac0 + k2 - 3))
                            elif kk >= 1 and k2 == 0:
                                C.wait(t, toks[("gelu", kk - 1, FBG - 1, NT - 1)])
                            for ch in range(NCH):
                                ins = t.matmul(bank[:],
                                               Wfc[kk % 2][i][:, ch * 128:(ch + 1) * 128],
                                               xbf[ch][:, tt * 512:(tt + 1) * 512],
                                               start=(ch == 0), stop=(ch == NCH - 1))
                            toks[("fcmm", kk, i, tt)] = C.inc(ins, "TE")

                @blk.scalar
                def _(s, e=e, kk=kk):
                    s = C.ser(s, "sAC")
                    if kk >= 1:
                        C.wait(s._raw, toks[("pjmm", kk - 1, NCH - 1, NT - 1)])
                    for i in range(FBG):
                        for tt in range(NT):
                            k2 = i * NT + tt
                            C.wait(s._raw, toks[("fcmm", kk, i, tt)])
                            toks[("gelu", kk, i, tt)] = s.activation(
                                out=hgrp[i][:, tt * 512:(tt + 1) * 512],
                                in_=PSf[k2 % 4][:], func=AF.Gelu)

                @blk.vector
                def _(v, e=e, kk=kk, routed=routed):
                    v = C.ser(v, "sDV")
                    for i in range(FBG):
                        for tt in range(NT):
                            C.wait(v._raw, toks[("gelu", kk, i, tt)])
                            if routed:
                                tk = v.tensor_tensor(
                                    out=hgrp[i][:, tt * 512:(tt + 1) * 512],
                                    in0=hgrp[i][:, tt * 512:(tt + 1) * 512],
                                    in1=cbc[tt][:], op=ALU.mult)
                            else:
                                tk = toks[("gelu", kk, i, tt)]
                            toks[("hsc", e, i, tt)] = tk
                            toks[("hsc2", kk, i * NT + tt)] = tk

                @blk.tensor
                def _(t, kk=kk, e=e):
                    C.wait(t, toks[("wldp", kk)])
                    dv0 = C.sem("sDV")[1]
                    for do in range(NCH):
                        for tt in range(NT):
                            k2 = do * NT + tt
                            bank = PSp[k2 % 2]
                            if k2 >= 2:
                                C.wait(t, ("sDV", dv0 + k2 - 1))
                            else:
                                C.wait(t, ("sDV", dv0))
                            C.wait(t, toks[("hsc", e, FBG - 1, tt)])
                            for i in range(FBG):
                                ins = t.matmul(bank[:],
                                               Wpj[do][:, i * 128:(i + 1) * 128],
                                               hgrp[i][:, tt * 512:(tt + 1) * 512],
                                               start=(i == 0), stop=(i == FBG - 1))
                            toks[("pjmm", kk, do, tt)] = C.inc(ins, "TE")

                @blk.vector
                def _(v, kk=kk):
                    v = C.ser(v, "sDV")
                    for do in range(NCH):
                        for tt in range(NT):
                            C.wait(v._raw, toks[("pjmm", kk, do, tt)])
                            if kk == 0:
                                C.wait(v._raw, toks[("x2", do)])
                            xsl = slice(tt * 512, (tt + 1) * 512)
                            v.tensor_tensor(out=C.x[do][:, xsl],
                                            in0=PSp[(do * NT + tt) % 2][:],
                                            in1=C.x[do][:, xsl], op=ALU.add)

        C.set_fence([C.tok("sDV"), C.tok("TE")])


def emit_l1_attn(C):
    nc = C.nc
    blk = C.block
    NH, HD = 16, 64
    GAM = [1.0 - 2.0 ** (-5.0 - hh) for hh in range(NH)]
    with C.scope():
        toks = {}
        rmsw = C.sbc["l1a_rmsw"]
        gnw = C.sbc["l1a_gnw"]
        gnb = C.sbc["l1a_gnb"]
        bgat = C.sbc["l1a_bgat"]
        bproj = C.sbc["l1a_bproj"]
        flags = C.sbc["c_flags"]
        x_ = [C.sb(f"l1a_xn{ch}", [128, TL], BF16) for ch in range(NCH)]
        vtok = [C.sb(f"l1a_v{t2}", [128, DM], BF16) for t2 in range(16)]
        q2 = C.sb("l1a_q2", [128, TL], BF16)
        k2 = C.sb("l1a_k2", [128, TL], BF16)
        tri01 = C.sb("l1a_tri01", [128, 4 * 512], BF16)
        sq1 = C.sb("l1a_sq1", [128, 512], BF16)
        t512 = [C.sb(f"l1a_t5{i}", [128, 512], F32) for i in range(1)]
        rrms = C.sb("l1a_rrms", [128, NCH], F32)
        r_tmp = C.sb("l1a_rt1", [1, 512], F32)
        r_tmp2 = C.sb("l1a_rt2", [1, 512], F32)
        nrmbc = [C.sb(f"l1a_nb{tt}", [128, 512], BF16) for tt in range(NT)]
        y = [C.sb(f"l1a_y{ch}", [128, TL], BF16) for ch in range(NCH)]
        PB = [C.ps(f"l1a_pb{i}", [128, 512], F32) for i in range(4)]
        PAV = [C.ps(f"l1a_pav{i}", [64, 512], F32) for i in range(2)]
        PS_S = C.ps("l1a_pss", [64, 64], F32)
        PTR = C.ps("l1a_ptr", [128, 64], BF16)
        d_si = nc.dram_tensor("l1a_si", [128, 64], BF16)
        d_so = [nc.dram_tensor(f"l1a_so{ch}", [256, 64], BF16) for ch in range(NCH)]
        cc_s = [C.fresh_cc_sem() for ch in range(NCH)]

        for tt in range(NT):
            sl = slice(tt * 512, (tt + 1) * 512)

            @blk.scalar
            def _(s, tt=tt, sl=sl):
                s = C.ser(s, "sAC")
                if tt == 0:
                    C.fw(s._raw)
                te0 = C.sem("TE")[1]
                for ch in range(NCH):
                    k = tt * NCH + ch
                    if ch >= 1:
                        C.wait(s._raw, ("TE", te0 + ch))
                    elif tt >= 1:
                        C.wait(s._raw, toks[("rsmm", tt * NCH - 1)])
                    toks[("rssq", k)] = s.activation(
                        out=sq1[:], in_=C.x[ch][:, sl], func=AF.Square)

            @blk.tensor
            def _(t, tt=tt):
                if tt == 0:
                    C.fw(t)
                if tt >= 2:
                    C.wait(t, toks[("rsln", tt - 2)])
                for ch in range(NCH):
                    C.wait(t, toks[("rssq", tt * NCH + ch)])
                    ins = t.matmul(PB[tt % 2][0:1, :], C.ones_bf[:], sq1[:],
                                   start=(ch == 0), stop=(ch == NCH - 1))
                    toks[("rsmm", tt * NCH + ch)] = C.inc(ins, "TE")

            @blk.scalar
            def _(s, tt=tt):
                s = C.ser(s, "sAC")
                C.wait(s._raw, toks[("rsmm", tt * NCH + NCH - 1)])
                s.activation(out=r_tmp[:], in_=PB[tt % 2][0:1, :], func=AF.Ln)
                toks[("rsln", tt)] = s.activation(out=r_tmp[:], in_=r_tmp[:],
                                                  func=AF.Exp, scale=0.5)

            @blk.vector
            def _(v, tt=tt):
                v = C.ser(v, "sDV")
                C.wait(v._raw, toks[("rsln", tt)])
                toks[("rsd", tt)] = v.tensor_scalar(
                    out=r_tmp[:], in0=r_tmp[:], scalar1=1.0 / 32.0,
                    scalar2=1e-8, op0=ALU.mult, op1=ALU.max)

            @blk.tensor
            def _(t, tt=tt):
                C.wait(t, toks[("rsd", tt)])
                if tt >= 2:
                    C.wait(t, toks[("nbc", tt - 2)])
                ins = t.matmul(PB[2 + tt % 2][:], C.ones_f[:], r_tmp[:],
                               start=True, stop=True)
                toks[("nbcm", tt)] = C.inc(ins, "TE")

            @blk.vector
            def _(v, tt=tt):
                v = C.ser(v, "sDV")
                C.wait(v._raw, toks[("nbcm", tt)])
                toks[("nbc", tt)] = v.tensor_copy(nrmbc[tt][:], PB[2 + tt % 2][:])

            @blk.scalar
            def _(s, tt=tt):
                s = C.ser(s, "sAC")
                C.wait(s._raw, toks[("nbc", tt)])
                s.activation(out=r_tmp2[:], in_=r_tmp[:], func=AF.Ln)
                toks[("rsr", tt)] = s.activation(out=r_tmp2[:], in_=r_tmp2[:],
                                                 func=AF.Exp, scale=-1.0)

            @blk.tensor
            def _(t, tt=tt):
                C.wait(t, toks[("rsr", tt)])
                if tt >= 2:
                    C.wait(t, toks[("rsap", tt - 2, NCH - 1)])
                ins = t.matmul(PB[2 + tt % 2][:], C.ones_f[:], r_tmp2[:],
                               start=True, stop=True)
                toks[("rsbc", tt)] = C.inc(ins, "TE")

            @blk.vector
            def _(v, tt=tt, sl=sl):
                v = C.ser(v, "sDV")
                C.wait(v._raw, toks[("rsbc", tt)])
                for ch in range(NCH):
                    v.tensor_tensor(out=t512[0][:], in0=C.x[ch][:, sl],
                                    in1=PB[2 + tt % 2][:], op=ALU.mult)
                    toks[("rsap", tt, ch)] = v.tensor_scalar(
                        out=x_[ch][:, sl], in0=t512[0][:],
                        scalar1=rmsw[:, ch:ch + 1], scalar2=None, op0=ALU.mult)

        @blk.vector
        def _(v):
            v = C.ser(v, "sDV")
            C.wait(v._raw, toks[("rsap", NT - 1, NCH - 1)])
            for ch in range(NCH):
                toks[("x2", ch)] = v.tensor_scalar(out=C.x[ch][:], in0=C.x[ch][:],
                                                   scalar1=2.0, scalar2=None,
                                                   op0=ALU.mult)
            toks["rrms"] = v.reciprocal(rrms[:], rmsw[:])

        @blk.gpsimd
        def _(g):
            C.fw(g)
            ins = g.dma_start(out=tri01[:], in_=C.dram["l1a_tri01"])
            C.inc16(ins, "GDg")
            toks["tri"] = C.tok("GDg")

        with C.scope():
            Wv = [C.sb(f"l1a_wv{c2}", [128, DM], BF16) for c2 in range(NCH)]

            @blk.gpsimd
            def _(g):
                for c2 in range(NCH):
                    ins = g.dma_start(out=Wv[c2][:],
                                      in_=C.dram["l1a_wv"][c2 * 128:(c2 + 1) * 128, :])
                    C.inc16(ins, "GVV")
                toks["wv"] = C.tok("GVV")

            for t2 in range(16):
                @blk.tensor
                def _(t, t2=t2):
                    C.wait(t, toks["wv"])
                    C.wait(t, toks[("rsap", NT - 1, NCH - 1)])
                    dv0 = C.sem("sDV")[1]
                    for half in range(2):
                        if t2 >= 1:
                            C.wait(t, ("sDV", dv0 - 1 + half))
                        for c2 in range(NCH):
                            ins = t.matmul(PB[2 + half][:],
                                           x_[c2][:, t2 * 128:(t2 + 1) * 128],
                                           Wv[c2][:, half * 512:(half + 1) * 512],
                                           start=(c2 == 0), stop=(c2 == NCH - 1))
                        toks[("vmm", t2, half)] = C.inc(ins, "TE")

                @blk.vector
                def _(v, t2=t2):
                    v = C.ser(v, "sDV")
                    for half in range(2):
                        C.wait(v._raw, toks[("vmm", t2, half)])
                        toks[("vev", t2, half)] = v.tensor_copy(
                            vtok[t2][:, half * 512:(half + 1) * 512],
                            PB[2 + half][:])

        ret_scope = contextlib.ExitStack()
        C.es_stack.append(ret_scope)
        tabA = C.sb("l1a_tabA", [128, TL], BF16)
        tabB = C.sb("l1a_tabB", [128, TL], BF16)
        W4 = [C.sb(f"l1a_w4{r}", [128, DM], BF16) for r in range(2)]
        ktok = C.sb("l1a_ktok", [128, 1024], BF16)
        Smine = C.sb("l1a_smine", [128, 64], BF16)
        Sx = C.sb("l1a_sx", [128, 64], BF16)
        etile = C.sb("l1a_et", [128, 512], BF16)
        for ch in range(NCH):
            @blk.gpsimd
            def _(g, ch=ch):
                if ch >= 1:
                    C.wait(g, toks[("kap", ch - 1, NT - 1)])
                else:
                    C.wait(g, toks[("vev", 15, 1)])
                for wi, wname in enumerate(["l1a_wq", "l1a_wqp"]):
                    ins = g.dma_start(out=W4[wi][:], in_=C.dram[wname][ch])
                    C.inc16(ins, "GW4")
                ins = g.dma_start(out=tabA[:],
                                  in_=C.dram["l1a_cosq"][ch * 128:(ch + 1) * 128, :])
                C.inc16(ins, "GW4")
                ins = g.dma_start(out=tabB[:],
                                  in_=C.dram["l1a_sinq"][ch * 128:(ch + 1) * 128, :])
                C.inc16(ins, "GW4")
                toks[("qld", ch)] = C.tok("GW4")

            for tt in range(NT):
                sl = slice(tt * 512, (tt + 1) * 512)

                @blk.tensor
                def _(t, ch=ch, tt=tt, sl=sl):
                    C.wait(t, toks[("qld", ch)])
                    if tt >= 1:
                        C.wait(t, toks[("qap", ch, tt - 1)])
                    elif ch >= 1:
                        C.wait(t, toks[("kap", ch - 1, NT - 1)])
                        C.wait(t, toks[("gnap", 2 * ch - 1, NT - 1)])
                    for wi in range(2):
                        for c2 in range(NCH):
                            ins = t.matmul(PB[wi][:],
                                           W4[wi][:, c2 * 128:(c2 + 1) * 128],
                                           x_[c2][:, sl],
                                           start=(c2 == 0), stop=(c2 == NCH - 1))
                        toks[("qmm", ch, tt, wi)] = C.inc(ins, "TE")

                @blk.vector
                def _(v, ch=ch, tt=tt, sl=sl):
                    v = C.ser(v, "sDV")
                    C.wait(v._raw, toks[("qmm", ch, tt, 1)])
                    v.tensor_tensor(out=t512[0][:], in0=PB[0][:],
                                    in1=tabA[:, sl], op=ALU.mult)
                    v.tensor_tensor(out=q2[:, sl], in0=PB[1][:],
                                    in1=tabB[:, sl], op=ALU.mult)
                    toks[("qap", ch, tt)] = v.tensor_tensor(
                        out=q2[:, sl], in0=q2[:, sl], in1=t512[0][:], op=ALU.add)

            @blk.gpsimd
            def _(g, ch=ch):
                C.wait(g, toks[("qap", ch, NT - 1)])
                C.wait(g, toks[("qmm", ch, NT - 1, 1)])
                for wi, wname in enumerate(["l1a_wk", "l1a_wkp"]):
                    ins = g.dma_start(out=W4[wi][:], in_=C.dram[wname][ch])
                    C.inc16(ins, "GW4")
                ins = g.dma_start(out=tabA[:],
                                  in_=C.dram["l1a_cosk"][ch * 128:(ch + 1) * 128, :])
                C.inc16(ins, "GW4")
                ins = g.dma_start(out=tabB[:],
                                  in_=C.dram["l1a_sink"][ch * 128:(ch + 1) * 128, :])
                C.inc16(ins, "GW4")
                toks[("kld", ch)] = C.tok("GW4")

            for tt in range(NT):
                sl = slice(tt * 512, (tt + 1) * 512)

                @blk.tensor
                def _(t, ch=ch, tt=tt, sl=sl):
                    C.wait(t, toks[("kld", ch)])
                    if tt >= 1:
                        C.wait(t, toks[("kap", ch, tt - 1)])
                    for wi in range(2):
                        for c2 in range(NCH):
                            ins = t.matmul(PB[wi][:],
                                           W4[wi][:, c2 * 128:(c2 + 1) * 128],
                                           x_[c2][:, sl],
                                           start=(c2 == 0), stop=(c2 == NCH - 1))
                        toks[("kmm", ch, tt, wi)] = C.inc(ins, "TE")

                @blk.vector
                def _(v, ch=ch, tt=tt, sl=sl):
                    v = C.ser(v, "sDV")
                    C.wait(v._raw, toks[("kmm", ch, tt, 1)])
                    v.tensor_tensor(out=t512[0][:], in0=PB[0][:],
                                    in1=tabA[:, sl], op=ALU.mult)
                    v.tensor_tensor(out=k2[:, sl], in0=PB[1][:],
                                    in1=tabB[:, sl], op=ALU.mult)
                    toks[("kap", ch, tt)] = v.tensor_tensor(
                        out=k2[:, sl], in0=k2[:, sl], in1=t512[0][:], op=ALU.add)

            for hh in range(2):
                h = 2 * ch + hh
                rb = 64 * hh

                @blk.tensor
                def _(t, ch=ch, hh=hh, rb=rb):
                    C.wait(t, toks[("kap", ch, NT - 1)])
                    dv0 = C.sem("sDV")[1]
                    for st in range(16):
                        if st >= 1:
                            C.wait(t, ("sDV", dv0 + st))
                        ins = t.transpose(PTR[:],
                                          k2[rb:rb + 64, st * 128:(st + 1) * 128],
                                          C.ident_bf[rb:rb + 64, rb:rb + 64])
                        toks[("ktr", ch, hh, st)] = C.inc(ins, "TE")

                @blk.vector
                def _(v, ch=ch, hh=hh):
                    v = C.ser(v, "sDV")
                    if hh == 1:
                        C.wait(v._raw, toks[("smm", ch, 0)])
                    for st in range(16):
                        C.wait(v._raw, toks[("ktr", ch, hh, st)])
                        toks[("ktev", ch, hh, st)] = v.tensor_copy(
                            ktok[:, st * 64:(st + 1) * 64], PTR[:, 0:64])

                @blk.tensor
                def _(t, ch=ch, hh=hh, h=h):
                    for st in range(16):
                        C.wait(t, toks[("ktev", ch, hh, st)])
                        ins = t.matmul(PS_S[:], ktok[:, st * 64:(st + 1) * 64],
                                       vtok[st][:, h * 64:(h + 1) * 64],
                                       start=(st == 0), stop=(st == 15))
                    toks[("smm", ch, hh)] = C.inc(ins, "TE")

                @blk.vector
                def _(v, ch=ch, hh=hh, h=h, rb=rb):
                    v = C.ser(v, "sDV")
                    C.wait(v._raw, toks[("smm", ch, hh)])
                    if ch >= 1 and hh == 0:
                        C.wait(v._raw, toks[("sdma", ch - 1)])
                    toks[("sev", ch, hh)] = v.tensor_scalar(
                        out=Smine[rb:rb + 64, :], in0=PS_S[:],
                        scalar1=float(GAM[h] ** 2048), scalar2=None, op0=ALU.mult)

            @blk.sync
            def _(q, ch=ch):
                C.wait(q, toks[("sev", ch, 1)])
                ins = q.dma_start(out=d_si.ap(), in_=Smine[:])
                C.inc16(ins, f"SS{ch % 2}")
                toks[("sdma", ch)] = C.tok(f"SS{ch % 2}")

            @blk.gpsimd
            def _(g, ch=ch):
                C.wait(g, toks[("sdma", ch)])
                g.collective_compute("AllGather", ALU.bypass, replica_groups=PAIRS,
                                     ins=[d_si.ap().opt()],
                                     outs=[d_so[ch].ap().opt()]).then_inc(cc_s[ch])
                g.wait_ge(cc_s[ch], 1)
                if ch >= 1:
                    C.wait(g, toks[("crossdone", ch - 1)])
                ins = g.dma_start(out=Sx[:], in_=d_so[ch].ap()[0:128, :])
                C.inc16(ins, "GSX")
                toks[("sxld", ch)] = C.tok("GSX")

            @blk.vector
            def _(v, ch=ch):
                v = C.ser(v, "sDV")
                C.wait(v._raw, toks[("sxld", ch)])
                toks[("sxf", ch)] = v.tensor_scalar(
                    out=Sx[:], in0=Sx[:], scalar1=flags[:, 1:2], scalar2=None,
                    op0=ALU.mult)

            for hh in range(2):
                h = 2 * ch + hh
                rb = 64 * hh
                for Q in range(NT):
                    nst = 4 * Q + 4

                    @blk.tensor
                    def _(t, ch=ch, hh=hh, h=h, rb=rb, Q=Q, nst=nst):
                        dv0 = C.sem("sDV")[1]
                        C.wait(t, toks[("sxf", ch)])
                        if ("lastPAV", Q % 2) in toks:
                            C.wait(t, toks[("lastPAV", Q % 2)])
                        for st in range(nst):
                            bank = PB[2 + st % 2]
                            if st >= 2:
                                C.wait(t, ("sDV", dv0 + st - 1))
                            ins = t.matmul(bank[:],
                                           k2[rb:rb + 64, st * 128:(st + 1) * 128],
                                           q2[rb:rb + 64, Q * 512:(Q + 1) * 512],
                                           start=True, stop=True)
                            toks[("scm", h, Q, st)] = C.inc(ins, "TE")
                            C.wait(t, ("sDV", dv0 + st + 1))
                            ins = t.matmul(PAV[Q % 2][:],
                                           vtok[st][:, h * 64:(h + 1) * 64],
                                           etile[:], start=(st == 0), stop=False)
                            toks[("avm", h, Q, st)] = C.inc(ins, "TE")
                        ins = t.matmul(PAV[Q % 2][:], Sx[rb:rb + 64, :],
                                       q2[rb:rb + 64, Q * 512:(Q + 1) * 512],
                                       start=False, stop=True)
                        toks[("avf", h, Q)] = C.inc(ins, "TE")

                    @blk.vector
                    def _(v, h=h, Q=Q, nst=nst):
                        v = C.ser(v, "sDV")
                        C.wait(v._raw, toks["tri"])
                        for st in range(nst):
                            C.wait(v._raw, toks[("scm", h, Q, st)])
                            if st >= 1:
                                C.wait(v._raw, toks[("avm", h, Q, st - 1)])
                            if st >= 4 * Q:
                                vv = st - 4 * Q
                                v.tensor_tensor(
                                    out=etile[:], in0=PB[2 + st % 2][:],
                                    in1=tri01[:, vv * 512:(vv + 1) * 512],
                                    op=ALU.mult)
                            else:
                                v.tensor_copy(etile[:], PB[2 + st % 2][:])

                    @blk.vector
                    def _(v, ch=ch, rb=rb, Q=Q, h=h):
                        v = C.ser(v, "sDV")
                        C.wait(v._raw, toks[("avf", h, Q)])
                        toks[("yev", h, Q)] = v.tensor_copy(
                            y[ch][rb:rb + 64, Q * 512:(Q + 1) * 512],
                            PAV[Q % 2][:])
                        toks[("lastPAV", Q % 2)] = toks[("yev", h, Q)]

                    @blk.scalar
                    def _(s, ch=ch, rb=rb, Q=Q, h=h):
                        s = C.ser(s, "sAC")
                        C.wait(s._raw, toks[("yev", h, Q)])
                        toks[("gnsq", h, Q)] = s.activation(
                            out=sq1[0:64, :],
                            in_=y[ch][rb:rb + 64, Q * 512:(Q + 1) * 512],
                            func=AF.Square)

                    @blk.tensor
                    def _(t, ch=ch, rb=rb, Q=Q, h=h):
                        C.wait(t, toks[("gnsq", h, Q)])
                        if "gnvar_prev" in toks:
                            C.wait(t, toks["gnvar_prev"])
                        ins = t.matmul(PB[0][0:1, :], C.ones_bf[rb:rb + 64, :],
                                       y[ch][rb:rb + 64, Q * 512:(Q + 1) * 512],
                                       start=True, stop=True)
                        ins = t.matmul(PB[0][32:33, :], C.ones_bf[0:64, :],
                                       sq1[0:64, :], start=True, stop=True)
                        toks[("gnst", h, Q)] = C.inc(ins, "TE")

                    @blk.vector
                    def _(v, h=h, Q=Q):
                        v = C.ser(v, "sDV")
                        C.wait(v._raw, toks[("gnst", h, Q)])
                        if "gnbc_prev" in toks:
                            C.wait(v._raw, toks["gnbc_prev"])
                        v.tensor_scalar(out=r_tmp[:], in0=PB[0][0:1, :],
                                        scalar1=1.0 / 64.0, scalar2=None,
                                        op0=ALU.mult)
                        v.scalar_tensor_tensor(out=r_tmp2[:], in0=r_tmp[:],
                                               scalar=-1.0, op0=ALU.mult,
                                               in1=r_tmp[:], op1=ALU.mult)
                        toks[("gnvar", h, Q)] = v.scalar_tensor_tensor(
                            out=r_tmp2[:], in0=PB[0][32:33, :], scalar=1.0 / 64.0,
                            op0=ALU.mult, in1=r_tmp2[:], op1=ALU.add)
                        toks["gnvar_prev"] = toks[("gnvar", h, Q)]

                    @blk.scalar
                    def _(s, h=h, Q=Q):
                        s = C.ser(s, "sAC")
                        C.wait(s._raw, toks[("gnvar", h, Q)])
                        if "gnbc_prev" in toks:
                            C.wait(s._raw, toks["gnbc_prev"])
                        s.activation(out=r_tmp2[:], in_=r_tmp2[:], func=AF.Ln,
                                     bias=C.constf(1e-5, rows=1))
                        toks[("gnrs", h, Q)] = s.activation(
                            out=r_tmp2[:], in_=r_tmp2[:], func=AF.Exp, scale=-0.5)

                    @blk.vector
                    def _(v, h=h, Q=Q):
                        v = C.ser(v, "sDV")
                        C.wait(v._raw, toks[("gnrs", h, Q)])
                        toks[("gnnm", h, Q)] = v.scalar_tensor_tensor(
                            out=r_tmp[:], in0=r_tmp[:], scalar=-1.0, op0=ALU.mult,
                            in1=r_tmp2[:], op1=ALU.mult)

                    @blk.tensor
                    def _(t, h=h, Q=Q):
                        C.wait(t, toks[("gnnm", h, Q)])
                        if "gnap_prev" in toks:
                            C.wait(t, toks["gnap_prev"])
                        ins = t.matmul(PB[1][0:64, :], C.ones_f[:, 0:64], r_tmp2[:],
                                       start=True, stop=True)
                        ins = t.matmul(PB[1][64:128, :], C.ones_f[:, 0:64], r_tmp[:],
                                       start=True, stop=True)
                        toks[("gnbc", h, Q)] = C.inc(ins, "TE")
                        toks["gnbc_prev"] = toks[("gnbc", h, Q)]

                    @blk.vector
                    def _(v, ch=ch, rb=rb, Q=Q, h=h, hh=hh):
                        v = C.ser(v, "sDV")
                        C.wait(v._raw, toks[("gnbc", h, Q)])
                        ysl = y[ch][rb:rb + 64, Q * 512:(Q + 1) * 512]
                        v.tensor_tensor(out=t512[0][0:64, :], in0=ysl,
                                        in1=PB[1][0:64, :], op=ALU.mult)
                        v.tensor_tensor(out=t512[0][0:64, :], in0=t512[0][0:64, :],
                                        in1=PB[1][64:128, :], op=ALU.add)
                        toks[("gnap", h, Q)] = v.tensor_scalar(
                            out=ysl, in0=t512[0][0:64, :],
                            scalar1=gnw[rb:rb + 64, ch:ch + 1],
                            scalar2=gnb[rb:rb + 64, ch:ch + 1],
                            op0=ALU.mult, op1=ALU.add)
                        toks["gnap_prev"] = toks[("gnap", h, Q)]

            toks[("crossdone", ch)] = toks[("avf", 2 * ch + 1, NT - 1)]

        C.es_stack.pop().close()
        Wst = [C.sb(f"l1a_wst{r}", [128, DM], BF16) for r in range(2)]
        etile = C.sb("l1a_et2", [128, 512], BF16)
        for o in range(NCH):
            @blk.gpsimd
            def _(g, o=o):
                if o >= 2:
                    C.wait(g, toks[("gmm", o - 2, NT - 1)])
                else:
                    C.wait(g, toks[("gnap", NH - 1, NT - 1)])
                    C.wait(g, toks[("avf", NH - 1, NT - 1)])
                ins = g.dma_start(out=Wst[o % 2][:], in_=C.dram["l1a_wg"][o])
                C.inc16(ins, f"GG{o % 2}")
                toks[("gld", o)] = C.tok(f"GG{o % 2}")

            @blk.vector
            def _(v, o=o):
                v = C.ser(v, "sDV")
                C.wait(v._raw, toks[("gld", o)])
                if o == 0:
                    C.wait(v._raw, toks["rrms"])
                for c2 in range(NCH):
                    tk = v.tensor_scalar(out=Wst[o % 2][:, c2 * 128:(c2 + 1) * 128],
                                         in0=Wst[o % 2][:, c2 * 128:(c2 + 1) * 128],
                                         scalar1=rrms[:, c2:c2 + 1], scalar2=None,
                                         op0=ALU.mult)
                toks[("gsc", o)] = tk

            for tt in range(NT):
                @blk.tensor
                def _(t, o=o, tt=tt):
                    C.wait(t, toks[("gsc", o)])
                    if o == 0 and tt == 0:
                        C.wait(t, toks[("gnap", NH - 1, NT - 1)])
                    dv0 = C.sem("sDV")[1]
                    if not (o == 0 and tt == 0):
                        C.wait(t, ("sDV", dv0))
                    for c2 in range(NCH):
                        ins = t.matmul(PB[2][:],
                                       Wst[o % 2][:, c2 * 128:(c2 + 1) * 128],
                                       x_[c2][:, tt * 512:(tt + 1) * 512],
                                       start=(c2 == 0), stop=(c2 == NCH - 1))
                    toks[("gmm", o, tt)] = C.inc(ins, "TE")

                @blk.vector
                def _(v, o=o, tt=tt):
                    v = C.ser(v, "sDV")
                    C.wait(v._raw, toks[("gmm", o, tt)])
                    toks[("gmul", o, tt)] = v.tensor_tensor(
                        out=t512[0][:], in0=PB[2][:], in1=nrmbc[tt][:], op=ALU.mult)

                @blk.scalar
                def _(s, o=o, tt=tt):
                    s = C.ser(s, "sAC")
                    C.wait(s._raw, toks[("gmul", o, tt)])
                    toks[("gsilu", o, tt)] = s.activation(
                        out=etile[:], in_=t512[0][:], func=AF.Silu,
                        bias=bgat[:, o:o + 1])

                @blk.vector
                def _(v, o=o, tt=tt):
                    v = C.ser(v, "sDV")
                    C.wait(v._raw, toks[("gsilu", o, tt)])
                    toks[("ygate", o, tt)] = v.tensor_tensor(
                        out=y[o][:, tt * 512:(tt + 1) * 512],
                        in0=y[o][:, tt * 512:(tt + 1) * 512],
                        in1=etile[:], op=ALU.mult)

        for o in range(NCH):
            @blk.gpsimd
            def _(g, o=o):
                if o >= 2:
                    C.wait(g, toks[("pmm", o - 2, NT - 1)])
                else:
                    C.wait(g, toks[("gmm", NCH - 2 + o, NT - 1)])
                ins = g.dma_start(out=Wst[o % 2][:], in_=C.dram["l1a_wproj"][o])
                C.inc16(ins, f"GP{o % 2}")
                toks[("pld", o)] = C.tok(f"GP{o % 2}")

            for tt in range(NT):
                @blk.tensor
                def _(t, o=o, tt=tt):
                    C.wait(t, toks[("pld", o)])
                    if o == 0 and tt == 0:
                        C.wait(t, toks[("ygate", NCH - 1, NT - 1)])
                        C.wait(t, toks[("gmm", NCH - 1, NT - 1)])
                    dv0 = C.sem("sDV")[1]
                    if not (o == 0 and tt == 0):
                        C.wait(t, ("sDV", dv0))
                    for c2 in range(NCH):
                        ins = t.matmul(PB[3][:],
                                       Wst[o % 2][:, c2 * 128:(c2 + 1) * 128],
                                       y[c2][:, tt * 512:(tt + 1) * 512],
                                       start=(c2 == 0), stop=(c2 == NCH - 1))
                    toks[("pmm", o, tt)] = C.inc(ins, "TE")

                @blk.vector
                def _(v, o=o, tt=tt):
                    v = C.ser(v, "sDV")
                    C.wait(v._raw, toks[("pmm", o, tt)])
                    if o == 0 and tt == 0:
                        C.wait(v._raw, toks[("x2", NCH - 1)])
                    xsl = slice(tt * 512, (tt + 1) * 512)
                    toks[("pev", o, tt)] = v.scalar_tensor_tensor(
                        out=C.x[o][:, xsl], in0=PB[3][:],
                        scalar=bproj[:, o:o + 1], op0=ALU.add,
                        in1=C.x[o][:, xsl], op1=ALU.add)

        with C.scope():
            harena = [C.sb(f"l1a_ha{i}", [128, TL], BF16) for i in range(2)]
            Wfe = [C.sb(f"l1a_wfe{r}", [128, DM], BF16) for r in range(2)]
            Wffs = [C.sb(f"l1a_wffs{r}", [128, 256], BF16) for r in range(2)]
            for eg in range(8):
                @blk.gpsimd
                def _(g, eg=eg):
                    if eg >= 1:
                        C.wait(g, toks[("fcm", eg - 1, 1, NT - 1)])
                    for e2 in range(2):
                        ins = g.dma_start(out=Wfe[e2][:],
                                          in_=C.dram["l1a_wfe"][eg * 2 + e2])
                        C.inc16(ins, f"GF{e2}")
                    toks[("feld", eg)] = C.tok("GF1")
                    toks[("feld0", eg)] = C.tok("GF0")

                @blk.tensor
                def _(t, eg=eg):
                    C.wait(t, toks[("feld", eg)])
                    C.wait(t, toks[("feld0", eg)])
                    if eg == 0:
                        C.wait(t, toks[("pev", NCH - 1, NT - 1)])
                    ac0 = C.sem("sAC")[1]
                    for e2 in range(2):
                        for tt in range(NT):
                            k2_ = e2 * NT + tt
                            bank = PB[k2_ % 2]
                            if k2_ >= 2:
                                C.wait(t, ("sAC", ac0 + k2_ - 1))
                            elif eg >= 1:
                                C.wait(t, toks[("fgel", eg - 1, 1, NT - 1)])
                            for c2 in range(NCH):
                                ins = t.matmul(bank[:],
                                               Wfe[e2][:, c2 * 128:(c2 + 1) * 128],
                                               x_[c2][:, tt * 512:(tt + 1) * 512],
                                               start=(c2 == 0), stop=(c2 == NCH - 1))
                            toks[("fcm", eg, e2, tt)] = C.inc(ins, "TE")

                @blk.scalar
                def _(s, eg=eg):
                    s = C.ser(s, "sAC")
                    for e2 in range(2):
                        for tt in range(NT):
                            C.wait(s._raw, toks[("fcm", eg, e2, tt)])
                            toks[("fgel", eg, e2, tt)] = s.activation(
                                out=harena[e2][:, tt * 512:(tt + 1) * 512],
                                in_=PB[(e2 * NT + tt) % 2][:], func=AF.Gelu)

                for o in range(NCH):
                    @blk.gpsimd
                    def _(g, eg=eg, o=o):
                        if eg * NCH + o >= 2:
                            pk = eg * NCH + o - 2
                            C.wait(g, toks[("ffm", pk // NCH, pk % NCH, NT - 1)])
                        ins = g.dma_start(out=Wffs[o % 2][:],
                                          in_=C.dram["l1a_wff"][eg, o])
                        C.inc16(ins, f"GH{o % 2}")
                        toks[("ffld", eg, o)] = C.tok(f"GH{o % 2}")

                    @blk.tensor
                    def _(t, eg=eg, o=o):
                        C.wait(t, toks[("ffld", eg, o)])
                        C.wait(t, toks[("fgel", eg, 1, NT - 1)])
                        dv0 = C.sem("sDV")[1]
                        for tt in range(NT):
                            bank = PB[2 + tt % 2]
                            if tt >= 2:
                                C.wait(t, ("sDV", dv0 + tt - 1))
                            else:
                                C.wait(t, ("sDV", dv0))
                            for e2 in range(2):
                                ins = t.matmul(bank[:],
                                               Wffs[o % 2][:, e2 * 128:(e2 + 1) * 128],
                                               harena[e2][:, tt * 512:(tt + 1) * 512],
                                               start=(e2 == 0), stop=(e2 == 1))
                            toks[("ffm", eg, o, tt)] = C.inc(ins, "TE")

                    @blk.vector
                    def _(v, eg=eg, o=o):
                        v = C.ser(v, "sDV")
                        for tt in range(NT):
                            C.wait(v._raw, toks[("ffm", eg, o, tt)])
                            xsl = slice(tt * 512, (tt + 1) * 512)
                            toks[("ffev", eg, o, tt)] = v.tensor_tensor(
                                out=C.x[o][:, xsl], in0=PB[2 + tt % 2][:],
                                in1=C.x[o][:, xsl], op=ALU.add)

        C.set_fence([toks[("ffev", 7, NCH - 1, NT - 1)],
                     toks[("ffm", 7, NCH - 1, NT - 1)]])



def emit_l3_ffn(C):
    nc = C.nc
    blk = C.block
    NIT = L3F_NIT
    QTT = [(i * 512, 512) for i in range(4)]
    KTT = [(0, 256)] + [(256 + i * 512, 512) for i in range(4)]
    flags = C.sbc["c_flags"]
    pmask = C.sbc["l3f_pmask"]
    scols = {"q": (C.sbc["l3f_qs"], C.sbc["l3f_qsp"]),
             "k": (C.sbc["l3f_ks"], C.sbc["l3f_ksp"])}
    b1a = C.sbc["l3f_b1a"]
    b1g = C.sbc["l3f_b1g"]
    b2c = C.sbc["l3f_b2"]
    toks = {}
    with C.scope():
        wmask = C.sb("l3f_wmask", [128, 1024], BF16)
        permh = C.sb("l3f_permh", [128, 128], BF16)
        tabs = {nm: C.sb(f"l3f_t{nm}", [128, 2304], BF16)
                for nm in ["cq", "sq", "ck", "sk"]}
        PB = [C.ps(f"l3f_pb{i}", [128, 512], F32) for i in range(3)]
        PBC = C.ps("l3f_pbc", [128, 512], F32)
        ones33 = C.sb("l3f_ones33", [33, 128], F32)
        ones_b33 = C.sb("l3f_onesb33", [128, 33], BF16)
        bno = C.sb("l3f_bno", [128, 4], F32)
        d_x0 = nc.dram_tensor("l3f_x0", [DM, TL], F32)
        d_xp_i = nc.dram_tensor("l3f_xpi", [DM, 257], BF16)
        d_xp_o = [nc.dram_tensor(f"l3f_xpo{i}", [2 * DM, 257], BF16)
                  for i in range(NIT)]
        d_bn_i = nc.dram_tensor("l3f_bni", [128, 4], F32)
        d_bn_o = [nc.dram_tensor(f"l3f_bno{i}", [256, 4], F32)
                  for i in range(NIT)]
        cc_x = [C.fresh_cc_sem() for _ in range(NIT)]
        cc_b = [C.fresh_cc_sem() for _ in range(NIT)]

        @blk.gpsimd
        def _(g):
            C.fw(g)
            for nm in ["cq", "sq", "ck", "sk"]:
                ins = g.dma_start(out=tabs[nm][:], in_=C.dram[f"l3f_{nm}"])
                C.inc16(ins, "3GT")
            ins = g.dma_start(out=wmask[:], in_=C.dram["l3f_wmask"])
            C.inc16(ins, "3GT")
            ins = g.dma_start(out=permh[:], in_=C.dram["l3f_permh"])
            toks["tabs"] = C.inc16(ins, "3GT")
            for ch in range(NCH):
                ins = g.dma_start(out=d_x0.ap()[ch * 128:(ch + 1) * 128, :],
                                  in_=C.x[ch][:])
                C.inc16(ins, "3GS")
            toks["x0w"] = C.tok("3GS")

        @blk.vector
        def _(v):
            v = C.ser(v, "sDV")
            C.fw(v._raw)
            v.memset(ones33[:], 1.0)
            v.memset(ones_b33[:], 1.0)
            toks["pbz"] = v.memset(PB[2][:], 0.0)

        ifence = list(C.fence_toks)

        for it in range(NIT):
            @blk.gpsimd
            def _(g, it=it):
                for tk in ifence:
                    C.wait(g, tk)
                if it >= 1:
                    g.wait_ge(cc_x[it - 1], 1)
                for ch in range(NCH):
                    ins = g.dma_start(
                        out=d_xp_i.ap()[ch * 128:(ch + 1) * 128, :],
                        in_=C.x[ch][:, TL - 257:TL])
                    C.inc16(ins, "3GX")
                toks[("xpw", it)] = C.tok("3GX")

            @blk.gpsimd
            def _(g, it=it):
                C.wait(g, toks[("xpw", it)])
                g.collective_compute("AllGather", ALU.bypass,
                                     replica_groups=PAIRS,
                                     ins=[d_xp_i.ap().opt()],
                                     outs=[d_xp_o[it].ap().opt()]).then_inc(cc_x[it])

            with C.scope():
                xe = [C.sb(f"l3f_xe{ch}", [128, 2304], BF16) for ch in range(NCH)]
                vch = C.sb("l3f_vch", [128, 2304], BF16)
                q3 = C.sb("l3f_q3", [128, TL], BF16)
                k3 = C.sb("l3f_k3", [128, 2304], BF16)
                qb = C.sb("l3f_qb", [128, 512], BF16)
                sq1 = C.sb("l3f_sq1", [128, 512], BF16)
                et = C.sb("l3f_et", [128, 2048], BF16)
                aog = [C.sb(f"l3f_ao{i2}", [128, TL], BF16) for i2 in range(4)]
                t512 = C.sb("l3f_t512", [128, 512], F32)
                t512b = C.sb("l3f_t512b", [128, 512], F32)
                rbc = C.sb("l3f_rbc", [128, 512], F32)
                wrbc = C.sb("l3f_wrbc", [128, 256], F32)
                r_all = C.sb("l3f_rall", [33, 512], F32)
                xprs = [C.sb(f"l3f_xpr{i2}", [128, 257], BF16) for i2 in range(2)]
                wqs = [C.sb(f"l3f_wqs{i2}", [128, DM], BF16) for i2 in range(2)]
                wks = [C.sb(f"l3f_wks{i2}", [128, DM], BF16) for i2 in range(2)]
                wvs = [C.sb(f"l3f_wvs{i2}", [128, DM], BF16) for i2 in range(2)]
                wos = [C.sb(f"l3f_wos{i2}", [128, DM], BF16) for i2 in range(2)]
                PV = [C.ps(f"l3f_pv{i2}", [128, 512], F32) for i2 in range(2)]

                for ch in range(NCH):
                    @blk.gpsimd
                    def _(g, it=it, ch=ch):
                        if ch == 0:
                            g.wait_ge(cc_x[it], 1)
                        if ch >= 2:
                            C.wait(g, toks[("xem", it, ch - 2)])
                        ins = g.dma_start(
                            out=xprs[ch % 2][:],
                            in_=d_xp_o[it].ap()[ch * 128:(ch + 1) * 128, :])
                        toks[("xpl", it, ch)] = C.inc16(ins, f"3GX{ch % 2}")

                    @blk.vector
                    def _(v, it=it, ch=ch):
                        v = C.ser(v, "sDV")
                        C.wait(v._raw, toks[("xpl", it, ch)])
                        xpr = xprs[ch % 2]
                        if ch < 4:
                            v.tensor_scalar(out=xe[ch][:, 0:256],
                                            in0=xpr[:, 1:257],
                                            scalar1=flags[:, 1:2], scalar2=None,
                                            op0=ALU.mult)
                            toks[("xem", it, ch)] = v.tensor_copy(
                                xe[ch][:, 256:2304], C.x[ch][:])
                        else:
                            v.tensor_scalar(out=xe[ch][:, 0:256],
                                            in0=xpr[:, 0:256],
                                            scalar1=flags[:, 1:2], scalar2=None,
                                            op0=ALU.mult)
                            v.tensor_scalar(out=xe[ch][:, 256:257],
                                            in0=xpr[:, 256:257],
                                            scalar1=flags[:, 1:2], scalar2=None,
                                            op0=ALU.mult)
                            toks[("xem", it, ch)] = v.tensor_copy(
                                xe[ch][:, 257:2304], C.x[ch][:, 0:TL - 1])

                for ch in range(NCH):
                    @blk.gpsimd
                    def _(g, it=it, ch=ch):
                        if ch >= 2:
                            C.wait(g, toks[("qkmL", it, ch - 2)])
                            C.wait(g, toks[("vmL", it, ch - 2)])
                        elif it >= 1:
                            C.wait(g, toks[("qkmL", it - 1, ch + 6)])
                            C.wait(g, toks[("vmL", it - 1, ch + 6)])
                        for nm, slot in [("l3f_wq", wqs), ("l3f_wk", wks),
                                         ("l3f_wv", wvs)]:
                            ins = g.dma_start(out=slot[ch % 2][:],
                                              in_=C.dram[nm][it, ch])
                            C.inc16(ins, f"3GW{ch % 2}")
                        toks[("wl", it, ch)] = C.tok(f"3GW{ch % 2}")

                    @blk.tensor
                    def _(t, it=it, ch=ch):
                        C.wait(t, toks[("wl", it, ch)])
                        C.wait(t, toks[("xem", it, NCH - 1)])
                        if ch >= 1:
                            C.wait(t, toks[("wtt", it, ch - 1, NCH - 1)])
                        if ch == 0:
                            for tk in ifence:
                                C.wait(t, tk)
                            C.wait(t, toks["tabs"])
                            C.wait(t, toks["pbz"])
                        dv0 = C.sem("sDV")[1]
                        last = None
                        for t2 in range(18):
                            if t2 >= 2:
                                C.wait(t, ("sDV", dv0 + t2 - 1))
                            for c2 in range(NCH):
                                ins = t.matmul(
                                    PV[t2 % 2][:, 0:128],
                                    xe[c2][:, t2 * 128:(t2 + 1) * 128],
                                    wvs[ch % 2][:, c2 * 128:(c2 + 1) * 128],
                                    start=(c2 == 0), stop=(c2 == NCH - 1))
                            toks[("vm", it, ch, t2)] = C.inc(ins, "TE")
                        toks[("vmL", it, ch)] = toks[("vm", it, ch, 17)]

                    @blk.vector
                    def _(v, it=it, ch=ch):
                        v = C.ser(v, "sDV")
                        for t2 in range(18):
                            C.wait(v._raw, toks[("vm", it, ch, t2)])
                            v.tensor_copy(vch[:, t2 * 128:(t2 + 1) * 128],
                                          PV[t2 % 2][:, 0:128])
                        toks[("vev", it, ch)] = C.tok("sDV")

                    qkm_last = None
                    for side, dst, TTL in [("q", q3, QTT), ("k", k3, KTT)]:
                        wsl = wqs if side == "q" else wks
                        gsc, gpc = scols[side]
                        ctab = tabs["cq"] if side == "q" else tabs["ck"]
                        stab = tabs["sq"] if side == "q" else tabs["sk"]
                        for tj, (cb, cw) in enumerate(TTL):
                            xcb = cb + 256 if side == "q" else cb
                            tb = cb + 256 if side == "q" else cb
                            first = (side == "q" and tj == 0)
                            key = (it, ch, side, tj)
                            pkey = (it, ch, "q", tj - 1) if side == "k" and tj == 0 \
                                else (it, ch, side, tj - 1)
                            if side == "q" and tj == 0:
                                pkey = None
                            elif side == "k" and tj == 0:
                                pkey = (it, ch, "q", len(QTT) - 1)

                            @blk.tensor
                            def _(t, it=it, ch=ch, key=key, pkey=pkey, cb=cb,
                                  cw=cw, xcb=xcb, wsl=wsl, first=first):
                                if first:
                                    if (it, ch) != (0, 0):
                                        pc = (it, ch - 1) if ch else (it - 1, NCH - 1)
                                        C.wait(t, toks[("wrL",) + pc])
                                if pkey is not None:
                                    C.wait(t, toks[("ap",) + pkey])
                                for c2 in range(NCH):
                                    ins = t.matmul(
                                        PB[0][:, 0:cw],
                                        wsl[ch % 2][:, c2 * 128:(c2 + 1) * 128],
                                        xe[c2][:, xcb:xcb + cw],
                                        start=(c2 == 0), stop=(c2 == NCH - 1))
                                toks[("pm",) + key] = C.inc(ins, "TE")

                            @blk.scalar
                            def _(s, key=key, pkey=pkey, cw=cw):
                                s = C.ser(s, "sAC")
                                C.wait(s._raw, toks[("pm",) + key])
                                if pkey is not None:
                                    C.wait(s._raw, toks[("st",) + pkey])
                                s.activation(out=sq1[:, 0:cw], in_=PB[0][:, 0:cw],
                                             func=AF.Square)
                                toks[("sqv",) + key] = C.tok("sAC")
                                if pkey is not None:
                                    C.wait(s._raw, toks[("perm",) + pkey])
                                toks[("cpv",) + key] = s.activation(
                                    out=qb[:, 0:cw], in_=PB[0][:, 0:cw],
                                    func=AF.Identity)

                            @blk.tensor
                            def _(t, key=key, pkey=pkey, cw=cw):
                                C.wait(t, toks[("sqv",) + key])
                                t.matmul(PB[2][0:33, 0:cw], ones_b33[0:64, :],
                                         sq1[0:64, 0:cw], start=True, stop=True)
                                ins = t.matmul(PB[2][32:64, 0:cw],
                                               ones_b33[64:128, 0:32],
                                               sq1[64:128, 0:cw],
                                               start=True, stop=True)
                                toks[("st",) + key] = C.inc(ins, "TE")
                                C.wait(t, toks[("cpv",) + key])
                                ins = t.matmul(PB[1][:, 0:cw], permh[:],
                                               qb[:, 0:cw], start=True, stop=True)
                                toks[("perm",) + key] = C.inc(ins, "TE")

                            @blk.scalar
                            def _(s, key=key, cw=cw):
                                s = C.ser(s, "sAC")
                                C.wait(s._raw, toks[("st",) + key])
                                s.activation(out=r_all[:, 0:cw],
                                             in_=PB[2][0:33, 0:cw], func=AF.Ln,
                                             bias=C.constf(1e-24, rows=33))
                                toks[("rw",) + key] = s.activation(
                                    out=r_all[:, 0:cw], in_=r_all[:, 0:cw],
                                    func=AF.Exp, scale=-0.5)

                            @blk.tensor
                            def _(t, key=key, cw=cw):
                                C.wait(t, toks[("rw",) + key])
                                t.matmul(PBC[0:64, 0:cw], ones33[0:1, 0:64],
                                         r_all[0:1, 0:cw], start=True, stop=True)
                                ins = t.matmul(PBC[64:128, 0:cw],
                                               ones33[32:33, 0:64],
                                               r_all[32:33, 0:cw],
                                               start=True, stop=True)
                                toks[("bc",) + key] = C.inc(ins, "TE")

                            @blk.scalar
                            def _(s, key=key, pkey=pkey, cw=cw):
                                s = C.ser(s, "sAC")
                                C.wait(s._raw, toks[("bc",) + key])
                                if pkey is not None:
                                    C.wait(s._raw, toks[("ap",) + pkey])
                                toks[("rbe",) + key] = s.activation(
                                    out=rbc[:, 0:cw], in_=PBC[:, 0:cw],
                                    func=AF.Identity)

                            @blk.vector
                            def _(v, it=it, key=key, cb=cb, cw=cw, tb=tb,
                                  dst=dst, gsc=gsc, gpc=gpc, ctab=ctab,
                                  stab=stab):
                                v = C.ser(v, "sDV")
                                C.wait(v._raw, toks[("rbe",) + key])
                                v.scalar_tensor_tensor(
                                    out=t512[:, 0:cw], in0=PB[0][:, 0:cw],
                                    scalar=gsc[:, it:it + 1], op0=ALU.mult,
                                    in1=ctab[:, tb:tb + cw], op1=ALU.mult)
                                v.scalar_tensor_tensor(
                                    out=t512b[:, 0:cw], in0=PB[1][:, 0:cw],
                                    scalar=gpc[:, it:it + 1], op0=ALU.mult,
                                    in1=stab[:, tb:tb + cw], op1=ALU.mult)
                                v.tensor_tensor(out=t512[:, 0:cw],
                                                in0=t512[:, 0:cw],
                                                in1=t512b[:, 0:cw], op=ALU.add)
                                toks[("ap",) + key] = v.tensor_tensor(
                                    out=dst[:, cb:cb + cw], in0=t512[:, 0:cw],
                                    in1=rbc[:, 0:cw], op=ALU.mult)
                    toks[("qkmL", it, ch)] = toks[("pm", it, ch, "k", len(KTT) - 1)]

                    for wi in range(NCH):
                        @blk.tensor
                        def _(t, it=it, ch=ch, wi=wi):
                            if wi == 0:
                                C.wait(t, toks[("ap", it, ch, "k", len(KTT) - 1)])
                            ac0 = C.sem("sAC")[1]
                            for idx in range(8):
                                hh, kt = idx // 4, idx % 4
                                rb = 64 * hh
                                if idx >= 2:
                                    C.wait(t, ("sAC", ac0 + idx - 1))
                                kc = wi * 256 + kt * 128
                                t.matmul(PB[idx % 2][:, 0:256],
                                         k3[rb:rb + 64, kc:kc + 128],
                                         q3[rb:rb + 64, wi * 256:(wi + 1) * 256],
                                         start=True, stop=False)
                                ins = t.matmul(PB[idx % 2][:, 0:256],
                                               C.ident_bf[:],
                                               wmask[:, kt * 256:(kt + 1) * 256],
                                               start=False, stop=True)
                                toks[("ws", it, ch, wi, idx)] = C.inc(ins, "TE")

                        @blk.scalar
                        def _(s, it=it, ch=ch, wi=wi):
                            s = C.ser(s, "sAC")
                            for idx in range(8):
                                kt = idx % 4
                                C.wait(s._raw, toks[("ws", it, ch, wi, idx)])
                                bias = (pmask[:, 0:1] if (wi == 0 and kt < 2)
                                        else C.constf(0.0))
                                toks[("we", it, ch, wi, idx)] = s.activation(
                                    out=et[:, idx * 256:(idx + 1) * 256],
                                    in_=PB[idx % 2][:, 0:256], func=AF.Exp,
                                    scale=8.0, bias=bias)

                        @blk.tensor
                        def _(t, it=it, ch=ch, wi=wi):
                            if wi >= 1:
                                C.wait(t, toks[("wtt", it, ch, wi - 1)])
                            elif ch >= 1:
                                C.wait(t, toks[("wtt", it, ch - 1, NCH - 1)])
                            if wi == 0:
                                C.wait(t, toks[("vev", it, ch)])
                            for hh in range(2):
                                for kt in range(4):
                                    idx = hh * 4 + kt
                                    C.wait(t, toks[("we", it, ch, wi, idx)])
                                    ins = t.matmul(
                                        PB[2][hh * 32:hh * 32 + 33 - hh, 0:256],
                                        ones_b33[:, 0:33 - hh],
                                        et[:, idx * 256:(idx + 1) * 256],
                                        start=(kt == 0), stop=(kt == 3))
                            toks[("wlm", it, ch, wi)] = C.inc(ins, "TE")
                            for hh in range(2):
                                for kt in range(4):
                                    idx = hh * 4 + kt
                                    t2v = wi * 2 + kt
                                    ins = t.matmul(
                                        PV[wi % 2][hh * 64:hh * 64 + 64, 0:256],
                                        vch[:, t2v * 128 + hh * 64:
                                            t2v * 128 + hh * 64 + 64],
                                        et[:, idx * 256:(idx + 1) * 256],
                                        start=(kt == 0), stop=(kt == 3))
                            toks[("wav", it, ch, wi)] = C.inc(ins, "TE")

                        @blk.scalar
                        def _(s, it=it, ch=ch, wi=wi):
                            s = C.ser(s, "sAC")
                            C.wait(s._raw, toks[("wlm", it, ch, wi)])
                            s.activation(out=r_all[:, 0:256],
                                         in_=PB[2][0:33, 0:256], func=AF.Ln)
                            toks[("wr", it, ch, wi)] = s.activation(
                                out=r_all[:, 0:256], in_=r_all[:, 0:256],
                                func=AF.Exp, scale=-1.0)
                            if wi == NCH - 1:
                                toks[("wrL", it, ch)] = toks[("wr", it, ch, wi)]

                        @blk.tensor
                        def _(t, it=it, ch=ch, wi=wi):
                            C.wait(t, toks[("wr", it, ch, wi)])
                            t.matmul(PB[2][0:64, 256:512], ones33[0:1, 0:64],
                                     r_all[0:1, 0:256], start=True, stop=True)
                            ins = t.matmul(PB[2][64:128, 256:512],
                                           ones33[32:33, 0:64],
                                           r_all[32:33, 0:256],
                                           start=True, stop=True)
                            toks[("wbc", it, ch, wi)] = C.inc(ins, "TE")

                        @blk.scalar
                        def _(s, it=it, ch=ch, wi=wi):
                            s = C.ser(s, "sAC")
                            C.wait(s._raw, toks[("wbc", it, ch, wi)])
                            if wi >= 1:
                                C.wait(s._raw, toks[("wtt", it, ch, wi - 1)])
                            elif ch >= 1:
                                C.wait(s._raw, toks[("wtt", it, ch - 1, NCH - 1)])
                            toks[("wre", it, ch, wi)] = s.activation(
                                out=wrbc[:], in_=PB[2][:, 256:512],
                                func=AF.Identity)

                        @blk.vector
                        def _(v, it=it, ch=ch, wi=wi):
                            v = C.ser(v, "sDV")
                            C.wait(v._raw, toks[("wre", it, ch, wi)])
                            C.wait(v._raw, toks[("wav", it, ch, wi)])
                            toks[("wtt", it, ch, wi)] = v.tensor_tensor(
                                out=aog[ch % 4][:, wi * 256:(wi + 1) * 256],
                                in0=PV[wi % 2][:, 0:256],
                                in1=wrbc[:], op=ALU.mult)

                    if ch in (3, 7):
                        gg = ch // 4
                        for o in range(NCH + 1):
                            if o < NCH:
                                @blk.gpsimd
                                def _(g, it=it, gg=gg, o=o):
                                    if o >= 2:
                                        C.wait(g, toks[("wom", it, gg, o - 2)])
                                    elif gg == 1:
                                        C.wait(g, toks[("wom", it, 0, o + 6)])
                                    elif it >= 1:
                                        C.wait(g, toks[("wom", it - 1, 1, o + 6)])
                                    ins = g.dma_start(out=wos[o % 2][:],
                                                      in_=C.dram["l3f_wo"][it, o])
                                    toks[("wol", it, gg, o)] = C.inc16(
                                        ins, f"3GO{o % 2}")
                            if o >= 1:
                                oo = o - 1

                                @blk.tensor
                                def _(t, it=it, ch=ch, gg=gg, oo=oo):
                                    C.wait(t, toks[("wol", it, gg, oo)])
                                    if oo == 0:
                                        C.wait(t, toks[("wtt", it, ch, NCH - 1)])
                                    else:
                                        C.wait(t, toks[("woe", it, gg, oo - 1,
                                                        NT - 1)])
                                    dv0 = C.sem("sDV")[1]
                                    for tt in range(NT):
                                        if tt >= 2:
                                            C.wait(t, ("sDV", dv0 + tt - 1))
                                        for i2 in range(4):
                                            c2 = gg * 4 + i2
                                            ins = t.matmul(
                                                PB[tt % 2][:],
                                                wos[oo % 2][:, c2 * 128:
                                                            (c2 + 1) * 128],
                                                aog[i2][:, tt * 512:
                                                        (tt + 1) * 512],
                                                start=(i2 == 0), stop=(i2 == 3))
                                        toks[("womt", it, gg, oo, tt)] = C.inc(
                                            ins, "TE")
                                    toks[("wom", it, gg, oo)] = \
                                        toks[("womt", it, gg, oo, NT - 1)]

                                @blk.vector
                                def _(v, it=it, gg=gg, oo=oo):
                                    v = C.ser(v, "sDV")
                                    if (it, gg, oo) == (0, 0, 0):
                                        C.wait(v._raw, toks["x0w"])
                                    for tt in range(NT):
                                        C.wait(v._raw,
                                               toks[("womt", it, gg, oo, tt)])
                                        sl = slice(tt * 512, (tt + 1) * 512)
                                        toks[("woe", it, gg, oo, tt)] = \
                                            v.tensor_tensor(
                                                out=C.x[oo][:, sl],
                                                in0=C.x[oo][:, sl],
                                                in1=PB[tt % 2][:], op=ALU.add)
                                    if oo == NCH - 1:
                                        toks[("woeL", it, gg)] = \
                                            toks[("woe", it, gg, oo, NT - 1)]

            if L3F_ATTN_ONLY:
                ifence = [toks[("woeL", it, 1)]]
                continue

            @blk.vector
            def _(v, it=it):
                v = C.ser(v, "sDV")
                for ch in range(4, NCH):
                    tk = v.tensor_copy(bno[:, ch - 4:ch - 3],
                                       C.x[ch][:, TL - 1:TL])
                toks[("bnc", it)] = tk

            @blk.gpsimd
            def _(g, it=it):
                C.wait(g, toks[("woeL", it, 1)])
                C.wait(g, toks[("bnc", it)])
                if it >= 1:
                    g.wait_ge(cc_b[it - 1], 1)
                ins = g.dma_start(out=d_bn_i.ap(), in_=bno[:])
                C.inc16(ins, "3GB")
                toks[("bnw", it)] = C.tok("3GB")

            @blk.gpsimd
            def _(g, it=it):
                C.wait(g, toks[("bnw", it)])
                g.collective_compute("AllGather", ALU.bypass,
                                     replica_groups=PAIRS,
                                     ins=[d_bn_i.ap().opt()],
                                     outs=[d_bn_o[it].ap().opt()]).then_inc(cc_b[it])

            with C.scope():
                xn = [C.sb(f"l3f_xn{ch}", [128, TL], BF16) for ch in range(NCH)]
                with C.scope():
                    xs2 = [C.sb(f"l3f_xs2{ch}", [128, TL], BF16)
                           for ch in range(NCH)]
                    bnb = C.sb("l3f_bnb", [128, 4], F32)

                    @blk.gpsimd
                    def _(g, it=it):
                        g.wait_ge(cc_b[it], 1)
                        ins = g.dma_start(out=bnb[:],
                                          in_=d_bn_o[it].ap()[0:128, :])
                        toks[("bnl", it)] = C.inc16(ins, "3GB")

                    @blk.vector
                    def _(v, it=it):
                        v = C.ser(v, "sDV")
                        C.wait(v._raw, toks[("bnl", it)])
                        for ch in range(NCH):
                            if ch < 4:
                                tk = v.tensor_copy(xs2[ch][:], C.x[ch][:])
                            else:
                                v.tensor_scalar(out=xs2[ch][:, 0:1],
                                                in0=bnb[:, ch - 4:ch - 3],
                                                scalar1=flags[:, 1:2],
                                                scalar2=None, op0=ALU.mult)
                                tk = v.tensor_copy(xs2[ch][:, 1:TL],
                                                   C.x[ch][:, 0:TL - 1])
                        toks[("xs2L", it)] = tk

                    C.set_fence([toks[("xs2L", it)]])
                    lnt = emit_ln(C, f"l3fln{it}", xs2, xn, 1e-05,
                                  gamma=C.sbc["l3f_g"][:, it * 8:(it + 1) * 8],
                                  beta=C.sbc["l3f_b"][:, it * 8:(it + 1) * 8])
                    toks[("lnf", it)] = lnt[-1]

                harena = [C.sb(f"l3f_h{j}", [128, TL], BF16) for j in range(11)]
                gl = [C.sb(f"l3f_gl{i2}", [128, 512], BF16) for i2 in range(2)]
                w1as = [C.sb(f"l3f_w1a{i2}", [128, DM], BF16) for i2 in range(2)]
                w1gs = [C.sb(f"l3f_w1g{i2}", [128, DM], BF16) for i2 in range(2)]
                w2s = [C.sb(f"l3f_w2s{i2}", [128, 1408], BF16) for i2 in range(2)]
                W2B = [PB[0], PB[1], PB[2], PBC]

                for grp in range(2):
                    for ee in range(11):
                        e = grp * 11 + ee

                        @blk.gpsimd
                        def _(g, it=it, e=e):
                            if e >= 2:
                                C.wait(g, toks[("w1m", it, e - 2)])
                            else:
                                C.wait(g, toks[("lnf", it)])
                                if it >= 1:
                                    C.wait(g, toks[("w1m", it - 1, e + 20)])
                            for nm, slot in [("l3f_w1a", w1as), ("l3f_w1g", w1gs)]:
                                ins = g.dma_start(out=slot[e % 2][:],
                                                  in_=C.dram[nm][it, e])
                                C.inc16(ins, f"3G1{e % 2}")
                            toks[("w1l", it, e)] = C.tok(f"3G1{e % 2}")

                        @blk.tensor
                        def _(t, it=it, e=e, grp=grp):
                            C.wait(t, toks[("w1l", it, e)])
                            if e == 0:
                                C.wait(t, toks[("lnf", it)])
                            elif e == 11 and grp == 1:
                                C.wait(t, toks[("w2e", it, 0)])
                            if e >= 1:
                                C.wait(t, toks[("w1h", it, e - 1, NT - 1)])
                                C.wait(t, toks[("w1ge", it, e - 1, NT - 1)])
                            dv0 = C.sem("sDV")[1]
                            ac0 = C.sem("sAC")[1]
                            for tt in range(NT):
                                if tt >= 2:
                                    C.wait(t, ("sDV", dv0 + tt - 1))
                                for c2 in range(NCH):
                                    ins = t.matmul(
                                        PB[tt % 2][:],
                                        w1as[e % 2][:, c2 * 128:(c2 + 1) * 128],
                                        xn[c2][:, tt * 512:(tt + 1) * 512],
                                        start=(c2 == 0), stop=(c2 == NCH - 1))
                                toks[("w1am", it, e, tt)] = C.inc(ins, "TE")
                                if tt >= 2:
                                    C.wait(t, ("sAC", ac0 + tt - 1))
                                gbank = PB[2] if tt % 2 == 0 else PBC
                                for c2 in range(NCH):
                                    ins = t.matmul(
                                        gbank[:],
                                        w1gs[e % 2][:, c2 * 128:(c2 + 1) * 128],
                                        xn[c2][:, tt * 512:(tt + 1) * 512],
                                        start=(c2 == 0), stop=(c2 == NCH - 1))
                                toks[("w1gm", it, e, tt)] = C.inc(ins, "TE")
                            toks[("w1m", it, e)] = toks[("w1gm", it, e, NT - 1)]

                        @blk.scalar
                        def _(s, it=it, e=e):
                            s = C.ser(s, "sAC")
                            dv0 = C.sem("sDV")[1]
                            for tt in range(NT):
                                C.wait(s._raw, toks[("w1gm", it, e, tt)])
                                if tt >= 2:
                                    C.wait(s._raw, ("sDV", dv0 + tt - 1))
                                gbank = PB[2] if tt % 2 == 0 else PBC
                                toks[("w1ge", it, e, tt)] = s.activation(
                                    out=gl[tt % 2][:], in_=gbank[:],
                                    func=AF.Gelu,
                                    bias=b1g[:, it * 22 + e:it * 22 + e + 1])

                        @blk.vector
                        def _(v, it=it, e=e, ee=ee):
                            v = C.ser(v, "sDV")
                            for tt in range(NT):
                                C.wait(v._raw, toks[("w1am", it, e, tt)])
                                C.wait(v._raw, toks[("w1ge", it, e, tt)])
                                toks[("w1h", it, e, tt)] = v.scalar_tensor_tensor(
                                    out=harena[ee][:, tt * 512:(tt + 1) * 512],
                                    in0=PB[tt % 2][:],
                                    scalar=b1a[:, it * 22 + e:it * 22 + e + 1],
                                    op0=ALU.add, in1=gl[tt % 2][:], op1=ALU.mult)

                    for o in range(NCH + 1):
                        if o < NCH:
                            @blk.gpsimd
                            def _(g, it=it, grp=grp, o=o):
                                if o >= 2:
                                    C.wait(g, toks[("w2m", it, grp, o - 2)])
                                elif grp == 1:
                                    C.wait(g, toks[("w2m", it, 0, o + 6)])
                                else:
                                    C.wait(g, toks[("lnf", it)])
                                    if it >= 1:
                                        C.wait(g, toks[("w2m", it - 1, 1, o + 6)])
                                ins = g.dma_start(
                                    out=w2s[o % 2][:],
                                    in_=C.dram["l3f_w2"][it, o, grp])
                                toks[("w2l", it, grp, o)] = C.inc16(
                                    ins, f"3G2{o % 2}")
                        if o >= 1:
                            oo = o - 1

                            @blk.tensor
                            def _(t, it=it, grp=grp, oo=oo):
                                C.wait(t, toks[("w2l", it, grp, oo)])
                                if oo == 0:
                                    C.wait(t, toks[("w1h", it, grp * 11 + 10,
                                                    NT - 1)])
                                    C.wait(t, toks[("w1ge", it, grp * 11 + 10,
                                                    NT - 1)])
                                else:
                                    C.wait(t, toks[("w2ev", it, grp, oo - 1,
                                                    NT - 1)])
                                for tt in range(NT):
                                    for j in range(11):
                                        ins = t.matmul(
                                            W2B[tt][:],
                                            w2s[oo % 2][:, j * 128:(j + 1) * 128],
                                            harena[j][:, tt * 512:(tt + 1) * 512],
                                            start=(j == 0), stop=(j == 10))
                                    toks[("w2mt", it, grp, oo, tt)] = C.inc(
                                        ins, "TE")
                                toks[("w2m", it, grp, oo)] = \
                                    toks[("w2mt", it, grp, oo, NT - 1)]

                            @blk.vector
                            def _(v, it=it, grp=grp, oo=oo):
                                v = C.ser(v, "sDV")
                                for tt in range(NT):
                                    C.wait(v._raw, toks[("w2mt", it, grp, oo, tt)])
                                    sl = slice(tt * 512, (tt + 1) * 512)
                                    if grp == 0:
                                        toks[("w2ev", it, grp, oo, tt)] = \
                                            v.tensor_tensor(
                                                out=C.x[oo][:, sl],
                                                in0=C.x[oo][:, sl],
                                                in1=W2B[tt][:], op=ALU.add)
                                    else:
                                        toks[("w2ev", it, grp, oo, tt)] = \
                                            v.scalar_tensor_tensor(
                                                out=C.x[oo][:, sl],
                                                in0=W2B[tt][:],
                                                scalar=b2c[:, it * 8 + oo:
                                                           it * 8 + oo + 1],
                                                op0=ALU.add,
                                                in1=C.x[oo][:, sl], op1=ALU.add)
                                if oo == NCH - 1:
                                    toks[("w2e", it, grp)] = \
                                        toks[("w2ev", it, grp, oo, NT - 1)]

            ifence = [toks[("w2e", it, 1)]]

        with C.scope():
            x0s = [C.sb(f"l3f_x0s{i2}", [128, TL], F32) for i2 in range(2)]
            for ch in range(NCH):
                @blk.gpsimd
                def _(g, ch=ch):
                    if ch >= 2:
                        C.wait(g, toks[("x0a", ch - 2)])
                    else:
                        for tk in ifence:
                            C.wait(g, tk)
                    ins = g.dma_start(out=x0s[ch % 2][:],
                                      in_=d_x0.ap()[ch * 128:(ch + 1) * 128, :])
                    toks[("x0l", ch)] = C.inc16(ins, f"3GS{ch % 2}")

                @blk.vector
                def _(v, ch=ch):
                    v = C.ser(v, "sDV")
                    C.wait(v._raw, toks[("x0l", ch)])
                    toks[("x0a", ch)] = v.tensor_tensor(
                        out=C.x[ch][:], in0=C.x[ch][:], in1=x0s[ch % 2][:],
                        op=ALU.add)

        C.set_fence([toks[("x0a", NCH - 1)]])


def emit_postln(C, tag):
    toks = emit_ln(C, tag, C.x, C.x, 1e-5)
    C.set_fence(toks)


def _shard_x(x):
    out = []
    for c in range(NCORES):
        b, h = c // 2, c % 2
        out.append(np.ascontiguousarray(x[b, h * TL:(h + 1) * TL, :].T))
    return out


def _unshard_x(xts):
    x = np.empty((B, T, DM), np.float32)
    for c in range(NCORES):
        b, h = c // 2, c % 2
        x[b, h * TL:(h + 1) * TL, :] = xts[c].T
    return x


def _host_consts():
    return {
        "c_ones": np.ones((1, 128), np.float32),
        "c_ones_col": np.ones((128, 1), np.float32),
        "c_ident": np.eye(128, dtype=np.float32),
        "c_misc": np.tile(np.asarray(CONST_VALUES, np.float32), (128, 1)),
        "c_trow": np.arange(TL, dtype=np.float32).reshape(1, TL),
    }


def _layer_weight_arrays(inputs, layers):
    w = {}

    def blocked(mat):
        return np.ascontiguousarray(
            mat.reshape(NCH, 128, NCH, 128).transpose(2, 1, 0, 3).reshape(NCH, 128, DM))

    def cols(vec):
        return np.ascontiguousarray(np.asarray(vec, np.float32).reshape(NCH, 128).T)

    if "l1_ffn" in layers:
        for g, wkey, ukey in [("f", "L1_ffn_wf", "L1_ffn_uf"),
                              ("i", "L1_ffn_wi", "L1_ffn_ui"),
                              ("c", "L1_ffn_wc", "L1_ffn_uc"),
                              ("o", "L1_ffn_wo", "L1_ffn_uo")]:
            w[f"l1f_w{g}"] = blocked(np.asarray(inputs[wkey], np.float32).T)
            if g == "o":
                w["l1f_uo"] = blocked(np.asarray(inputs[ukey], np.float32).T)
            else:
                w[f"l1f_u{g}"] = np.ascontiguousarray(np.asarray(inputs[ukey], np.float32).T)
        w["l1f_bias"] = np.concatenate(
            [cols(inputs[k]) for k in ["L1_ffn_bf", "L1_ffn_bi", "L1_ffn_bc", "L1_ffn_bo"]], 1)
        w["l1f_inc"] = cols(inputs["L1_ffn_in_c"])
    if "l1_attn" in layers:
        wqkvff = np.asarray(inputs["L1_attn_w_qkvff"], np.float32)
        Cd = DM
        perm = np.arange(Cd).reshape(-1, 2)[:, ::-1].reshape(-1)
        Wq = wqkvff[0:Cd]
        Wk = wqkvff[Cd:2 * Cd]
        Wv = wqkvff[2 * Cd:3 * Cd]
        Wffin = wqkvff[3 * Cd:5 * Cd]
        w["l1a_wq"] = blocked(Wq.T)
        w["l1a_wqp"] = blocked(Wq[perm].T)
        w["l1a_wk"] = blocked(Wk.T)
        w["l1a_wkp"] = blocked(Wk[perm].T)
        w["l1a_wv"] = np.ascontiguousarray(Wv.T)
        w["l1a_wg"] = blocked(np.asarray(inputs["L1_attn_w_gated"], np.float32).T)
        w["l1a_wproj"] = blocked(np.asarray(inputs["L1_attn_w_proj"], np.float32).T)
        wfeT = np.ascontiguousarray(Wffin.T)
        w["l1a_wfe"] = np.ascontiguousarray(
            wfeT.reshape(NCH, 128, 16, 128).transpose(2, 1, 0, 3).reshape(16, 128, DM))
        wffT = np.asarray(inputs["L1_attn_w_ff"], np.float32).T
        w["l1a_wff"] = np.ascontiguousarray(
            wffT.reshape(NCH, 2, 128, NCH, 128).transpose(0, 3, 2, 1, 4)
            .reshape(NCH, NCH, 128, 256))
        kk_ = np.arange(128)[:, None]
        qq_ = np.arange(512)[None, :]
        tm = np.zeros((128, 4, 512), np.float32)
        for vv in range(4):
            tm[:, vv, :] = (qq_ >= kk_ + 128 * vv).astype(np.float32)
        w["l1a_tri01"] = tm.reshape(128, 2048)
        w["l1a_rmsw"] = cols(inputs["L1_attn_rms_w"])
        w["l1a_gnw"] = cols(inputs["L1_attn_gn_w"])
        w["l1a_gnb"] = cols(inputs["L1_attn_gn_b"])
        w["l1a_bgat"] = cols(inputs["L1_attn_b_gated"])
        w["l1a_bproj"] = cols(inputs["L1_attn_b_proj"])
    if "l2_attn" in layers:
        wqkv = np.asarray(inputs["L2_attn_w_qkv"], np.float32)
        w["l2_wq"] = blocked(wqkv[0:DM].T)
        w["l2_wk"] = blocked(wqkv[DM:2 * DM].T)
        w["l2_wo"] = blocked(np.asarray(inputs["L2_attn_w_out"], np.float32).T)
        wv = wqkv[2 * DM:3 * DM].T
        wva = np.zeros((DM, 1040), np.float32)
        bva = np.zeros((1, 1040), np.float32)
        bqkv = np.asarray(inputs["L2_attn_b_qkv"], np.float32)
        for hh in range(16):
            wva[:, 65 * hh:65 * hh + 64] = wv[:, 64 * hh:64 * hh + 64]
            bva[0, 65 * hh:65 * hh + 64] = bqkv[2 * DM + 64 * hh:2 * DM + 64 * hh + 64]
            bva[0, 65 * hh + 64] = 1.0
        w["l2_wva"] = wva
        w["l2_bvad"] = bva
        w["l2_wentd"] = cols(np.asarray(inputs["L2_attn_w_ent"], np.float32)[0])
        tm = np.zeros((128, 4, 512), np.float32)
        kk = np.arange(128)[:, None]
        qq = np.arange(512)[None, :]
        for vv in range(4):
            tm[:, vv, :] = np.where(qq >= kk + 128 * vv, 0.0, -1e30)
        w["l2_trimask"] = tm.reshape(128, 2048)
        w["l2_g"] = cols(inputs["L2_attn_ln_g"])
        w["l2_b"] = cols(inputs["L2_attn_ln_b"])
        w["l2_bq"] = cols(bqkv[0:DM])
        w["l2_bk"] = cols(bqkv[DM:2 * DM])
        w["l2_bent"] = np.full((128, 1), float(np.asarray(inputs["L2_attn_b_ent"]).ravel()[0]), np.float32)
        w["l2_bo01"] = cols(np.asarray(inputs["L2_attn_b_out"], np.float32) * 0.1)
    if "l2_ffn" in layers:
        wfcr = np.asarray(inputs["L2_ffn_Wfc_r"], np.float32)
        wfcs = np.asarray(inputs["L2_ffn_Wfc_sh"], np.float32)
        wpr = np.asarray(inputs["L2_ffn_Wproj_r"], np.float32)
        wps = np.asarray(inputs["L2_ffn_Wproj_sh"], np.float32)
        F_ = 4096
        wfc_all = np.concatenate([wfcr, wfcs], 0)
        wt = wfc_all.transpose(0, 2, 1)
        w["l2f_wfc"] = np.ascontiguousarray(
            wt.reshape(10, NCH, 128, 32, 128).transpose(0, 3, 2, 1, 4)
            .reshape(10, 32, 128, DM))
        wp_all = np.concatenate([wpr, wps], 0).transpose(0, 2, 1)
        w["l2f_wpj"] = np.ascontiguousarray(
            wp_all.reshape(10, 4, 8, 128, NCH, 128)
            .transpose(0, 4, 1, 3, 2, 5).reshape(10, NCH, 4, 128, DM))
        oh = np.zeros((8, 8, 128), np.float32)
        for e in range(8):
            oh[e, e, :] = 1.0
        w["l2f_onehot"] = np.ascontiguousarray(oh.transpose(1, 0, 2).reshape(8, 8 * 128))
        centT = np.asarray(inputs["L2_ffn_centroids"], np.float32).T
        w["l2f_cent"] = np.ascontiguousarray(
            centT.reshape(NCH, 128, 8).transpose(1, 0, 2).reshape(128, 64))
        w["l2f_rbias"] = np.tile(np.asarray(inputs["L2_ffn_routing_bias"],
                                            np.float32).reshape(1, 8), (128, 1))
    if "l3_attn" in layers:
        for nm, key in [("wk", "L3_attn_Wk"), ("wv", "L3_attn_Wv"),
                        ("wr", "L3_attn_Wr"), ("wo", "L3_attn_Wo")]:
            w[f"l3a_{nm}"] = blocked(np.asarray(inputs[key], np.float32).T)
        w["l3a_wrowd"] = np.asarray(inputs["L3_attn_spatial_decay"],
                                    np.float32).reshape(1, DM)
        w["l3a_mix"] = np.concatenate(
            [cols(inputs[k]) for k in ["L3_attn_mix_k", "L3_attn_mix_v",
                                       "L3_attn_mix_r"]], 1)
        w["l3a_wcols"] = cols(inputs["L3_attn_spatial_decay"])
        w["l3a_ucols"] = cols(inputs["L3_attn_spatial_first"])
    if "l0_attn" in layers:
        Wq0 = np.asarray(inputs["L0_attn_Wq"], np.float32)
        Wd0 = np.asarray(inputs["L0_attn_Wdown"], np.float32)
        Wu0 = np.asarray(inputs["L0_attn_Wup"], np.float32)
        Wo0 = np.asarray(inputs["L0_attn_Wout"], np.float32)
        bu0 = np.asarray(inputs["L0_attn_bup"], np.float32)
        w["l0a_wq"] = blocked(Wq0)
        w["l0a_wo"] = blocked(Wo0)
        w["l0a_wdn"] = np.ascontiguousarray(
            Wd0.reshape(NCH, 128, 2, 128).transpose(2, 1, 0, 3)
            .reshape(2, 128, DM))
        ext = np.zeros((256, 2048), np.float32)
        bk_ = np.zeros((128, 8), np.float32)
        bv_ = np.zeros((128, 8), np.float32)
        for j in range(8):
            ext[:, j * 128:j * 128 + 64] = Wu0[:, (2 * j) * 128:
                                                (2 * j) * 128 + 64]
            ext[:, j * 128 + 64:(j + 1) * 128] = Wu0[:, (2 * j + 1) * 128:
                                                     (2 * j + 1) * 128 + 64]
            bk_[0:64, j] = bu0[(2 * j) * 128:(2 * j) * 128 + 64]
            bk_[64:128, j] = bu0[(2 * j + 1) * 128:(2 * j + 1) * 128 + 64]
        for gp in range(8):
            ext[:, 1024 + gp * 128:1024 + gp * 128 + 64] = \
                Wu0[:, (2 * gp) * 128 + 64:(2 * gp + 1) * 128]
            ext[:, 1024 + gp * 128 + 64:1024 + (gp + 1) * 128] = \
                Wu0[:, (2 * gp + 1) * 128 + 64:(2 * gp + 2) * 128]
            bv_[0:64, gp] = bu0[(2 * gp) * 128 + 64:(2 * gp + 1) * 128]
            bv_[64:128, gp] = bu0[(2 * gp + 1) * 128 + 64:(2 * gp + 2) * 128]
        w["l0a_wup"] = np.ascontiguousarray(
            ext.reshape(2, 128, 16, 128).transpose(2, 1, 0, 3)
            .reshape(16, 128, 256))
        w["l0a_bk"] = bk_
        w["l0a_bv"] = bv_
        w["l0a_bq"] = cols(inputs["L0_attn_bq"])
        w["l0a_bo"] = cols(inputs["L0_attn_bout"])
        w["l0a_bdn"] = np.ascontiguousarray(
            np.asarray(inputs["L0_attn_bdown"], np.float32).reshape(2, 128).T)
        P0 = np.zeros((128, 128), np.float32)
        m0 = np.arange(128)
        P0[(m0 // 64) * 64 + (m0 % 64 + 32) % 64, m0] = 1.0
        w["l0a_permh"] = P0
        Lg_ = np.zeros((128, 512), np.float32)
        for j in range(8):
            for hp in range(2):
                v_ = 2 * j + hp
                Lg_[0:64, v_ * 32 + 4 * j + hp] = 1.0
                Lg_[64:128, v_ * 32 + 4 * j + 2 + hp] = 1.0
        w["l0a_Lg"] = Lg_
        Ls_ = np.zeros((32, 2), np.float32)
        Ls_[np.arange(32), np.arange(32) % 2] = 1.0
        w["l0a_Lsum"] = Ls_
        Lb_ = np.zeros((2, 32), np.float32)
        Lb_[np.arange(32) % 2, np.arange(32)] = 1.0
        w["l0a_Lbc"] = Lb_
        w["l0a_idstk"] = np.concatenate([np.eye(64), np.eye(64)],
                                        0).astype(np.float32)
        sel_ = np.zeros((32, 16, 128), np.float32)
        for gp in range(8):
            for hp in range(2):
                v_ = gp * 2 + hp
                sel_[4 * gp + hp, v_, 0:64] = 1.0
                sel_[4 * gp + 2 + hp, v_, 64:128] = 1.0
        w["l0a_sel2"] = np.ascontiguousarray(sel_.reshape(32, 2048))
    if "l3_ffn" in layers:
        NIT = 6
        HALF, PADH = 2730, 2816
        Wqkv = np.asarray(inputs["L3_ffn_Wqkv"], np.float32)
        w["l3f_wq"] = np.stack([blocked(Wqkv[i][:, 0:DM]) for i in range(NIT)])
        w["l3f_wk"] = np.stack([blocked(Wqkv[i][:, DM:2 * DM]) for i in range(NIT)])
        wv = Wqkv[:, :, 2 * DM:3 * DM]
        w["l3f_wv"] = np.ascontiguousarray(
            wv.reshape(NIT, NCH, 128, NCH, 128).transpose(0, 3, 2, 1, 4)
            .reshape(NIT, NCH, 128, DM))
        w["l3f_wo"] = np.stack(
            [blocked(np.asarray(inputs["L3_ffn_Wo"], np.float32)[i])
             for i in range(NIT)])
        W1 = np.asarray(inputs["L3_ffn_W1"], np.float32)
        w1a = np.zeros((NIT, DM, PADH), np.float32)
        w1g = np.zeros((NIT, DM, PADH), np.float32)
        w1a[:, :, :HALF] = W1[:, :, :HALF]
        w1g[:, :, :HALF] = W1[:, :, HALF:]

        def blk_w1(m):
            return np.ascontiguousarray(
                m.reshape(NIT, NCH, 128, 22, 128).transpose(0, 3, 2, 1, 4)
                .reshape(NIT, 22, 128, DM))
        w["l3f_w1a"] = blk_w1(w1a)
        w["l3f_w1g"] = blk_w1(w1g)
        W2 = np.asarray(inputs["L3_ffn_W2"], np.float32)
        w2p = np.zeros((NIT, PADH, DM), np.float32)
        w2p[:, :HALF] = W2
        w["l3f_w2"] = np.ascontiguousarray(
            w2p.reshape(NIT, 22, 128, NCH, 128).transpose(0, 3, 2, 1, 4)
            .reshape(NIT, NCH, 128, 2, 1408).transpose(0, 1, 3, 2, 4))
        b1 = np.asarray(inputs["L3_ffn_b1"], np.float32)
        b1a = np.zeros((NIT, PADH), np.float32)
        b1g = np.zeros((NIT, PADH), np.float32)
        b1a[:, :HALF] = b1[:, :HALF]
        b1g[:, :HALF] = b1[:, HALF:]
        w["l3f_b1a"] = np.ascontiguousarray(b1a.reshape(NIT * 22, 128).T)
        w["l3f_b1g"] = np.ascontiguousarray(b1g.reshape(NIT * 22, 128).T)
        w["l3f_b2"] = np.concatenate(
            [cols(np.asarray(inputs["L3_ffn_b2"], np.float32)[i])
             for i in range(NIT)], 1)
        w["l3f_g"] = np.concatenate(
            [cols(np.asarray(inputs["L3_ffn_ln_g"], np.float32)[i])
             for i in range(NIT)], 1)
        w["l3f_b"] = np.concatenate(
            [cols(np.asarray(inputs["L3_ffn_ln_b"], np.float32)[i])
             for i in range(NIT)], 1)
        qs = np.asarray(inputs["L3_ffn_q_scale"], np.float32)
        ks = np.asarray(inputs["L3_ffn_k_scale"], np.float32)
        perm64 = (np.arange(64) + 32) % 64

        def scol(m):
            return np.ascontiguousarray(np.tile(m, (1, 2)).T)
        w["l3f_qs"] = scol(qs)
        w["l3f_qsp"] = scol(qs[:, perm64])
        w["l3f_ks"] = scol(ks)
        w["l3f_ksp"] = scol(ks[:, perm64])
        kk3 = np.arange(128)[:, None]
        jq3 = np.arange(256)[None, :]
        wm = np.zeros((128, 4, 256), np.float32)
        for kt in range(4):
            jk3 = kt * 128 + kk3
            wm[:, kt, :] = np.where((jq3 <= jk3) & (jk3 <= jq3 + 256), 0.0, -1e30)
        w["l3f_wmask"] = np.ascontiguousarray(wm.reshape(128, 1024))
        P = np.zeros((128, 128), np.float32)
        m_ = np.arange(128)
        P[(m_ // 64) * 64 + (m_ % 64 + 32) % 64, m_] = 1.0
        w["l3f_permh"] = P
    if "l0_ffn" in layers:
        w["l0f_w"] = np.ascontiguousarray(inputs["L0_ffn_weight"], np.float32)
        w["l0f_g"] = np.ascontiguousarray(
            np.asarray(inputs["L0_ffn_ln_gamma"], np.float32).reshape(NCH, 128).T)
        w["l0f_b"] = np.ascontiguousarray(
            np.asarray(inputs["L0_ffn_ln_beta"], np.float32).reshape(NCH, 128).T)
    return w


_WEIGHT_DECLS = {
    "l0_ffn": {"l0f_w": ([DM, DM], False),
               "l0f_g": ([128, NCH], True),
               "l0f_b": ([128, NCH], True)},
    "l1_ffn": {"l1f_wf": ([NCH, 128, DM], False), "l1f_wi": ([NCH, 128, DM], False),
               "l1f_wc": ([NCH, 128, DM], False), "l1f_wo": ([NCH, 128, DM], False),
               "l1f_uf": ([DM, DM], False), "l1f_ui": ([DM, DM], False),
               "l1f_uc": ([DM, DM], False), "l1f_uo": ([NCH, 128, DM], False),
               "l1f_bias": ([128, 4 * NCH], True),
               "l1f_inc": ([128, NCH], True)},
    "l1_attn": {"l1a_wq": ([NCH, 128, DM], False), "l1a_wqp": ([NCH, 128, DM], False),
                "l1a_wk": ([NCH, 128, DM], False), "l1a_wkp": ([NCH, 128, DM], False),
                "l1a_wv": ([DM, DM], False),
                "l1a_wg": ([NCH, 128, DM], False),
                "l1a_wproj": ([NCH, 128, DM], False),
                "l1a_wfe": ([16, 128, DM], False),
                "l1a_wff": ([NCH, NCH, 128, 256], False),
                "l1a_tri01": ([128, 2048], False),
                "l1a_cosq": ([DM, TL], False), "l1a_sinq": ([DM, TL], False),
                "l1a_cosk": ([DM, TL], False), "l1a_sink": ([DM, TL], False),
                "l1a_rmsw": ([128, NCH], True),
                "l1a_gnw": ([128, NCH], True), "l1a_gnb": ([128, NCH], True),
                "l1a_bgat": ([128, NCH], True), "l1a_bproj": ([128, NCH], True)},
    "l2_attn": {"l2_wq": ([NCH, 128, DM], False), "l2_wk": ([NCH, 128, DM], False),
                "l2_wo": ([NCH, 128, DM], False),
                "l2_wva": ([DM, 1040], False),
                "l2_wentd": ([128, NCH], False),
                "l2_trimask": ([128, 2048], False),
                "l2_bvad": ([1, 1040], False),
                "l2_g": ([128, NCH], True), "l2_b": ([128, NCH], True),
                "l2_bq": ([128, NCH], True), "l2_bk": ([128, NCH], True),
                "l2_bent": ([128, 1], True), "l2_bo01": ([128, NCH], True),
                "l2_qidx": ([128, 16], True, mybir.dt.uint32),
                "l2_kidx": ([128, 48], True, mybir.dt.uint32),
                "l2_aidx": ([128, 16], True, mybir.dt.uint32),
                "l2_jflag": ([128, 192], True)},
    "l2_ffn": {"l2f_wfc": ([10, 32, 128, DM], False),
               "l2f_wpj": ([10, NCH, 4, 128, DM], False),
               "l2f_onehot": ([8, 8 * 128], False),
               "l2f_cent": ([128, 64], True),
               "l2f_rbias": ([128, 8], True)},
    "l3_attn": {"l3a_wk": ([NCH, 128, DM], False), "l3a_wv": ([NCH, 128, DM], False),
                "l3a_wr": ([NCH, 128, DM], False), "l3a_wo": ([NCH, 128, DM], False),
                "l3a_wrowd": ([1, DM], False),
                "l3a_mix": ([128, 24], True),
                "l3a_wcols": ([128, NCH], True),
                "l3a_ucols": ([128, NCH], True)},
    "l0_attn": {"l0a_wq": ([NCH, 128, DM], False),
                "l0a_wdn": ([2, 128, DM], False),
                "l0a_wup": ([16, 128, 256], False),
                "l0a_wo": ([NCH, 128, DM], False),
                "l0a_c": ([128, TL], False), "l0a_s": ([128, TL], False),
                "l0a_permh": ([128, 128], False),
                "l0a_Lg": ([128, 512], False),
                "l0a_Lsum": ([32, 2], False),
                "l0a_Lbc": ([2, 32], False),
                "l0a_idstk": ([128, 64], False),
                "l0a_sel2": ([32, 2048], False),
                "l0a_bq": ([128, NCH], True),
                "l0a_bdn": ([128, 2], True),
                "l0a_bk": ([128, 8], True),
                "l0a_bv": ([128, NCH], True),
                "l0a_bo": ([128, NCH], True)},
    "l3_ffn": {"l3f_wq": ([6, NCH, 128, DM], False),
               "l3f_wk": ([6, NCH, 128, DM], False),
               "l3f_wv": ([6, NCH, 128, DM], False),
               "l3f_wo": ([6, NCH, 128, DM], False),
               "l3f_w1a": ([6, 22, 128, DM], False),
               "l3f_w1g": ([6, 22, 128, DM], False),
               "l3f_w2": ([6, NCH, 2, 128, 1408], False),
               "l3f_wmask": ([128, 1024], False),
               "l3f_permh": ([128, 128], False),
               "l3f_cq": ([128, 2304], False), "l3f_sq": ([128, 2304], False),
               "l3f_ck": ([128, 2304], False), "l3f_sk": ([128, 2304], False),
               "l3f_qs": ([128, 6], True), "l3f_qsp": ([128, 6], True),
               "l3f_ks": ([128, 6], True), "l3f_ksp": ([128, 6], True),
               "l3f_g": ([128, 48], True), "l3f_b": ([128, 48], True),
               "l3f_b2": ([128, 48], True),
               "l3f_b1a": ([128, 132], True), "l3f_b1g": ([128, 132], True),
               "l3f_pmask": ([128, 1], True)},
}

_EMITTERS = {
    "l0_attn": emit_l0_attn,
    "l0_ffn": emit_l0_ffn,
    "l1_ffn": emit_l1_ffn,
    "l1_attn": emit_l1_attn,
    "l2_attn": emit_l2_attn,
    "l2_ffn": emit_l2_ffn,
    "l3_attn": emit_l3_attn,
    "l3_ffn": emit_l3_ffn,
}


def _percore_arrays(layers, c):
    h = c % 2
    out = {"c_flags": np.stack([np.full(128, 1.0 - h, np.float32),
                                np.full(128, float(h), np.float32)], 1)}
    if "l2_attn" in layers:
        qmegas = [[0, 3], [1, 2]][h]
        lists = []
        for a, L in [(0, 2), (1, 4)]:
            m = qmegas[a]
            ent = [m] + list(range(0, m))
            ent = (ent + [None] * L)[:L]
            lists.append(ent)
        entries = lists[0] + lists[1]
        p = np.arange(128, dtype=np.uint32)
        kidx = np.zeros((128, 48), np.uint32)
        for e, m in enumerate(entries):
            mm = 0 if m is None else m
            for j in range(NCH):
                kidx[:, e * NCH + j] = mm * 1024 + j * 128 + p
        qidx = np.zeros((128, 16), np.uint32)
        for a in range(2):
            for ch in range(NCH):
                qidx[:, a * NCH + ch] = qmegas[a] * 1024 + ch * 128 + p
        blkof = {0: 0, 3: 1, 1: 2, 2: 3}
        aidx = np.zeros((128, 16), np.uint32)
        for m in range(2):
            g = 2 * h + m
            for ch in range(NCH):
                aidx[:, m * NCH + ch] = blkof[g] * 1024 + ch * 128 + p
        jf = np.zeros((4, 6, 8), np.float32)
        for a, L in [(0, 2), (1, 4)]:
            for qo in range(2):
                qh = a * 2 + qo
                for ei in range(L):
                    m = lists[a][ei]
                    if m is None:
                        jf[qh, ei, :] = -1e30
                    elif ei == 0 and qo == 0:
                        jf[qh, ei, 4:8] = -1e30
        out.update({"l2_qidx": qidx, "l2_kidx": kidx, "l2_aidx": aidx,
                    "l2_jflag": np.tile(jf.reshape(1, 192), (128, 1))})
    if "l0_attn" in layers:
        hd0 = np.arange(0, 64, 2, dtype=np.float64)
        invf0 = 1.0 / 10000.0 ** (hd0 / 64.0)
        t0 = h * 2048.0 + np.arange(TL, dtype=np.float64)
        fr0 = t0[:, None] * invf0[None, :]
        c0 = np.concatenate([np.cos(fr0), np.cos(fr0)], 1)
        s0 = np.concatenate([np.sin(fr0), np.sin(fr0)], 1)
        sg0 = np.concatenate([-np.ones(32), np.ones(32)])
        out["l0a_c"] = np.ascontiguousarray(
            np.tile(c0.T, (2, 1)).astype(np.float32))
        out["l0a_s"] = np.ascontiguousarray(
            np.tile((s0 * sg0[None, :]).T, (2, 1)).astype(np.float32))
    if "l3_ffn" in layers:
        hd = np.arange(0, 64, 2, dtype=np.float64)
        inv_freq = 1.0 / 10000.0 ** (hd / 64.0)
        base_scale = (hd + 0.4 * 64.0) / (1.4 * 64.0)
        jj = np.arange(2304, dtype=np.float64)
        tt_ = h * 2048.0 + jj - 256.0
        fr = tt_[:, None] * inv_freq[None, :]
        power = (tt_ - 2048.0) / 128.0
        sc_ = base_scale[None, :] ** power[:, None]
        cos2 = np.concatenate([np.cos(fr), np.cos(fr)], 1)
        sin2 = np.concatenate([np.sin(fr), np.sin(fr)], 1)
        sc2 = np.concatenate([sc_, sc_], 1)
        sgn3 = np.concatenate([-np.ones(32), np.ones(32)])
        for nm, a in [("cq", cos2 * sc2), ("sq", sin2 * sc2 * sgn3[None, :]),
                      ("ck", cos2 / sc2), ("sk", sin2 / sc2 * sgn3[None, :])]:
            out[f"l3f_{nm}"] = np.ascontiguousarray(
                np.tile(a.T, (2, 1)).astype(np.float32))
        out["l3f_pmask"] = np.full((128, 1), 0.0 if h == 1 else -1e30,
                                   np.float32)
    if "l1_attn" in layers:
        Cd, Td, SB = DM, T, 512.0
        half = Cd // 2
        j = np.arange(half, dtype=np.float64)
        scale_vec = (2.0 * j + 0.4 * Cd) / (1.4 * Cd)
        power = (np.arange(Td, dtype=np.float64) - Td // 2) / SB
        scale = scale_vec[None, :] ** power[:, None]
        inv_freq = 1.0 / 10000.0 ** (j / half)
        sinu = np.arange(Td, dtype=np.float64)[:, None] * inv_freq[None, :]
        sn, cs = np.sin(sinu), np.cos(sinu)
        GAM = np.array([1.0 - 2.0 ** (-5.0 - hh) for hh in range(16)])
        iloc = np.arange(TL, dtype=np.float64)
        rows = slice(h * TL, (h + 1) * TL)

        def expand(a):
            return np.repeat(a, 2, axis=1)

        sgn = np.tile(np.array([-1.0, 1.0]), half)
        gam_c = np.repeat(GAM, 64)
        for nm, dscale in [("q", False), ("k", True)]:
            sc = (1.0 / scale) if dscale else scale
            cost = expand(cs[rows] * sc[rows])
            sint = expand(sn[rows] * sc[rows]) * sgn[None, :]
            gpow = gam_c[None, :] ** (iloc[:, None] * (1.0 if not dscale else -1.0))
            cost = cost * gpow
            sint = sint * gpow
            if nm == "q":
                cost = cost / 8.0
                sint = sint / 8.0
            out[f"l1a_cos{nm}"] = np.ascontiguousarray(cost.T.astype(np.float32))
            out[f"l1a_sin{nm}"] = np.ascontiguousarray(sint.T.astype(np.float32))
    return out


def build_program(layers):
    nc = bass.Bass("TRN2", target_bir_lowering=False)
    xin = nc.dram_tensor("xT", [DM, TL], F32, kind="ExternalInput").ap()
    xout = nc.dram_tensor("xO", [DM, TL], F32, kind="ExternalOutput").ap()
    dram = {}
    const_drams = {}
    for name, shape in [("c_ones", [1, 128]), ("c_ones_col", [128, 1]),
                        ("c_ident", [128, 128]), ("c_misc", [128, len(CONST_VALUES)]),
                        ("c_trow", [1, TL]), ("c_flags", [128, 2])]:
        const_drams[name] = nc.dram_tensor(name, shape, F32, kind="ExternalInput").ap()
    for ly in layers:
        for name, spec in _WEIGHT_DECLS[ly].items():
            shape = spec[0]
            dt = spec[2] if len(spec) > 2 else F32
            dram[name] = nc.dram_tensor(name, shape, dt, kind="ExternalInput").ap()

    with contextlib.ExitStack() as es:
        block = es.enter_context(nc.Block())
        Cx = Ctx(nc, block, es)
        Cx.dram = dram
        Cx.const_drams = const_drams
        Cx.x = [Cx.sb(f"x{ch}", [128, TL], F32) for ch in range(NCH)]
        Cx.ones_f = Cx.sb("ones_f", [1, 128], F32)
        Cx.ones_f_col = Cx.sb("ones_fc", [128, 1], F32)
        Cx.ones_bf = Cx.sb("ones_bf", [128, 1], BF16)
        Cx.ones_bf_row = Cx.sb("ones_bfr", [1, 128], BF16)
        Cx.ident_bf = Cx.sb("ident_bf", [128, 128], BF16)
        Cx.cmisc = Cx.sb("cmisc", [128, len(CONST_VALUES)], F32)
        Cx.sbc = {"c_flags": Cx.sb("sbc_cflags", [128, 2], F32)}
        for ly in layers:
            for name, spec in _WEIGHT_DECLS[ly].items():
                shape, preload = spec[0], spec[1]
                dt = spec[2] if len(spec) > 2 else F32
                if preload:
                    Cx.sbc[name] = Cx.sb("sbc_" + name, shape, dt)

        @block.sync
        def _(q):
            for ch in range(NCH):
                ins = q.dma_start(out=Cx.x[ch][:], in_=xin[ch * 128:(ch + 1) * 128, :])
                Cx.inc16(ins, "LD")
            ins = q.dma_start(out=Cx.ones_f[:], in_=const_drams["c_ones"])
            Cx.inc16(ins, "LD")
            ins = q.dma_start(out=Cx.ones_f_col[:], in_=const_drams["c_ones_col"])
            Cx.inc16(ins, "LD")
            ins = q.dma_start(out=Cx.cmisc[:], in_=const_drams["c_misc"])
            Cx.inc16(ins, "LD")

            for name in Cx.sbc:
                ins = q.dma_start(out=Cx.sbc[name][:],
                                  in_=dram.get(name) or const_drams[name])
                Cx.inc16(ins, "LD")

        @block.gpsimd
        def _(g):
            ins = g.dma_start(out=Cx.ones_bf[:], in_=const_drams["c_ones_col"])
            Cx.inc16(ins, "GD")
            ins = g.dma_start(out=Cx.ones_bf_row[:], in_=const_drams["c_ones"])
            Cx.inc16(ins, "GD")
            ins = g.dma_start(out=Cx.ident_bf[:], in_=const_drams["c_ident"])
            Cx.inc16(ins, "GD")

        Cx.set_fence([Cx.tok("LD"), Cx.tok("GD")])

        for i, ly in enumerate(layers):
            _EMITTERS[ly](Cx)
            emit_postln(Cx, f"pln{i}")

        st_toks = []

        @block.sync
        def _(q):
            Cx.fw(q)
            for ch in range(NCH):
                ins = q.dma_start(out=xout[ch * 128:(ch + 1) * 128, :], in_=Cx.x[ch][:])
                st_toks.append(Cx.inc16(ins, "ST"))
            Cx.wait(q, st_toks[-1])

        @block.gpsimd
        def _(g):
            Cx.wait(g, st_toks[-1])

    return nc


def run_layers(layers, x_full, inputs):
    nc = build_program(layers)
    consts = _host_consts()
    warr = _layer_weight_arrays(inputs, layers)
    xts = _shard_x(np.asarray(x_full, np.float32))
    in_maps = []
    for c in range(NCORES):
        m = {"xT": xts[c]}
        m.update(consts)
        m.update(warr)
        m.update(_percore_arrays(layers, c))
        in_maps.append(m)
    res = run_bass_kernel_spmd(nc, in_maps, core_ids=list(range(NCORES)))
    return _unshard_x([res.results[c]["xO"] for c in range(NCORES)])


def kernel(**inputs):
    x = np.asarray(inputs["L0_attn_x"], np.float32)
    for ly in ALL_LAYERS:
        x = run_layers([ly], x, inputs)
    return x
